# version 12
# baseline (speedup 1.0000x reference)
import numpy as np
from contextlib import ExitStack

import concourse.bass as bass
import concourse.mybir as mybir

F32 = mybir.dt.float32
BF16 = mybir.dt.bfloat16
I32 = mybir.dt.int32
AF = mybir.ActivationFunctionType
ALU = mybir.AluOpType
AX = mybir.AxisListType.X

D = 1024
ND = D // 128
LN_EPS = 1e-5
ENGINES = ("sync", "vector", "scalar", "tensor", "gpsimd")


class Plan:
    def __init__(self, nc):
        self.nc = nc
        self.ops = {e: [] for e in ENGINES}
        self.sems = {}
        self.counts = {}
        self.es = ExitStack()
        self.bar = {e: 0 for e in ENGINES}
        self._since_bar = {e: False for e in ENGINES}
        self._last = {e: None for e in ENGINES}
        self._last_inc = {e: None for e in ENGINES}

    def sem(self, name):
        if name not in self.sems:
            self.sems[name] = self.es.enter_context(self.nc.semaphore(name))
            self.counts[name] = 0
        return self.sems[name]

    def op(self, engine, fn, waits=(), inc=None, inc_amt=1, dma=None, real=True):
        for w, _ in waits:
            self.sem(w)
        if dma is None:
            dma = (inc_amt == 16)
        waits = list(waits)
        if not dma and real and self._last_inc[engine] is not None:
            se = self._last_inc[engine]
            if se not in waits:
                waits.append(se)
        if inc is not None:
            self.sem(inc)
            self.counts[inc] += inc_amt
        rec = {"fn": fn, "waits": waits, "dma": dma,
               "incs": [] if inc is None else [(inc, inc_amt)]}
        if inc is not None and not dma and real:
            self._last_inc[engine] = (inc, self.counts[inc])
        self.ops[engine].append(rec)
        if real:
            self._since_bar[engine] = True
            self._last[engine] = rec
        return self.counts[inc] if inc is not None else None

    def val(self, name):
        return self.counts.get(name, 0)

    def barrier(self):
        conds = dict(getattr(self, "_bar_conds", {}))
        for e in ENGINES:
            if self._since_bar[e]:
                rec = self._last[e]
                if rec["incs"]:
                    s, _ = rec["incs"][0]
                    conds[e] = (s, self.counts[s])
                else:
                    amt = 16 if rec["dma"] else 1
                    self.sem("bar_" + e)
                    self.counts["bar_" + e] += amt
                    rec["incs"].append(("bar_" + e, amt))
                    conds[e] = ("bar_" + e, self.counts["bar_" + e])
                self._since_bar[e] = False
        self._bar_conds = conds
        return list(conds.values())

    def wait(self, engine, waits):
        if waits:
            self.op(engine, lambda eng: None, waits=waits, real=False)

    def switch_segment(self, fn):
        if not hasattr(self, "segments"):
            self.segments = []
        self.segments.append(("block", self.ops))
        self.ops = {e: [] for e in ENGINES}
        self.segments.append(("switch", fn))

    def build(self):
        nc = self.nc
        sems = self.sems
        segs = getattr(self, "segments", [])
        segs = segs + [("block", self.ops)]

        def runner(ops, engine):
            def run(eng):
                for rec in ops[engine]:
                    for w, v in rec["waits"]:
                        eng.wait_ge(sems[w], v)
                    ins = rec["fn"](eng)
                    if rec["incs"]:
                        assert ins is not None, f"{engine} op with inc returned None"
                        for s, a in rec["incs"]:
                            ins.then_inc(sems[s], a)
            return run

        for kind, payload in segs:
            if kind == "block":
                if not any(payload[e] for e in ENGINES):
                    continue
                with nc.Block() as blk:
                    blk.sync(runner(payload, "sync"))
                    blk.vector(runner(payload, "vector"))
                    blk.scalar(runner(payload, "scalar"))
                    blk.tensor(runner(payload, "tensor"))
                    blk.gpsimd(runner(payload, "gpsimd"))
            else:
                for arm in nc.switch_core_id(engines=[nc.sync], n=128):
                    payload(nc, arm.logical % 8, sems)
        self.es.close()



def host_consts(const_vals=()):
    c = np.zeros((128, NCONST), dtype=np.float32)
    for j, v in enumerate(const_vals):
        c[:, j] = v
    sel0 = np.zeros((128, 128), dtype=np.float32)
    sel0[0, :] = 1.0
    sel1 = np.zeros((128, 128), dtype=np.float32)
    sel1[1, :] = 1.0
    onesblk = np.zeros((128, 128), dtype=np.float32)
    onesblk[:64, :64] = 1.0
    onesblk[64:, 64:] = 1.0
    return {
        "c_eye": np.eye(128, dtype=np.float32),
        "c_onesblk": onesblk,
        "c_ones128": np.ones((128, 128), dtype=np.float32),
        "c_sel0": sel0,
        "c_sel1": sel1,
        "c_scal": c,
    }


def declare_inputs(nc, arrs):
    out = {}
    for k, v in arrs.items():
        dt = I32 if v.dtype == np.int32 else F32
        out[k] = nc.dram_tensor(k, list(v.shape), dt, kind="ExternalInput").ap()
    return out


NCONST = 32


class Sbuf:

    def __init__(self, p):
        self.p = p
        self.t = {}
        self.const_vals = []
        self._scopes = []

    def push(self):
        self._scopes.append((ExitStack(), []))

    def alias(self, name, ap):
        self.t[name] = ap
        if self._scopes:
            self._scopes[-1][1].append(name)

    def once(self, key):
        if key in self.t:
            return False
        self.alias(key, True)
        return True

    def pop(self):
        es, names = self._scopes.pop()
        for n in names:
            del self.t[n]
        es.close()

    def const(self, val, parts=128):
        val = float(val)
        tile = self.sb("k_scal", (128, NCONST))
        if val not in self.const_vals:
            assert len(self.const_vals) < NCONST
            self.const_vals.append(val)
        j = self.const_vals.index(val)
        return tile[0:parts, j:j + 1]

    def _alloc(self, name, cm):
        if self._scopes:
            es, names = self._scopes[-1]
            self.t[name] = es.enter_context(cm)
            names.append(name)
        else:
            self.t[name] = self.p.es.enter_context(cm)
        return self.t[name]

    def _uname(self, name):
        self._uctr = getattr(self, "_uctr", 0) + 1
        return name if not self._scopes else f"u{self._uctr}_{name}"

    def sb(self, name, shape, dtype=F32):
        if name not in self.t:
            self._alloc(name, self.p.nc.sbuf_tensor(self._uname(name), list(shape), dtype))
        return self.t[name]

    def ps(self, name, shape, dtype=F32):
        if name not in self.t:
            self._alloc(name, self.p.nc.psum_tensor(self._uname(name), list(shape), dtype))
        return self.t[name]



def emit_ln(p, m, sx, xln, Lc, tag, dwaits=(), gamma=None, beta=None, eps2=None,
            vsem="v", ssem="s", tsem="t"):
    psA = m.ps("psA", (128, Lc))
    psB = m.ps("psB", (128, Lc))
    psR = m.ps("psR", (128, Lc))
    xbf = [m.sb(f"ln_xbf{i}", (128, Lc), BF16) for i in range(ND)]
    xsq = [m.sb(f"ln_xsq{i}", (128, Lc), BF16) for i in range(ND)]
    rowr = m.sb("ln_rowr", (128, Lc), BF16)
    e2row = m.sb("ln_e2row", (1, Lc))
    m2row = m.sb("ln_m2row", (1, Lc))
    varrow = m.sb("ln_varrow", (1, Lc))
    tln = xbf
    ones128 = m.sb("k_ones128_bf", (128, 128), BF16)
    sel0 = m.sb("k_sel0_bf", (128, 128), BF16)

    if m.once("ln_rows_zeroed"):
        p.op("vector", lambda e: e.memset(rowr[:, :], 0.0), inc=vsem)

    for i in range(ND):
        p.op("vector", lambda e, i=i: e.tensor_copy(out=xbf[i][:, :], in_=sx[i]),
             waits=dwaits if i == 0 else (), inc=vsem)
    for i in range(ND):
        p.op("scalar", lambda e, i=i: e.activation(out=xsq[i][:, :], in_=sx[i],
                                                   func=AF.Square),
             waits=dwaits if i == 0 else (), inc=ssem)
    vcopy, scopy = p.val(vsem), p.val(ssem)
    for i in range(ND):
        p.op("tensor", lambda e, i=i: e.matmul(out=psA[:, :], lhsT=ones128[:, :],
                                               rhs=xbf[i][:, :], start=(i == 0),
                                               stop=(i == ND - 1)),
             waits=[(vsem, vcopy)] if i == 0 else ())
    for i in range(ND):
        p.op("tensor", lambda e, i=i: e.matmul(out=psB[:, :], lhsT=ones128[:, :],
                                               rhs=xsq[i][:, :], start=(i == 0),
                                               stop=(i == ND - 1)),
             waits=[(ssem, scopy)] if i == 0 else (),
             inc=tsem if i == ND - 1 else None)
    t_stats = p.val(tsem)
    p.op("scalar", lambda e: e.activation(out=e2row[:, :], in_=psB[0:1, :],
                                          func=AF.Copy, scale=1.0 / D),
         waits=[(tsem, t_stats)], inc=ssem)
    p.op("scalar", lambda e: e.activation(out=m2row[:, :], in_=psA[0:1, :],
                                          func=AF.Square, scale=1.0 / D), inc=ssem)
    s_rows = p.val(ssem)
    p.op("vector", lambda e: e.tensor_sub(out=varrow[:, :], in0=e2row[:, :],
                                          in1=m2row[:, :]),
         waits=[(ssem, s_rows)], inc=vsem)
    if eps2 is not None:
        p.op("vector", lambda e: e.scalar_tensor_tensor(
            out=varrow[:, :], in0=varrow[:, :], scalar=float(eps2),
            in1=varrow[:, :], op0=ALU.mult, op1=ALU.add),
            waits=[(vsem, p.val(vsem))], inc=vsem)
        eps_b = float(LN_EPS * eps2)
    else:
        eps_b = float(LN_EPS)
    v_var = p.val(vsem)
    eps_ap = m.const(eps_b, parts=1)
    p.op("scalar", lambda e: e.activation(out=varrow[:, :], in_=varrow[:, :],
                                          func=AF.Ln, bias=eps_ap),
         waits=[(vsem, v_var)], inc=ssem)
    p.op("scalar", lambda e: e.activation(out=varrow[:, :], in_=varrow[:, :],
                                          func=AF.Exp, scale=-0.5), inc=ssem)
    s_rstd = p.val(ssem)
    p.op("vector", lambda e: e.tensor_scalar(out=rowr[0:1, :], in0=varrow[:, :],
                                             scalar1=-1.0, scalar2=None,
                                             op0=ALU.mult),
         waits=[(ssem, s_rstd)], inc=vsem)
    v_neg = p.val(vsem)
    p.op("tensor", lambda e: e.matmul(out=psR[:, :], lhsT=sel0[:, :],
                                      rhs=rowr[:, :], start=True, stop=True),
         waits=[(vsem, v_neg)], inc=tsem)
    t_rep = p.val(tsem)
    for i in range(ND):
        p.op("vector", lambda e, i=i: e.scalar_tensor_tensor(
            out=tln[i][:, :], in0=psA[:, :], scalar=1.0 / D, in1=sx[i],
            op0=ALU.mult, op1=ALU.subtract),
             waits=[(tsem, t_rep)] if i == 0 else (), inc=vsem)
        if gamma is None:
            p.op("vector", lambda e, i=i: e.tensor_mul(out=xln[i][:, :],
                                                       in0=tln[i][:, :], in1=psR[:, :]),
                 waits=[(vsem, p.val(vsem))], inc=vsem)
        else:
            p.op("vector", lambda e, i=i: e.tensor_mul(out=tln[i][:, :],
                                                       in0=tln[i][:, :], in1=psR[:, :]),
                 waits=[(vsem, p.val(vsem))], inc=vsem)
    if gamma is not None:
        for i in range(ND):
            p.op("vector", lambda e, i=i: e.scalar_tensor_tensor(
                out=xln[i][:, :], in0=tln[i][:, :], scalar=gamma[i],
                in1=beta[i].broadcast_to([128, Lc]), op0=ALU.mult, op1=ALU.add),
                inc=vsem)
    return (("vector", vsem, p.val(vsem)), ("scalar", ssem, p.val(ssem)))



def emit_ffn1_prep(p, m):
    nc = p.nc
    dram = m.dram
    p._wd_bf = nc.dram_tensor("f1_wd_bf", [24, 128, 1024], BF16,
                              kind="Internal").ap()
    p._wu_bf = nc.dram_tensor("f1_wu_bf", [8, 128, 3072], BF16,
                              kind="Internal").ap()
    stg = [m.sb(f"f1p_stage{j}", (128, 1024)) for j in range(2)]
    sdone = [0, 0]
    jobs = ([("wdT", k, 0, p._wd_bf[k]) for k in range(24)] +
            [("wuT", k, j, p._wu_bf[k][:, j * 1024:(j + 1) * 1024])
             for k in range(ND) for j in range(3)])
    for i, (nm, k, j, dst) in enumerate(jobs):
        par = i % 2
        dgot = p.op("gpsimd", lambda e, nm=nm, k=k, j=j, par=par: e.dma_start(
            out=stg[par][:, :],
            in_=dram[nm][k * 128:(k + 1) * 128, j * 1024:(j + 1) * 1024]),
            waits=[("f1ps", sdone[par])] if sdone[par] else (),
            inc="f1pd", inc_amt=16)
        p.op("gpsimd", lambda e, dst=dst, par=par: e.dma_start(
            out=dst, in_=stg[par][:, :]),
            waits=[("f1pd", dgot)], inc="f1ps", inc_amt=16)
        sdone[par] = p.val("f1ps")
    p._wd_ready = ("f1ps", p.val("f1ps"))


def emit_ffn1(p, m, xin_ap, xout_ap, L, Lc, prefix="f1", dma_eng="sync", entry_waits=()):
    nc = p.nc
    HT = 3072 // 128
    dram = m.dram
    wg = m.sb(prefix + "wg", (128, 8 * 3072), BF16)
    wuW = [m.sb(f"{prefix}wuW{j}", (128, ND * 128), BF16) for j in range(2)]
    wdW = [m.sb(f"{prefix}wdW{j}", (128, HT * 128), BF16) for j in range(2)]
    hh = m.sb(prefix + "hh", (128, HT * Lc), BF16)
    stage = hh[:, :].bitcast(F32)[:, 0:3072]
    wdtmp = hh[:, 6144:7168]
    sx = [[m.sb(f"{prefix}sx{par}_{i}", (128, Lc)) for i in range(ND)]
          for par in range(2)]
    xln = [[m.sb(f"{prefix}xln{par}_{i}", (128, Lc), BF16) for i in range(ND)]
           for par in range(2)]
    sg = [m.sb(f"{prefix}sg{j}", (128, Lc), BF16) for j in range(2)]
    xo = [m.sb(f"{prefix}xo{i}", (128, Lc)) for i in range(ND)]
    psG = [m.ps(f"{prefix}psG{j}", (128, Lc)) for j in range(2)]
    psU = [m.ps(f"{prefix}psU{j}", (128, Lc)) for j in range(2)]

    ds, vs, ss, ts = prefix + "d", prefix + "v", prefix + "s", prefix + "t"
    dsg = prefix + "dg"

    for name, dst in (("wgT", wg),):
        for k in range(ND):
            dgot = p.op("gpsimd", lambda e, name=name, k=k: e.dma_start(
                out=stage[:, :], in_=dram[name][k * 128:(k + 1) * 128, :]),
                waits=[(vs, p.val(vs))], inc=dsg, inc_amt=16)
            p.op("vector", lambda e, dst=dst, k=k: e.tensor_copy(
                out=dst[:, k * 3072:(k + 1) * 3072], in_=stage[:, :]),
                waits=[(dsg, dgot)], inc=vs)
    if not hasattr(p, "_wd_bf"):
        emit_ffn1_prep(p, m)
    wd_bf = p._wd_bf
    wu_bf = p._wu_bf
    wd_ready = p._wd_ready
    p.barrier()

    nchunks = L // Lc
    ln_done = {}
    resid_done = [0, 0]

    def stage_ln(c):
        cp = c % 2
        sl = slice(c * Lc, (c + 1) * Lc)
        dx = prefix + ("dxa" if cp == 0 else "dxb")
        dgot = None
        for i in range(ND):
            w0 = (([(vs, resid_done[cp])] if resid_done[cp] else []) +
                  (list(entry_waits) if c == 0 else [])) if i == 0 else ()
            dgot = p.op(dma_eng, lambda e, i=i, sl=sl, cp=cp: e.dma_start(
                out=sx[cp][i][:, :], in_=xin_ap(i, sl)),
                waits=w0, inc=dx, inc_amt=16)
        emit_ln(p, m, [sx[cp][i][:, :] for i in range(ND)], xln[cp], Lc, tag=prefix,
                dwaits=[(dx, dgot)], vsem=vs, ssem=ss, tsem=ts)
        ln_done[c] = (p.val(vs), p.val(ss))

    stage_ln(0)
    for c in range(nchunks):
        cp = c % 2
        sl = slice(c * Lc, (c + 1) * Lc)
        v_ln, s_ln = ln_done[c]
        guard = [None, None]
        wul = [0, 0]
        for h in range(HT):
            par = h % 2
            du_ = prefix + ("dua" if par == 0 else "dub")
            dwu = p.op("gpsimd", lambda e, h=h, par=par: e.dma_start(
                out=wuW[par][:, :].rearrange("p (k j) -> p k j", j=128),
                in_=wu_bf[:, :, h * 128:(h + 1) * 128].rearrange("k p j -> p k j")),
                waits=([(ts, wul[par])] if wul[par] else [wd_ready]),
                inc=du_, inc_amt=16)
            w0 = [(vs, v_ln), (ss, s_ln)] if h == 0 else (
                [(vs, guard[par])] if guard[par] else ())
            for k in range(ND):
                p.op("tensor", lambda e, h=h, k=k, par=par, cp=cp: e.matmul(
                    out=psG[par][:, :],
                    lhsT=wg[:, k * 3072 + h * 128:k * 3072 + (h + 1) * 128],
                    rhs=xln[cp][k][:, :], start=(k == 0), stop=(k == ND - 1)),
                    waits=w0 if k == 0 else ())
            for k in range(ND):
                p.op("tensor", lambda e, h=h, k=k, par=par, cp=cp: e.matmul(
                    out=psU[par][:, :],
                    lhsT=wuW[par][:, k * 128:(k + 1) * 128],
                    rhs=xln[cp][k][:, :], start=(k == 0), stop=(k == ND - 1)),
                    waits=[(du_, dwu)] if k == 0 else (),
                    inc=ts if k == ND - 1 else None)
            t_gu = p.val(ts)
            wul[par] = t_gu
            p.op("scalar", lambda e, h=h, par=par: e.activation(
                out=sg[par][:, :], in_=psG[par][:, :], func=AF.Silu,
                scale=1.0 / 127.0), waits=[(ts, t_gu)], inc=ss)
            s_silu = p.val(ss)
            p.op("vector", lambda e, h=h, par=par: e.scalar_tensor_tensor(
                out=hh[:, h * Lc:(h + 1) * Lc], in0=psU[par][:, :],
                scalar=1.0 / 127.0, in1=sg[par][:, :], op0=ALU.mult, op1=ALU.mult),
                waits=[(ss, s_silu)], inc=vs)
            guard[par] = p.val(vs)
            if h == 1 and c + 1 < nchunks:
                stage_ln(c + 1)
        v_hh_all = p.val(vs)
        psD = [m.t["psA"], m.t["psB"]]
        dguard = [None, None]
        dwl = [0, 0]
        for o in range(ND):
            par = o % 2
            dd = prefix + ("dda" if par == 0 else "ddb")
            dw = p.op("gpsimd", lambda e, o=o, par=par: e.dma_start(
                out=wdW[par][:, :].rearrange("p (k j) -> p k j", j=128),
                in_=wd_bf[:, :, o * 128:(o + 1) * 128].rearrange("k p j -> p k j")),
                waits=([(ts, dwl[par])] if dwl[par] else
                       ([wd_ready] if o < 2 else [])), inc=dd, inc_amt=16)
            w0 = [(dd, dw), (vs, v_hh_all)] + (
                [(vs, dguard[par])] if dguard[par] else [])
            for k in range(HT):
                p.op("tensor", lambda e, o=o, k=k, par=par: e.matmul(
                    out=psD[par][:, :],
                    lhsT=wdW[par][:, k * 128:(k + 1) * 128],
                    rhs=hh[:, k * Lc:(k + 1) * Lc], start=(k == 0),
                    stop=(k == HT - 1)),
                    waits=w0 if k == 0 else (), inc=ts if k == HT - 1 else None)
            t_d = p.val(ts)
            dwl[par] = t_d
            p.op("vector", lambda e, o=o, par=par, cp=cp: e.scalar_tensor_tensor(
                out=xo[o][:, :], in0=psD[par][:, :], scalar=0.5 / 127.0,
                in1=sx[cp][o][:, :], op0=ALU.mult, op1=ALU.add),
                waits=[(ts, t_d)], inc=vs)
            dguard[par] = p.val(vs)
            v_o = p.val(vs)
            p.op(dma_eng, lambda e, o=o, sl=sl: e.dma_start(
                out=xout_ap(o, sl), in_=xo[o][:, :]),
                waits=[(vs, v_o)], inc=ds, inc_amt=16)
        resid_done[cp] = p.val(vs)
    p.wait(dma_eng, [(ds, p.val(ds))])


def emit_const_loads(p, m):
    d = m.dram
    oc = m.sb("k_ones128_bf", (128, 128), BF16)
    sel0 = m.sb("k_sel0_bf", (128, 128), BF16)
    sel1 = m.sb("k_sel1_bf", (128, 128), BF16)
    eye = m.sb("k_eye_bf", (128, 128), BF16)
    onesblk = m.sb("k_onesblk_bf", (128, 128), BF16)
    scal = m.sb("k_scal", (128, NCONST))
    p.op("gpsimd", lambda e: e.dma_start(out=oc[:, :], in_=d["c_ones128"]),
         inc="cd_g", inc_amt=16)
    p.op("gpsimd", lambda e: e.dma_start(out=sel0[:, :], in_=d["c_sel0"]),
         inc="cd_g", inc_amt=16)
    p.op("gpsimd", lambda e: e.dma_start(out=sel1[:, :], in_=d["c_sel1"]),
         inc="cd_g", inc_amt=16)
    p.op("gpsimd", lambda e: e.dma_start(out=eye[:, :], in_=d["c_eye"]),
         inc="cd_g", inc_amt=16)
    p.op("gpsimd", lambda e: e.dma_start(out=onesblk[:, :], in_=d["c_onesblk"]),
         inc="cd_g", inc_amt=16)
    p.op("sync", lambda e: e.dma_start(out=scal[:, :], in_=d["c_scal"]),
         inc="cd_s", inc_amt=16)
    w = [("cd_g", p.val("cd_g")), ("cd_s", p.val("cd_s"))]
    p.wait("tensor", w)
    p.wait("vector", w)
    p.wait("scalar", w)


def build_ffn1_launch(in_shapes, L=4096, Lc=512):
    nc = bass.Bass("TRN2", target_bir_lowering=False)
    d = declare_inputs(nc, in_shapes)
    xout = nc.dram_tensor("xout", [ND, 128, L], F32, kind="ExternalOutput").ap()
    p = Plan(nc)
    m = Sbuf(p)
    m.dram = d
    emit_const_loads(p, m)
    emit_ffn1(p, m, lambda i, sl: d["xin"][i, :, sl],
              lambda o, sl: xout[o, :, sl], L, Lc)
    p.build()
    return nc, m


def build_ln_launch(in_shapes, L=4096, Lc=256):
    nc = bass.Bass("TRN2", target_bir_lowering=False)
    d = declare_inputs(nc, in_shapes)
    xout = nc.dram_tensor("xout", [ND, 128, L], BF16, kind="ExternalOutput").ap()
    p = Plan(nc)
    m = Sbuf(p)
    m.dram = d
    emit_const_loads(p, m)
    sx = [m.sb(f"dsx{i}", (128, Lc)) for i in range(ND)]
    xln = [m.sb(f"dxln{i}", (128, Lc), BF16) for i in range(ND)]
    prev_bar = ()
    for c in range(L // Lc):
        sl = slice(c * Lc, (c + 1) * Lc)
        dgot = None
        for i in range(ND):
            dgot = p.op("sync", lambda e, i=i, sl=sl: e.dma_start(
                out=sx[i][:, :], in_=d["xin"][i, :, sl]),
                waits=prev_bar if i == 0 else (), inc="d", inc_amt=16)
        emit_ln(p, m, [sx[i][:, :] for i in range(ND)], xln, Lc, tag="ln",
                dwaits=[("d", dgot)], vsem="v", ssem="s", tsem="t")
        v_ln = p.val("v")
        for i in range(ND):
            p.op("sync", lambda e, i=i, sl=sl: e.dma_start(
                out=xout[i, :, sl], in_=xln[i][:, :]),
                waits=[("v", v_ln)] if i == 0 else (), inc="d", inc_amt=16)
        prev_bar = p.barrier()
    p.wait("sync", [("d", p.val("d"))])
    p.build()
    return nc, m



def emit_ffn0(p, m, xin_ap, xout_ap, Ll, prefix="f0", entry_waits=()):
    nc = p.nc
    d = m.dram
    T = 8
    WA = m.sb(prefix + "WA", (128, 6 * ND * 1024), BF16)
    woff = {"wfT": 0, "wiT": 1, "wcT": 2, "ufT": 3, "uiT": 4, "ucT": 5}

    def wtile(name, k, o):
        base = woff[name] * ND * 1024 + k * 1024
        return WA[:, base + o * 128: base + (o + 1) * 128]

    bias = {nm: m.sb(prefix + nm, (128, ND)) for nm in ("bfT", "biT", "bcT", "boT", "incT")}
    psF = [m.ps(f"{prefix}psF{j}", (128, Ll)) for j in range(2)]
    psI = [m.ps(f"{prefix}psI{j}", (128, Ll)) for j in range(2)]
    psC = [m.ps(f"{prefix}psC{j}", (128, Ll)) for j in range(2)]
    psO = m.ps(prefix + "psO", (128, Ll))
    m.alias("psA", psF[0])
    m.alias("psB", psI[0])
    m.alias("psR", psC[0])
    sx = [m.sb(f"{prefix}sx{i}", (128, Ll)) for i in range(ND)]
    xln = [m.sb(f"{prefix}xln{i}", (128, Ll), BF16) for i in range(ND)]
    cst = [m.sb(f"{prefix}c{i}", (128, Ll)) for i in range(ND)]
    cbf = [[m.sb(f"{prefix}cb{par}_{i}", (128, Ll), BF16) for i in range(ND)]
           for par in range(2)]
    gF = [m.sb(f"{prefix}gF{j}", (128, Ll), BF16) for j in range(2)]
    gI = [m.sb(f"{prefix}gI{j}", (128, Ll), BF16) for j in range(2)]
    gC = [m.sb(f"{prefix}gC{j}", (128, Ll), BF16) for j in range(2)]
    t1b = m.sb(prefix + "t1", (128, Ll), BF16)
    t2f = m.sb(prefix + "t2", (128, Ll))
    xstash = nc.dram_tensor(f"{prefix}_xstash{sum(len(v) for v in p.ops.values())}",
                            [T, ND, 128, Ll], BF16, kind="Internal").ap()

    ds, dsg, vs, ss, ts = (prefix + "d", prefix + "dg", prefix + "v",
                           prefix + "s", prefix + "t")

    for nm in woff:
        for k in range(ND):
            p.op("gpsimd", lambda e, nm=nm, k=k: e.dma_start(
                out=WA[:, woff[nm] * ND * 1024 + k * 1024:
                       woff[nm] * ND * 1024 + (k + 1) * 1024],
                in_=d[nm][k * 128:(k + 1) * 128, :]), inc=dsg, inc_amt=16)
    for nm in bias:
        p.op("sync", lambda e, nm=nm: e.dma_start(out=bias[nm][:, :], in_=d[nm]),
             inc=prefix + "db", inc_amt=16)
    w_weights = [(dsg, p.val(dsg)), (prefix + "db", p.val(prefix + "db"))]

    for k in range(ND):
        p.op("vector", lambda e, k=k: e.tensor_scalar(
            out=cst[k][:, :], in0=bias["incT"][:, k:k + 1].broadcast_to([128, Ll]),
            scalar1=1.0, scalar2=None, op0=ALU.mult),
            waits=w_weights if k == 0 else (), inc=vs)
        p.op("vector", lambda e, k=k: e.tensor_copy(out=cbf[0][k][:, :], in_=cst[k][:, :]),
             inc=vs)

    prev_bar = tuple(entry_waits)
    for t in range(T):
        dx = prefix + ("dxa" if t % 2 == 0 else "dxb")
        dgot = None
        for i in range(ND):
            dgot = p.op("sync", lambda e, t=t, i=i: e.dma_start(
                out=sx[i][:, :], in_=xin_ap(t, i)),
                waits=prev_bar if i == 0 else (), inc=dx, inc_amt=16)
        emit_ln(p, m, [sx[i][:, :] for i in range(ND)], xln, Ll, tag=prefix,
                dwaits=[(dx, dgot)], vsem=vs, ssem=ss, tsem=ts)
        v_ln = p.val(vs)
        for i in range(ND):
            p.op("sync", lambda e, t=t, i=i: e.dma_start(
                out=xstash[t, i, :, :], in_=xln[i][:, :]),
                waits=[(vs, v_ln)] if i == 0 else (), inc=prefix + "dst", inc_amt=16)
        prev_bar = p.barrier()

    xt = [m.sb(f"{prefix}xt{i}", (128, Ll), BF16) for i in range(ND)]
    stash_done = [(prefix + "dst", p.val(prefix + "dst"))]
    for t in range(T):
        dx = prefix + ("dsa" if t % 2 == 0 else "dsb")
        dgot = None
        for i in range(ND):
            dgot = p.op("sync", lambda e, t=t, i=i: e.dma_start(
                out=xt[i][:, :], in_=xstash[t, i, :, :]),
                waits=(list(prev_bar) + stash_done) if i == 0 else (),
                inc=dx, inc_amt=16)
        v_prev_c = p.val(vs)
        guard = [None, None]
        for k in range(ND):
            par = k % 2
            if guard[par] is not None:
                p.op("tensor", lambda e: None, waits=[(vs, guard[par])], real=False)
            for ps, wname, uname in ((psF[par], "wfT", "ufT"), (psI[par], "wiT", "uiT"),
                                     (psC[par], "wcT", "ucT")):
                for kk in range(ND):
                    p.op("tensor", lambda e, ps=ps, wname=wname, k=k, kk=kk: e.matmul(
                        out=ps[:, :], lhsT=wtile(wname, kk, k), rhs=xt[kk][:, :],
                        start=(kk == 0), stop=False),
                        waits=[(dx, dgot), (vs, v_prev_c)] if (k == 0 and kk == 0 and ps is psF[par]) else ())
                for kk in range(ND):
                    p.op("tensor", lambda e, ps=ps, uname=uname, k=k, kk=kk, t=t: e.matmul(
                        out=ps[:, :], lhsT=wtile(uname, kk, k), rhs=cbf[t % 2][kk][:, :],
                        start=False, stop=(kk == ND - 1)),
                        inc=ts if (kk == ND - 1 and ps is psC[par]) else None)
            t_g = p.val(ts)
            p.op("scalar", lambda e, k=k, par=par: e.activation(
                out=gF[par][:, :], in_=psF[par][:, :], func=AF.Sigmoid,
                bias=bias["bfT"][:, k:k + 1]), waits=[(ts, t_g)], inc=ss)
            p.op("scalar", lambda e, k=k, par=par: e.activation(
                out=gI[par][:, :], in_=psI[par][:, :], func=AF.Sigmoid,
                bias=bias["biT"][:, k:k + 1]), inc=ss)
            p.op("scalar", lambda e, k=k, par=par: e.activation(
                out=gC[par][:, :], in_=psC[par][:, :], func=AF.Tanh,
                bias=bias["bcT"][:, k:k + 1]), inc=ss)
            s_g = p.val(ss)
            p.op("vector", lambda e, k=k, par=par: e.tensor_mul(
                out=t1b[:, :], in0=gI[par][:, :], in1=gC[par][:, :]),
                waits=[(ss, s_g)], inc=vs)
            p.op("vector", lambda e, k=k, par=par: e.tensor_mul(
                out=t2f[:, :], in0=gF[par][:, :], in1=cst[k][:, :]), inc=vs)
            p.op("vector", lambda e, k=k: e.tensor_add(out=cst[k][:, :], in0=t2f[:, :],
                                                       in1=t1b[:, :]), inc=vs)
            p.op("vector", lambda e, k=k, t=t: e.tensor_copy(
                out=cbf[(t + 1) % 2][k][:, :], in_=cst[k][:, :]), inc=vs)
            guard[par] = p.val(vs)
        prev_bar = p.barrier()

    steps_done = p.barrier()
    for nm, off in (("woT", 0), ("uoT", 1)):
        for k in range(ND):
            p.op("gpsimd", lambda e, nm=nm, off=off, k=k: e.dma_start(
                out=WA[:, off * ND * 1024 + k * 1024:off * ND * 1024 + (k + 1) * 1024],
                in_=d[nm][k * 128:(k + 1) * 128, :]),
                waits=steps_done if (nm == "woT" and k == 0) else (),
                inc=dsg, inc_amt=16)
    w_wo = [(dsg, p.val(dsg))]

    def wotile(off, k, o):
        base = off * ND * 1024 + k * 1024
        return WA[:, base + o * 128: base + (o + 1) * 128]

    xo = cst
    for t in range(T):
        dx = prefix + ("doa" if t % 2 == 0 else "dob")
        dgx = None
        for i in range(ND):
            p.op("sync", lambda e, t=t, i=i: e.dma_start(
                out=xt[i][:, :], in_=xstash[t, i, :, :]),
                waits=prev_bar if i == 0 else (), inc=dx, inc_amt=16)
            dgx = p.op("sync", lambda e, t=t, i=i: e.dma_start(
                out=sx[i][:, :], in_=xin_ap(t, i)), inc=dx, inc_amt=16)
        for k in range(ND):
            for kk in range(ND):
                p.op("tensor", lambda e, k=k, kk=kk: e.matmul(
                    out=psO[:, :], lhsT=wotile(0, kk, k), rhs=xt[kk][:, :],
                    start=(kk == 0), stop=False),
                    waits=([(dx, dgx)] + (w_wo if t == 0 else [])) if kk == 0 else ())
            for kk in range(ND):
                p.op("tensor", lambda e, k=k, kk=kk: e.matmul(
                    out=psO[:, :], lhsT=wotile(1, kk, k), rhs=cbf[0][kk][:, :],
                    start=False, stop=(kk == ND - 1)),
                    inc=ts if kk == ND - 1 else None)
            t_o = p.val(ts)
            p.op("scalar", lambda e, k=k: e.activation(
                out=t2f[:, :], in_=psO[:, :], func=AF.Tanh, bias=bias["boT"][:, k:k + 1]),
                waits=[(ts, t_o)], inc=ss)
            s_t = p.val(ss)
            p.op("vector", lambda e, k=k: e.scalar_tensor_tensor(
                out=xo[k][:, :], in0=t2f[:, :], scalar=0.5, in1=sx[k][:, :],
                op0=ALU.mult, op1=ALU.add), waits=[(ss, s_t)], inc=vs)
            v_x = p.val(vs)
            p.op("sync", lambda e, t=t, k=k: e.dma_start(
                out=xout_ap(t, k), in_=xo[k][:, :]),
                waits=[(vs, v_x)], inc=prefix + "do", inc_amt=16)
            p.op("tensor", lambda e: None, waits=[(vs, v_x)], real=False)
        prev_bar = p.barrier()
    p.wait("sync", [(prefix + "do", p.val(prefix + "do"))])


def build_ffn0_launch(in_shapes, Ll=512):
    nc = bass.Bass("TRN2", target_bir_lowering=False)
    d = declare_inputs(nc, in_shapes)
    xout = nc.dram_tensor("xout", [8, ND, 128, Ll], F32, kind="ExternalOutput").ap()
    p = Plan(nc)
    m = Sbuf(p)
    m.dram = d
    emit_const_loads(p, m)
    emit_ffn0(p, m, lambda t, i: d["xin"][t, i, :, :],
              lambda t, k: xout[t, k, :, :], Ll)
    p.build()
    return nc, m



def emit_att1(p, m, xin_ap, xout_ap, L, Lc, prefix="mb", entry_waits=()):
    nc = p.nc
    d = m.dram
    NDT = 16
    NE = 32
    NN = 16
    ds, dsg, vs, ss, ts = (prefix + "d", prefix + "dg", prefix + "v",
                           prefix + "s", prefix + "t")

    ipW = [m.sb(f"{prefix}ipW{par}", (128, ND * 128), BF16) for par in range(2)]
    opW = [m.sb(f"{prefix}opW{par}", (128, NDT * 128), BF16) for par in range(2)]
    xpW = m.sb(prefix + "xpW", (128, NDT * 128), BF16)
    dtW = m.sb(prefix + "dtW", (128, NDT * 128), BF16)
    Alog = m.sb(prefix + "Alog", (128, NDT * NN))
    Asb = m.sb(prefix + "A", (128, NDT * NN))
    convW = m.sb(prefix + "convW", (128, NDT * 4))
    convB = m.sb(prefix + "convB", (128, NDT))
    Dd = m.sb(prefix + "Dd", (128, NDT))
    dtb = m.sb(prefix + "dtb", (128, NDT))
    gamT = m.sb(prefix + "gamT", (128, ND))
    betT = m.sb(prefix + "betT", (128, ND))

    sx = [m.sb(f"{prefix}sx{i}", (128, Lc)) for i in range(ND)]
    scrA = m.sb(prefix + "scrA", (128, Lc * ND))
    xwf = [scrA[:, i * Lc:(i + 1) * Lc] for i in range(ND)]
    states = scrA[:, :].bitcast(BF16)
    xz = [m.sb(f"{prefix}xz{i}", (128, Lc), BF16) for i in range(ND)]
    ubig = m.sb(prefix + "ubig", (128, NDT * (Lc + 4)), BF16)
    uti = [ubig[:, t * (Lc + 4): t * (Lc + 4) + Lc + 4] for t in range(NDT)]
    xob = ubig[:, :].bitcast(F32)
    xo = [xob[:, i * ((Lc + 4) // 2 * 2): i * ((Lc + 4) // 2 * 2) + Lc] for i in range(ND)]
    utail = m.sb(prefix + "utail", (128, NDT * 4), BF16)
    uc = m.sb(prefix + "uc", (128, NDT * Lc), BF16)
    delta = m.sb(prefix + "delta", (128, NDT * Lc), BF16)
    du = m.sb(prefix + "du", (128, Lc), BF16)
    bcrows = m.sb(prefix + "bcrows", (128, NN * Lc), BF16)
    Brep = m.sb(prefix + "Brep", (128, NN * Lc), BF16)
    Crep = m.sb(prefix + "Crep", (128, NN * Lc), BF16)
    yg = m.sb(prefix + "yg", (128, NDT * Lc), BF16)
    xdbl = m.sb(prefix + "xdbl", (128, Lc))
    xdbf = m.sb(prefix + "xdbf", (128, Lc), BF16)
    dAb = m.sb(prefix + "dAb", (128, Lc), BF16)
    dAb2 = m.sb(prefix + "dAb2", (128, Lc), BF16)
    dBub = m.sb(prefix + "dBub", (128, Lc), BF16)
    yf = m.sb(prefix + "yf", (128, Lc))
    zt = m.sb(prefix + "zt", (128, Lc), BF16)
    carry = m.sb(prefix + "carry", (128, NDT * NN))
    tmp16 = m.sb(prefix + "tmp16", (128, 16))
    sel0 = m.sb("k_sel0_bf", (128, 128), BF16)
    sel1 = m.sb("k_sel1_bf", (128, 128), BF16)

    for i in range(ND):
        m.alias(f"ln_xsq{i}", uc[:, i * Lc:(i + 1) * Lc])
    psW0 = m.ps(prefix + "psW0", (128, Lc))
    psW1 = m.ps(prefix + "psW1", (128, Lc))
    psO = m.ps(prefix + "psO", (128, Lc))

    _uq = sum(len(v) for v in p.ops.values())
    zstash = nc.dram_tensor(f"{prefix}_zst{_uq}", [NDT, 128, Lc], BF16, kind="Internal").ap()
    xwstash = nc.dram_tensor(f"{prefix}_xwst{_uq}", [ND, 128, Lc], BF16, kind="Internal").ap()
    bcstash = nc.dram_tensor(f"{prefix}_bcst{_uq}", [2, NN, Lc], F32, kind="Internal").ap()

    p.op("sync", lambda e: e.dma_start(out=Alog[:, :], in_=d["A_logT"]),
         inc=prefix + "db", inc_amt=16)
    for nm, dst in (("convWT", convW), ("convBT", convB), ("DT", Dd),
                    ("dtbT", dtb), ("gammaT", gamT), ("betaT", betT)):
        p.op("sync", lambda e, nm=nm, dst=dst: e.dma_start(out=dst[:, :], in_=d[nm]),
             inc=prefix + "db", inc_amt=16)
    for k in range(NDT):
        p.op("gpsimd", lambda e, k=k: e.dma_start(
            out=xpW[:, k * 128:(k + 1) * 128],
            in_=d["xpT"][k * 128:(k + 1) * 128, :]), inc=dsg, inc_amt=16)
        p.op("gpsimd", lambda e, k=k: e.dma_start(
            out=dtW[:, k * 128:(k + 1) * 128],
            in_=d["dtT"][:, k * 128:(k + 1) * 128]), inc=dsg, inc_amt=16)
    w_bias = [(prefix + "db", p.val(prefix + "db")), (dsg, p.val(dsg))]
    p.op("scalar", lambda e: e.activation(out=Asb[:, :], in_=Alog[:, :], func=AF.Exp),
         waits=w_bias, inc=ss)
    p.op("vector", lambda e: e.tensor_scalar(out=Asb[:, :], in0=Asb[:, :],
                                             scalar1=-1.0, scalar2=None, op0=ALU.mult),
         waits=[(ss, p.val(ss))], inc=vs)
    p.op("vector", lambda e: e.memset(carry[:, :], 0.0), inc=vs)
    p.op("vector", lambda e: e.memset(bcrows[:, :], 0.0), inc=vs)
    p.op("vector", lambda e: e.memset(ubig[:, :], 0.0), inc=vs)
    p.op("vector", lambda e: e.memset(utail[:, :], 0.0), inc=vs)

    prev_bar = tuple(entry_waits)
    nch = L // Lc
    for c in range(nch):
        sl = slice(c * Lc, (c + 1) * Lc)
        dx = prefix + ("dxa" if c % 2 == 0 else "dxb")
        dgot = None
        for i in range(ND):
            dgot = p.op("sync", lambda e, i=i, sl=sl: e.dma_start(
                out=sx[i][:, :], in_=xin_ap(i, sl)),
                waits=prev_bar if i == 0 else (), inc=dx, inc_amt=16)
        emit_ln(p, m, [sx[i][:, :] for i in range(ND)], xwf, Lc, tag=prefix,
                dwaits=[(dx, dgot)], vsem=vs, ssem=ss, tsem=ts)
        v_l1 = p.val(vs)
        for i in range(ND):
            p.op("gpsimd", lambda e, i=i: e.dma_start(out=xwstash[i, :, :],
                                                      in_=xwf[i]),
                 waits=[(vs, v_l1)] if i == 0 else (), inc=prefix + "dst", inc_amt=16)
        emit_ln(p, m, xwf, xz, Lc, tag=prefix,
                dwaits=[(vs, v_l1)],
                gamma=[gamT[:, i:i + 1] for i in range(ND)],
                beta=[betT[:, i:i + 1] for i in range(ND)],
                eps2=1e-6, vsem=vs, ssem=ss, tsem=ts)
        v_l2, s_l2 = p.val(vs), p.val(ss)
        ipguard = [None, None]
        ipmm = [0, 0]
        for e_t in range(NE):
            par = e_t % 2
            dip = prefix + ("dia" if par == 0 else "dib")
            dw = p.op("gpsimd", lambda e, e_t=e_t, par=par: e.dma_start(
                out=ipW[par][:, :].rearrange("p (k j) -> p k j", j=128),
                in_=d["ipT"][:, e_t * 128:(e_t + 1) * 128].rearrange(
                    "(k p) j -> p k j", p=128)),
                waits=([(ts, ipmm[par])] if ipmm[par] else [(ts, p.val(ts))]),
                inc=dip, inc_amt=16)
            ps = psW0 if par == 0 else psW1
            for k in range(ND):
                w0 = (([(dip, dw), (vs, v_l2), (ss, s_l2)] +
                       ([ipguard[par]] if ipguard[par] else []))
                      if k == 0 else ())
                p.op("tensor", lambda e, e_t=e_t, k=k, ps=ps, par=par: e.matmul(
                    out=ps[:, :], lhsT=ipW[par][:, k * 128:(k + 1) * 128],
                    rhs=xz[k][:, :], start=(k == 0), stop=(k == ND - 1)),
                    waits=w0, inc=ts if k == ND - 1 else None)
            t_ip = p.val(ts)
            ipmm[par] = t_ip
            if e_t < NDT:
                p.op("scalar", lambda e, e_t=e_t, ps=ps: e.activation(
                    out=uti[e_t][:, 4:], in_=ps[:, :], func=AF.Copy),
                    waits=[(ts, t_ip)], inc=ss)
                p.op("scalar", lambda e, e_t=e_t: e.activation(
                    out=uti[e_t][:, 0:4], in_=utail[:, e_t * 4:(e_t + 1) * 4],
                    func=AF.Copy), inc=ss)
            else:
                zi = e_t - NDT
                p.op("scalar", lambda e, ps=ps: e.activation(
                    out=zt[:, :], in_=ps[:, :], func=AF.Copy),
                    waits=[(ts, t_ip), (prefix + "dst", p.val(prefix + "dst"))], inc=ss)
                s_z = p.val(ss)
                p.op("gpsimd", lambda e, zi=zi: e.dma_start(
                    out=zstash[zi, :, :], in_=zt[:, :]),
                    waits=[(ss, s_z)], inc=prefix + "dst", inc_amt=16)
            ipguard[par] = (ss, p.val(ss))
        s_ip_all = p.val(ss)
        for t in range(NDT):
            p.op("vector", lambda e, t=t: e.tensor_copy(
                out=utail[:, t * 4:(t + 1) * 4], in_=uti[t][:, Lc:Lc + 4]),
                waits=[(ss, s_ip_all)] if t == 0 else (), inc=vs)
        v_tails = p.val(vs)
        for t in range(NDT):
            p.op("scalar", lambda e, t=t: e.activation(
                out=yf[:, :], in_=uti[t][:, 1:1 + Lc], func=AF.Copy,
                scale=convW[:, 4 * t:4 * t + 1]),
                waits=[(vs, v_tails)] if t == 0 else (), inc=ss)
            s_c0 = p.val(ss)
            for kk in range(1, 4):
                p.op("vector", lambda e, t=t, kk=kk: e.scalar_tensor_tensor(
                    out=yf[:, :], in0=uti[t][:, 1 + kk:1 + kk + Lc],
                    scalar=convW[:, 4 * t + kk:4 * t + kk + 1], in1=yf[:, :],
                    op0=ALU.mult, op1=ALU.add),
                    waits=[(ss, s_c0)] if kk == 1 else (), inc=vs)
            v_cv = p.val(vs)
            p.op("scalar", lambda e, t=t: e.activation(
                out=uc[:, t * Lc:(t + 1) * Lc], in_=yf[:, :], func=AF.Silu,
                bias=convB[:, t:t + 1]), waits=[(vs, v_cv)], inc=ss)
        s_uc = p.val(ss)
        for k in range(NDT):
            p.op("tensor", lambda e, k=k: e.matmul(
                out=psW0[:, :], lhsT=xpW[:, k * 128:(k + 1) * 128],
                rhs=uc[:, k * Lc:(k + 1) * Lc], start=(k == 0), stop=(k == NDT - 1)),
                waits=[(ss, s_uc)] if k == 0 else (), inc=ts if k == NDT - 1 else None)
        t_xd = p.val(ts)
        p.op("vector", lambda e: e.tensor_copy(out=xdbl[:, :], in_=psW0[:, :]),
             waits=[(ts, t_xd)], inc=vs)
        p.op("vector", lambda e: e.tensor_copy(out=xdbf[:, :], in_=xdbl[:, :]), inc=vs)
        v_xd = p.val(vs)
        p.op("gpsimd", lambda e: e.dma_start(out=bcstash[0, :, :], in_=xdbl[64:80, :]),
             waits=[(vs, v_xd)], inc=prefix + "dbc", inc_amt=16)
        p.op("gpsimd", lambda e: e.dma_start(out=bcstash[1, :, :], in_=xdbl[80:96, :]),
             inc=prefix + "dbc", inc_amt=16)
        d_bc1 = p.val(prefix + "dbc")
        p.op("gpsimd", lambda e: e.dma_start(
            out=bcrows[0:1, :], in_=bcstash[0:1].rearrange("o n l -> o (n l)")),
            waits=[(prefix + "dbc", d_bc1)], inc=prefix + "dbc", inc_amt=16)
        p.op("gpsimd", lambda e: e.dma_start(
            out=bcrows[1:2, :], in_=bcstash[1:2].rearrange("o n l -> o (n l)")),
            inc=prefix + "dbc", inc_amt=16)
        d_bc = p.val(prefix + "dbc")
        bev, cev = None, None
        for n in range(NN):
            p.op("tensor", lambda e, n=n: e.matmul(
                out=psW0[:, :], lhsT=sel0[:, :], rhs=bcrows[:, n * Lc:(n + 1) * Lc],
                start=True, stop=True),
                waits=([(prefix + "dbc", d_bc)] if n == 0 else
                       ([(vs, bev)] if bev else ())), inc=ts)
            t_b = p.val(ts)
            p.op("vector", lambda e, n=n: e.tensor_copy(
                out=Brep[:, n * Lc:(n + 1) * Lc], in_=psW0[:, :]),
                waits=[(ts, t_b)], inc=vs)
            bev = p.val(vs)
            p.op("tensor", lambda e, n=n: e.matmul(
                out=psW1[:, :], lhsT=sel1[:, :], rhs=bcrows[:, n * Lc:(n + 1) * Lc],
                start=True, stop=True),
                waits=([(vs, cev)] if cev else ()), inc=ts)
            t_c = p.val(ts)
            p.op("vector", lambda e, n=n: e.tensor_copy(
                out=Crep[:, n * Lc:(n + 1) * Lc], in_=psW1[:, :]),
                waits=[(ts, t_c)], inc=vs)
            cev = p.val(vs)
        v_bc = p.val(vs)
        for t in range(NDT):
            ps = psW0 if t % 2 == 0 else psW1
            p.op("tensor", lambda e, t=t, ps=ps: e.matmul(
                out=ps[:, :], lhsT=dtW[:, t * 128:(t + 1) * 128], rhs=xdbf[:, :],
                start=True, stop=True),
                waits=[(vs, v_bc)] if t == 0 else (), inc=ts)
            t_dt = p.val(ts)
            p.op("scalar", lambda e, t=t, ps=ps: e.activation(
                out=yf[:, :], in_=ps[:, :], func=AF.Abs, bias=dtb[:, t:t + 1]),
                waits=[(ts, t_dt)], inc=ss)
            p.op("scalar", lambda e: e.activation(
                out=yf[:, :], in_=yf[:, :], func=AF.Exp, scale=-1.0), inc=ss)
            p.op("scalar", lambda e: e.activation(
                out=yf[:, :], in_=yf[:, :], func=AF.Ln, bias=1.0), inc=ss)
            s_sp = p.val(ss)
            p.op("vector", lambda e, t=t, ps=ps: e.scalar_tensor_tensor(
                out=dAb[:, :], in0=ps[:, :], scalar=dtb[:, t:t + 1],
                op0=ALU.add, op1=ALU.max, in1=m.const(0.0).broadcast_to([128, Lc])),
                waits=[(ss, s_sp)], inc=vs)
            p.op("vector", lambda e, t=t: e.tensor_add(
                out=delta[:, t * Lc:(t + 1) * Lc], in0=dAb[:, :], in1=yf[:, :]),
                inc=vs)
            v_dl = p.val(vs)
            p.op("tensor", lambda e: None, waits=[(vs, v_dl)], real=False)
        NH = NN // 2
        dAh = [bcrows[:, 0:NH * Lc], bcrows[:, NH * Lc:NN * Lc]]
        dAh3 = [a.rearrange("p (n l) -> p n l", l=Lc) for a in dAh]
        dBu_all = ubig[:, 0:NN * Lc]
        dBu3 = dBu_all.rearrange("p (n l) -> p n l", l=Lc)
        Brep3 = Brep[:, :].rearrange("p (n l) -> p n l", l=Lc)
        zc = m.const(0.0)
        scan_done = [0, 0]
        for t in range(NDT):
            p.op("vector", lambda e, t=t: e.tensor_mul(
                out=du[:, :], in0=delta[:, t * Lc:(t + 1) * Lc],
                in1=uc[:, t * Lc:(t + 1) * Lc]), inc=vs)
            v_du = p.val(vs)
            p.op("vector", lambda e: e.tensor_tensor(
                out=dBu3, in0=Brep3,
                in1=du[:, :].unsqueeze(1).broadcast_to([128, NN, Lc]),
                op=ALU.mult), inc=vs)
            for h in range(2):
                base = t * NN + h * NH
                war = ([(vs, scan_done[h])] if scan_done[h]
                       else [(vs, v_du)])
                for n in range(NH):
                    p.op("scalar", lambda e, t=t, n=n, h=h, base=base: e.activation(
                        out=dAh[h][:, n * Lc:(n + 1) * Lc],
                        in_=delta[:, t * Lc:(t + 1) * Lc], func=AF.Exp,
                        scale=Asb[:, base + n:base + n + 1]),
                        waits=war if n == 0 else (), inc=ss)
                s_dA = p.val(ss)
                p.op("vector", lambda e, h=h, base=base: e.tensor_tensor(
                    out=tmp16[:, 0:NH].unsqueeze(2), in0=dAh3[h][:, :, 0:1],
                    in1=carry[:, base:base + NH].unsqueeze(2), op=ALU.mult),
                    waits=[(ss, s_dA)], inc=vs)
                p.op("vector", lambda e, h=h: e.tensor_add(
                    out=dBu3[:, h * NH:(h + 1) * NH, 0:1],
                    in0=dBu3[:, h * NH:(h + 1) * NH, 0:1],
                    in1=tmp16[:, 0:NH].unsqueeze(2)), inc=vs)
                p.op("vector", lambda e, h=h: e.tensor_scalar(
                    out=dAh3[h][:, :, 0:1], in0=dAh3[h][:, :, 0:1],
                    scalar1=0.0, scalar2=None, op0=ALU.mult), inc=vs)
                p.op("vector", lambda e, h=h: e.tensor_tensor_scan(
                    out=states[:, h * NH * Lc:(h + 1) * NH * Lc], data0=dAh[h],
                    data1=dBu_all[:, h * NH * Lc:(h + 1) * NH * Lc], initial=zc,
                    op0=ALU.mult, op1=ALU.add), inc=vs)
                scan_done[h] = p.val(vs)
            v_scan = p.val(vs)
            sv = states.rearrange("p (n l) -> p n l", l=Lc)
            p.op("vector", lambda e, t=t, sv=sv: e.tensor_copy(
                out=carry[:, t * NN:(t + 1) * NN].unsqueeze(2),
                in_=sv[:, 0:NN, Lc - 1:Lc]), inc=vs)
            p.op("vector", lambda e: e.tensor_mul(
                out=states[:, 0:NN * Lc], in0=states[:, 0:NN * Lc],
                in1=Crep[:, :]), inc=vs)
            half = NN * Lc // 2
            p.op("vector", lambda e, half=half: e.tensor_add(
                out=states[:, 0:half], in0=states[:, 0:half],
                in1=states[:, half:2 * half]), inc=vs)
            q = half // 2
            p.op("vector", lambda e, q=q: e.tensor_add(
                out=states[:, 0:q], in0=states[:, 0:q], in1=states[:, q:2 * q]),
                inc=vs)
            o8 = q // 2
            p.op("vector", lambda e, o8=o8: e.tensor_add(
                out=states[:, 0:o8], in0=states[:, 0:o8], in1=states[:, o8:2 * o8]),
                inc=vs)
            p.op("vector", lambda e: e.tensor_add(
                out=yf[:, :], in0=states[:, 0:Lc], in1=states[:, Lc:2 * Lc]),
                inc=vs)
            p.op("vector", lambda e, t=t: e.scalar_tensor_tensor(
                out=yg[:, t * Lc:(t + 1) * Lc], in0=uc[:, t * Lc:(t + 1) * Lc],
                scalar=Dd[:, t:t + 1], in1=yf[:, :], op0=ALU.mult, op1=ALU.add),
                inc=vs)
        v_y = p.val(vs)
        zdone = [(prefix + "dst", p.val(prefix + "dst"))]
        for t in range(NDT):
            zb_t = zt if t % 2 == 0 else xdbf
            sz_t = dAb if t % 2 == 0 else dAb2
            dz = prefix + ("dza" if t % 2 == 0 else "dzb")
            dzg = p.op("gpsimd", lambda e, t=t, zb_t=zb_t: e.dma_start(
                out=zb_t[:, :], in_=zstash[t, :, :]),
                waits=(zdone + [(ss, p.val(ss)), (vs, p.val(vs))]) if t < 2 else
                      [(ss, p.val(ss))], inc=dz, inc_amt=16)
            p.op("scalar", lambda e, zb_t=zb_t, sz_t=sz_t: e.activation(
                out=sz_t[:, :], in_=zb_t[:, :], func=AF.Silu),
                waits=[(dz, dzg), (vs, p.val(vs))], inc=ss)
            s_z = p.val(ss)
            p.op("vector", lambda e, t=t, sz_t=sz_t: e.tensor_mul(
                out=yg[:, t * Lc:(t + 1) * Lc], in0=yg[:, t * Lc:(t + 1) * Lc],
                in1=sz_t[:, :]), waits=[(ss, s_z)], inc=vs)
        v_g = p.val(vs)
        xwdone = [(prefix + "dst", p.val(prefix + "dst"))]
        opmm = [0, 0]
        for o in range(ND):
            par = o % 2
            dop = prefix + ("dpa" if par == 0 else "dpb")
            dw = p.op("gpsimd", lambda e, o=o, par=par: e.dma_start(
                out=opW[par][:, :].rearrange("p (k j) -> p k j", j=128),
                in_=d["opT"][:, o * 128:(o + 1) * 128].rearrange(
                    "(k p) j -> p k j", p=128)),
                waits=([(ts, opmm[par])] if opmm[par] else [(ts, p.val(ts))]),
                inc=dop, inc_amt=16)
            dxw = p.op("sync", lambda e, o=o: e.dma_start(
                out=zt[:, :], in_=xwstash[o, :, :]),
                waits=(xwdone + [(vs, p.val(vs))]) if o == 0 else
                      [(vs, p.val(vs))], inc=prefix + "dxw", inc_amt=16)
            for k in range(NDT):
                p.op("tensor", lambda e, o=o, k=k, par=par: e.matmul(
                    out=psO[:, :], lhsT=opW[par][:, k * 128:(k + 1) * 128],
                    rhs=yg[:, k * Lc:(k + 1) * Lc], start=(k == 0), stop=(k == NDT - 1)),
                    waits=[(dop, dw), (vs, v_g)] if k == 0 else (),
                    inc=ts if k == NDT - 1 else None)
            t_o = p.val(ts)
            opmm[par] = t_o
            p.op("vector", lambda e, o=o: e.tensor_add(
                out=yf[:, :], in0=psO[:, :], in1=zt[:, :]),
                waits=[(ts, t_o), (prefix + "dxw", dxw)], inc=vs)
            p.op("vector", lambda e, o=o: e.tensor_add(
                out=xo[o], in0=yf[:, :], in1=sx[o][:, :]), inc=vs)
            v_x = p.val(vs)
            p.op("sync", lambda e, o=o, sl=sl: e.dma_start(
                out=xout_ap(o, sl), in_=xo[o]),
                waits=[(vs, v_x)], inc=prefix + "do", inc_amt=16)
            p.op("tensor", lambda e: None, waits=[(vs, v_x)], real=False)
        prev_bar = p.barrier()
    p.wait("sync", [(prefix + "do", p.val(prefix + "do"))])


def build_att1_launch(in_shapes, L=4096, Lc=512):
    nc = bass.Bass("TRN2", target_bir_lowering=False)
    d = declare_inputs(nc, in_shapes)
    xout = nc.dram_tensor("xout", [ND, 128, L], F32, kind="ExternalOutput").ap()
    p = Plan(nc)
    m = Sbuf(p)
    m.dram = d
    emit_const_loads(p, m)
    emit_att1(p, m, lambda i, sl: d["xin"][i, :, sl],
              lambda o, sl: xout[o, :, sl], L, Lc)
    p.build()
    return nc, m



def emit_att0(p, m, xin_ap, xout_ap, T, prefix="at", entry_waits=()):
    nc = p.nc
    d = m.dram
    Lc = 512
    NCH = T // Lc
    NTC = T // 128
    G = T // 64
    NPR = 8
    ds, dsg, vs, ss, ts = (prefix + "d", prefix + "dg", prefix + "v",
                           prefix + "s", prefix + "t")

    sx = [m.sb(f"{prefix}sx{i}", (128, Lc)) for i in range(ND)]
    xln = [m.sb(f"{prefix}xln{i}", (128, Lc), BF16) for i in range(ND)]
    B2 = lambda nm, sh, dt=BF16: [m.sb(f"{prefix}{nm}{j}", sh, dt) for j in range(2)]
    qp, kp, vT, ET, LT = (B2("qp", (128, T)), B2("kp", (128, T)), B2("vT", (128, T)),
                          B2("ET", (128, T)), B2("LT", (128, T)))
    wqkv = [[m.sb(f"{prefix}wqkv{j}_{par}", (128, ND * 128), BF16) for j in range(3)]
            for par in range(2)]
    xlc = [m.sb(f"{prefix}xlc{par}_{i}", (128, Lc), BF16) for par in range(2)
           for i in range(ND)]
    xlt = lambda par, i: xlc[par * ND + i]
    klraw, qlraw = B2("klraw", (128, 64), F32), B2("qlraw", (128, 64), F32)
    blk_k, blk_q = B2("blk_k", (128, 128)), B2("blk_q", (128, 128))
    Kblk, Vblk, KVb, Tb = (B2("Kblk", (128, 128)), B2("Vblk", (128, 128)),
                           B2("KVb", (128, 128)), B2("Tb", (128, 128)))
    PTsb, NVsb, PNv = B2("PTsb", (128, 128)), B2("NVsb", (128, 128)), B2("PNv", (128, 128))
    sm2, sm2b = B2("sm2", (128, 2), F32), B2("sm2b", (128, 2), F32)
    Lnc, vbf, Ech = B2("Lnc", (128, 128)), B2("vbf", (128, Lc)), B2("Ech", (128, Lc))
    Nmx, Ers = B2("Nmx", (128, NCH + 1), F32), B2("Ers", (128, NCH + 1), F32)
    col1 = B2("col1", (128, 1), F32)
    ytmp = B2("ytmp", (128, Lc))
    w2sb = m.sb(prefix + "w2sb", (128, ND * 1024), BF16)
    ypr = [m.sb(f"{prefix}ypr{pr}", (128, Lc), BF16) for pr in range(NPR)]
    xo = [m.sb(f"{prefix}xo{i}", (128, Lc)) for i in range(ND)]
    eye = m.sb("k_eye_bf", (128, 128), BF16)
    onesblk = m.sb("k_onesblk_bf", (128, 128), BF16)

    psW0 = m.ps(prefix + "psW0", (128, Lc))
    psW1 = m.ps(prefix + "psW1", (128, Lc))
    psWn = [psW0, psW1]
    psQ = [m.ps(f"{prefix}psQ{j}", (128, Lc)) for j in range(2)]
    psNV = [m.ps(f"{prefix}psNV{j}", (128, 128)) for j in range(2)]
    pst = [m.ps(f"{prefix}pst{j}", (128, 128), BF16) for j in range(2)]
    pstQ = [psQ[1].bitcast(BF16)[:, j * 128:(j + 1) * 128] for j in range(2)]
    m.alias("psA", psW0)
    m.alias("psB", psW1)
    m.alias("psR", psQ[0])

    _uq = sum(len(v) for v in p.ops.values())
    xlnstash = nc.dram_tensor(f"{prefix}_xlnst{_uq}", [ND, 128, T], BF16, kind="Internal").ap()
    ystash = nc.dram_tensor(f"{prefix}_yst{_uq}", [NPR, 128, T], BF16, kind="Internal").ap()

    for i in range(ND):
        m.alias(f"ln_xbf{i}", ET[0][:, i * Lc:(i + 1) * Lc])
        m.alias(f"ln_xsq{i}", ET[1][:, i * Lc:(i + 1) * Lc])

    prev_bar = tuple(entry_waits)
    for c in range(NCH):
        sl = slice(c * Lc, (c + 1) * Lc)
        dx = prefix + ("dxa" if c % 2 == 0 else "dxb")
        dgot = None
        for i in range(ND):
            dgot = p.op("sync", lambda e, i=i, sl=sl: e.dma_start(
                out=sx[i][:, :], in_=xin_ap(i, sl)),
                waits=prev_bar if i == 0 else (), inc=dx, inc_amt=16)
        emit_ln(p, m, [sx[i][:, :] for i in range(ND)], xln, Lc, tag=prefix,
                dwaits=[(dx, dgot)], vsem=vs, ssem=ss, tsem=ts)
        v_ln = p.val(vs)
        for i in range(ND):
            p.op("sync", lambda e, i=i, sl=sl: e.dma_start(
                out=xlnstash[i, :, sl], in_=xln[i][:, :]),
                waits=[(vs, v_ln)] if i == 0 else (), inc=prefix + "dst", inc_amt=16)
        prev_bar = p.barrier()
    stash_done = [(prefix + "dst", p.val(prefix + "dst"))]

    qkv_mm_done = [0, 0]
    chain_done = [0, 0]
    xlc_done = [0, 0]
    vt_evac_done = [0]
    lm_state = {}

    def emit_qkv(pr):
        yield
        par = pr % 2
        dwsem = prefix + ("dwa" if par == 0 else "dwb")
        dw = None
        for j, coff in enumerate((pr * 128, 1024 + pr * 128, 2048 + pr * 128)):
            dw = p.op("gpsimd", lambda e, j=j, coff=coff, par=par: e.dma_start(
                out=wqkv[par][j][:, :].rearrange("p (k j2) -> p k j2", j2=128),
                in_=d["wattn"][:, coff:coff + 128].rearrange("(k p) j2 -> p k j2", p=128)),
                waits=[(ts, qkv_mm_done[par])] if j == 0 else (),
                inc=dwsem, inc_amt=16)
        first = [(vs, chain_done[par])] if chain_done[par] else []
        for c in range(NCH):
            sl = slice(c * Lc, (c + 1) * Lc)
            cpar = c % 2
            dx = prefix + ("dla" if cpar == 0 else "dlb")
            dl = None
            for i in range(ND):
                w0 = (stash_done + [(ts, p.val(ts))]) if (pr == 0 and c == 0 and i == 0) \
                    else ((([(ts, xlc_done[cpar])] if xlc_done[cpar] else []) + first)
                          if i == 0 else ())
                dl = p.op("sync", lambda e, i=i, sl=sl, cpar=cpar: e.dma_start(
                    out=xlt(cpar, i)[:, :], in_=xlnstash[i, :, sl]),
                    waits=w0, inc=dx, inc_amt=16)
            for j, dst in ((0, qp[par]), (1, kp[par])):
                ps = psQ[j]
                for k in range(ND):
                    w0 = ([(dx, dl), (dwsem, dw)] +
                          ([(vs, vt_evac_done[0])] if vt_evac_done[0] else [])
                          ) if k == 0 else ()
                    p.op("tensor", lambda e, j=j, k=k, ps=ps, cpar=cpar, par=par: e.matmul(
                        out=ps[:, :], lhsT=wqkv[par][j][:, k * 128:(k + 1) * 128],
                        rhs=xlt(cpar, k)[:, :], start=(k == 0), stop=(k == ND - 1)),
                        waits=w0, inc=ts if k == ND - 1 else None)
                t_qk = p.val(ts)
                p.op("vector", lambda e, j=j, dst=dst, sl=sl, ps=ps: e.tensor_copy(
                    out=dst[:, sl], in_=ps[:, :]), waits=[(ts, t_qk)], inc=vs)
            v_qk = p.val(vs)
            p.op("tensor", lambda e: None, waits=[(vs, v_qk)], real=False)
            for k in range(ND):
                p.op("tensor", lambda e, k=k, cpar=cpar, par=par: e.matmul(
                    out=psQ[0][:, :], lhsT=wqkv[par][2][:, k * 128:(k + 1) * 128],
                    rhs=xlt(cpar, k)[:, :], start=(k == 0), stop=(k == ND - 1)),
                    inc=ts if k == ND - 1 else None)
            t_v = p.val(ts)
            xlc_done[cpar] = t_v
            p.op("vector", lambda e, par=par: e.tensor_copy(out=vbf[par][:, :],
                                                            in_=psQ[0][:, :]),
                 waits=[(ts, t_v)], inc=vs)
            v_vb = p.val(vs)
            for j2 in range(Lc // 128):
                tp_ = pstQ[j2 % 2]
                p.op("tensor", lambda e, j2=j2, par=par, tp_=tp_: e.transpose(
                    out=tp_[:, :], in_=vbf[par][:, j2 * 128:(j2 + 1) * 128],
                    identity=eye[:, :]),
                    waits=[(vs, v_vb)] if j2 == 0 else [(vs, p.val(vs))], inc=ts)
                t_tr = p.val(ts)
                p.op("vector", lambda e, c=c, j2=j2, par=par, tp_=tp_: e.tensor_copy(
                    out=vT[par][:, (c * 4 + j2) * 128:(c * 4 + j2 + 1) * 128],
                    in_=tp_[:, :]), waits=[(ts, t_tr)], inc=vs)
            vt_evac_done[0] = p.val(vs)
            yield
        qkv_mm_done[par] = p.val(ts)
        p.op("vector", lambda e, par=par: e.tensor_reduce(
            out=qlraw[par][:, :], in_=qp[par][:, :].rearrange("p (mm g) -> p mm g", mm=64),
            axis=AX, op=ALU.add), waits=[(vs, p.val(vs))], inc=vs)
        p.op("vector", lambda e, par=par: e.tensor_reduce(
            out=klraw[par][:, :], in_=kp[par][:, :].rearrange("p (mm g) -> p mm g", mm=64),
            axis=AX, op=ALU.add), inc=vs)
        p.op("vector", lambda e, par=par: e.memset(blk_k[par][:, :], 0.0), inc=vs)
        p.op("vector", lambda e, par=par: e.memset(blk_q[par][:, :], 0.0), inc=vs)
        p.op("vector", lambda e, par=par: e.memset(sm2b[par][:, :], 0.0), inc=vs)
        v_lm = p.val(vs)
        p.op("scalar", lambda e, par=par: e.activation(
            out=blk_k[par][0:64, 0:64], in_=klraw[par][0:64, :], func=AF.Copy,
            scale=1.0 / G), waits=[(vs, v_lm)], inc=ss)
        p.op("scalar", lambda e, par=par: e.activation(
            out=blk_k[par][64:128, 64:128], in_=klraw[par][64:128, :], func=AF.Copy,
            scale=1.0 / G), inc=ss)
        p.op("scalar", lambda e, par=par: e.activation(
            out=blk_q[par][0:64, 0:64], in_=qlraw[par][0:64, :], func=AF.Copy,
            scale=1.0 / G), inc=ss)
        p.op("scalar", lambda e, par=par: e.activation(
            out=blk_q[par][64:128, 64:128], in_=qlraw[par][64:128, :], func=AF.Copy,
            scale=1.0 / G), inc=ss)
        lm_state[pr] = (p.val(vs), p.val(ss))

    def emit_chain(pr):
        yield
        par = pr % 2
        v_lm, s_blk = lm_state[pr]
        for tc in range(NTC):
            pv = psNV[par]
            tp_ = pst[par]
            p.op("tensor", lambda e, tc=tc, par=par, pv=pv: e.matmul(
                out=pv[:, :], lhsT=qp[par][:, tc * 128:(tc + 1) * 128],
                rhs=blk_k[par][:, :], start=True, stop=True),
                waits=[(ss, s_blk)] if tc == 0 else [(vs, p.val(vs))], inc=ts)
            t_L = p.val(ts)
            p.op("vector", lambda e, par=par, pv=pv: e.tensor_reduce(
                out=sm2[par][:, :], in_=pv[:, :].rearrange("p (h mm) -> p h mm", h=2),
                axis=AX, op=ALU.max), waits=[(ts, t_L)], inc=vs)
            p.op("vector", lambda e, par=par: e.tensor_scalar(
                out=sm2[par][:, :], in0=sm2[par][:, :], scalar1=-1.0, scalar2=None,
                op0=ALU.mult), inc=vs)
            v_mx = p.val(vs)
            p.op("scalar", lambda e, par=par, pv=pv: e.activation(
                out=Lnc[par][:, 0:64], in_=pv[:, 0:64], func=AF.Exp,
                bias=sm2[par][:, 0:1], accum_out=sm2b[par][:, 0:1]),
                waits=[(vs, v_mx)], inc=ss)
            p.op("scalar", lambda e, par=par, pv=pv: e.activation(
                out=Lnc[par][:, 64:128], in_=pv[:, 64:128], func=AF.Exp,
                bias=sm2[par][:, 1:2], accum_out=sm2b[par][:, 1:2]), inc=ss)
            s_ex = p.val(ss)
            p.op("vector", lambda e, par=par: e.reciprocal(out=sm2[par][:, :],
                                                           in_=sm2b[par][:, :]),
                 waits=[(ss, s_ex)], inc=vs)
            p.op("vector", lambda e, par=par: e.tensor_scalar(
                out=Lnc[par][:, 0:64], in0=Lnc[par][:, 0:64], scalar1=sm2[par][:, 0:1],
                scalar2=None, op0=ALU.mult), waits=[(vs, p.val(vs))], inc=vs)
            p.op("vector", lambda e, par=par: e.tensor_scalar(
                out=Lnc[par][:, 64:128], in0=Lnc[par][:, 64:128],
                scalar1=sm2[par][:, 1:2], scalar2=None, op0=ALU.mult), inc=vs)
            v_Ln = p.val(vs)
            p.op("tensor", lambda e, par=par, tp_=tp_: e.transpose(
                out=tp_[:, :], in_=Lnc[par][:, :], identity=eye[:, :]),
                waits=[(vs, v_Ln)], inc=ts)
            t_lt = p.val(ts)
            p.op("vector", lambda e, tc=tc, par=par, tp_=tp_: e.tensor_copy(
                out=LT[par][:, tc * 128:(tc + 1) * 128], in_=tp_[:, :]),
                waits=[(ts, t_lt)], inc=vs)
            yield
        for c in range(NCH):
            p.op("tensor", lambda e, c=c, par=par: e.matmul(
                out=psWn[par][:, :], lhsT=blk_q[par][:, :], rhs=kp[par][:, c * Lc:(c + 1) * Lc],
                start=True, stop=True), waits=[(vs, p.val(vs))], inc=ts)
            t_N = p.val(ts)
            p.op("vector", lambda e, c=c, par=par: e.tensor_reduce(
                out=Nmx[par][:, c:c + 1], in_=psWn[par][:, :], axis=AX, op=ALU.max),
                waits=[(ts, t_N)], inc=vs)
            yield
        p.op("vector", lambda e, par=par: e.tensor_reduce(
            out=Nmx[par][:, NCH:NCH + 1], in_=Nmx[par][:, 0:NCH], axis=AX, op=ALU.max),
            inc=vs)
        p.op("vector", lambda e, par=par: e.tensor_scalar(
            out=col1[par][:, :], in0=Nmx[par][:, NCH:NCH + 1], scalar1=-1.0,
            scalar2=None, op0=ALU.mult), inc=vs)
        v_gm = p.val(vs)
        for c in range(NCH):
            p.op("tensor", lambda e, c=c, par=par: e.matmul(
                out=psWn[par][:, :], lhsT=blk_q[par][:, :], rhs=kp[par][:, c * Lc:(c + 1) * Lc],
                start=True, stop=True),
                waits=[(vs, v_gm)] if c == 0 else [(ss, p.val(ss))], inc=ts)
            t_N = p.val(ts)
            p.op("scalar", lambda e, c=c, par=par: e.activation(
                out=Ech[par][:, :], in_=psWn[par][:, :], func=AF.Exp, bias=col1[par][:, 0:1],
                accum_out=Ers[par][:, c:c + 1]), waits=[(ts, t_N)], inc=ss)
            s_E = p.val(ss)
            for j2 in range(Lc // 128):
                tp_ = pst[par]
                p.op("tensor", lambda e, j2=j2, par=par, tp_=tp_: e.transpose(
                    out=tp_[:, :], in_=Ech[par][:, j2 * 128:(j2 + 1) * 128],
                    identity=eye[:, :]),
                    waits=[(ss, s_E)] if j2 == 0 else [(vs, p.val(vs))], inc=ts)
                t_te = p.val(ts)
                p.op("vector", lambda e, c=c, j2=j2, par=par, tp_=tp_: e.tensor_copy(
                    out=ET[par][:, (c * 4 + j2) * 128:(c * 4 + j2 + 1) * 128],
                    in_=tp_[:, :]), waits=[(ts, t_te)], inc=vs)
            yield
        p.op("vector", lambda e, par=par: e.tensor_reduce(
            out=Ers[par][:, NCH:NCH + 1], in_=Ers[par][:, 0:NCH], axis=AX, op=ALU.add),
            inc=vs)
        p.op("vector", lambda e, par=par: e.reciprocal(
            out=col1[par][:, :], in_=Ers[par][:, NCH:NCH + 1]),
            waits=[(vs, p.val(vs))], inc=vs)
        v_rsn = p.val(vs)
        for tc in range(NTC):
            p.op("tensor", lambda e, tc=tc, par=par: e.matmul(
                out=psNV[par][:, :], lhsT=ET[par][:, tc * 128:(tc + 1) * 128],
                rhs=vT[par][:, tc * 128:(tc + 1) * 128], start=(tc == 0),
                stop=(tc == NTC - 1)),
                waits=[(vs, v_rsn)] if tc == 0 else (),
                inc=ts if tc == NTC - 1 else None)
        t_nv = p.val(ts)
        yield
        p.op("vector", lambda e, par=par: e.memset(NVsb[par][:, :], 0.0),
             waits=[(ts, t_nv)], inc=vs)
        p.op("vector", lambda e, par=par: e.tensor_scalar(
            out=NVsb[par][0:64, 0:64], in0=psNV[par][0:64, 0:64],
            scalar1=col1[par][0:64, 0:1], scalar2=None, op0=ALU.mult),
            waits=[(vs, p.val(vs))], inc=vs)
        p.op("vector", lambda e, par=par: e.tensor_scalar(
            out=NVsb[par][64:128, 64:128], in0=psNV[par][64:128, 64:128],
            scalar1=col1[par][64:128, 0:1], scalar2=None, op0=ALU.mult), inc=vs)
        v_nv = p.val(vs)
        p.op("tensor", lambda e, par=par: e.matmul(
            out=psNV[par][:, :], lhsT=blk_q[par][:, :], rhs=blk_k[par][:, :],
            start=True, stop=True), waits=[(vs, v_nv)], inc=ts)
        t_P = p.val(ts)
        p.op("vector", lambda e, par=par: e.tensor_reduce(
            out=sm2[par][:, :], in_=psNV[par][:, :].rearrange("p (h mm) -> p h mm", h=2),
            axis=AX, op=ALU.max), waits=[(ts, t_P)], inc=vs)
        p.op("vector", lambda e, par=par: e.tensor_scalar(
            out=sm2[par][:, :], in0=sm2[par][:, :], scalar1=-1.0, scalar2=None,
            op0=ALU.mult), inc=vs)
        p.op("vector", lambda e, par=par: e.memset(Kblk[par][:, :], 0.0), inc=vs)
        v_pm = p.val(vs)
        p.op("scalar", lambda e, par=par: e.activation(
            out=Kblk[par][0:64, 0:64], in_=psNV[par][0:64, 0:64], func=AF.Exp,
            bias=sm2[par][0:64, 0:1], accum_out=sm2b[par][0:64, 0:1]),
            waits=[(vs, v_pm)], inc=ss)
        p.op("scalar", lambda e, par=par: e.activation(
            out=Kblk[par][64:128, 64:128], in_=psNV[par][64:128, 64:128], func=AF.Exp,
            bias=sm2[par][64:128, 1:2], accum_out=sm2b[par][64:128, 1:2]), inc=ss)
        s_pk = p.val(ss)
        p.op("vector", lambda e, par=par: e.reciprocal(out=sm2[par][:, 0:1],
                                                       in_=sm2b[par][:, 0:1]),
             waits=[(ss, s_pk)], inc=vs)
        p.op("vector", lambda e, par=par: e.reciprocal(out=sm2[par][:, 1:2],
                                                       in_=sm2b[par][:, 1:2]), inc=vs)
        p.op("vector", lambda e, par=par: e.tensor_scalar(
            out=Kblk[par][0:64, 0:64], in0=Kblk[par][0:64, 0:64],
            scalar1=sm2[par][0:64, 0:1], scalar2=None, op0=ALU.mult),
            waits=[(vs, p.val(vs))], inc=vs)
        p.op("vector", lambda e, par=par: e.tensor_scalar(
            out=Kblk[par][64:128, 64:128], in0=Kblk[par][64:128, 64:128],
            scalar1=sm2[par][64:128, 1:2], scalar2=None, op0=ALU.mult), inc=vs)
        v_K = p.val(vs)
        p.op("tensor", lambda e, par=par: e.matmul(
            out=psNV[par][:, :], lhsT=onesblk[:, :], rhs=Kblk[par][:, :],
            start=True, stop=True), waits=[(vs, v_K)], inc=ts)
        t_cs = p.val(ts)
        p.op("vector", lambda e, par=par: e.tensor_reduce(
            out=sm2b[par][:, 0:1], in_=psNV[par][:, :], axis=AX, op=ALU.max),
            waits=[(ts, t_cs)], inc=vs)
        p.op("vector", lambda e, par=par: e.reciprocal(out=sm2b[par][:, 1:2],
                                                       in_=sm2b[par][:, 0:1]),
             waits=[(vs, p.val(vs))], inc=vs)
        v_dn = p.val(vs)
        p.op("tensor", lambda e, par=par: e.transpose(
            out=pst[par][:, :], in_=Kblk[par][:, :], identity=eye[:, :]),
            waits=[(vs, v_dn)], inc=ts)
        t_kt = p.val(ts)
        p.op("vector", lambda e, par=par: e.tensor_scalar(
            out=Vblk[par][:, :], in0=pst[par][:, :], scalar1=sm2b[par][:, 1:2],
            scalar2=None, op0=ALU.mult), waits=[(ts, t_kt)], inc=vs)
        p.op("vector", lambda e, par=par: e.tensor_copy(out=PTsb[par][:, :],
                                                        in_=pst[par][:, :]), inc=vs)
        v_v0 = p.val(vs)
        for it in range(6):
            p.op("tensor", lambda e, par=par: e.matmul(
                out=psNV[par][:, :], lhsT=PTsb[par][:, :], rhs=Vblk[par][:, :],
                start=True, stop=True),
                waits=[(vs, v_v0 if it == 0 else p.val(vs))], inc=ts)
            t_kv = p.val(ts)
            p.op("vector", lambda e, par=par: e.tensor_copy(out=KVb[par][:, :],
                                                            in_=psNV[par][:, :]),
                 waits=[(ts, t_kv)], inc=vs)
            p.op("vector", lambda e, par=par: e.scalar_tensor_tensor(
                out=Tb[par][:, :], in0=eye[:, :], scalar=7.0, in1=psNV[par][:, :],
                op0=ALU.mult, op1=ALU.subtract), inc=vs)
            v_t1 = p.val(vs)
            p.op("tensor", lambda e, par=par: e.transpose(
                out=pst[par][:, :], in_=KVb[par][:, :], identity=eye[:, :]),
                waits=[(vs, v_t1)], inc=ts)
            t_kvt = p.val(ts)
            p.op("vector", lambda e, par=par: e.tensor_copy(out=KVb[par][:, :],
                                                            in_=pst[par][:, :]),
                 waits=[(ts, t_kvt)], inc=vs)
            v_kvt = p.val(vs)
            p.op("tensor", lambda e, par=par: e.matmul(
                out=psNV[par][:, :], lhsT=KVb[par][:, :], rhs=Tb[par][:, :],
                start=True, stop=True), waits=[(vs, v_kvt)], inc=ts)
            t_t2 = p.val(ts)
            p.op("vector", lambda e, par=par: e.scalar_tensor_tensor(
                out=Tb[par][:, :], in0=eye[:, :], scalar=15.0, in1=psNV[par][:, :],
                op0=ALU.mult, op1=ALU.subtract), waits=[(ts, t_t2)], inc=vs)
            v_t3 = p.val(vs)
            p.op("tensor", lambda e, par=par: e.matmul(
                out=psNV[par][:, :], lhsT=KVb[par][:, :], rhs=Tb[par][:, :],
                start=True, stop=True), waits=[(vs, v_t3)], inc=ts)
            t_t4 = p.val(ts)
            p.op("vector", lambda e, par=par: e.scalar_tensor_tensor(
                out=Tb[par][:, :], in0=eye[:, :], scalar=13.0, in1=psNV[par][:, :],
                op0=ALU.mult, op1=ALU.subtract), waits=[(ts, t_t4)], inc=vs)
            v_t5 = p.val(vs)
            p.op("tensor", lambda e, par=par: e.transpose(
                out=pst[par][:, :], in_=Vblk[par][:, :], identity=eye[:, :]),
                waits=[(vs, v_t5)], inc=ts)
            t_vt = p.val(ts)
            p.op("vector", lambda e, par=par: e.tensor_copy(out=PTsb[par][:, :],
                                                            in_=pst[par][:, :]),
                 waits=[(ts, t_vt)], inc=vs)
            v_vt = p.val(vs)
            p.op("tensor", lambda e, par=par: e.matmul(
                out=psNV[par][:, :], lhsT=PTsb[par][:, :], rhs=Tb[par][:, :],
                start=True, stop=True), waits=[(vs, v_vt)], inc=ts)
            t_vu = p.val(ts)
            p.op("scalar", lambda e, par=par: e.activation(
                out=Vblk[par][:, :], in_=psNV[par][:, :], func=AF.Copy, scale=0.25),
                waits=[(ts, t_vu)], inc=ss)
            s_vu = p.val(ss)
            if it < 5:
                p.op("tensor", lambda e, par=par: e.transpose(
                    out=pst[par][:, :], in_=Kblk[par][:, :], identity=eye[:, :]),
                    waits=[(ss, s_vu)], inc=ts)
                t_kt2 = p.val(ts)
                p.op("vector", lambda e, par=par: e.tensor_copy(
                    out=PTsb[par][:, :], in_=pst[par][:, :]),
                    waits=[(ts, t_kt2)], inc=vs)
            yield
        p.op("tensor", lambda e, par=par: e.transpose(
            out=pst[par][:, :], in_=Vblk[par][:, :], identity=eye[:, :]),
            waits=[(vs, p.val(vs)), (ss, p.val(ss))], inc=ts)
        t_pt = p.val(ts)
        p.op("vector", lambda e, par=par: e.tensor_copy(out=PTsb[par][:, :],
                                                        in_=pst[par][:, :]),
             waits=[(ts, t_pt)], inc=vs)
        v_pt = p.val(vs)
        p.op("tensor", lambda e, par=par: e.matmul(
            out=psNV[par][:, :], lhsT=PTsb[par][:, :], rhs=NVsb[par][:, :],
            start=True, stop=True), waits=[(vs, v_pt)], inc=ts)
        t_pnv = p.val(ts)
        p.op("vector", lambda e, par=par: e.tensor_copy(out=PNv[par][:, :],
                                                        in_=psNV[par][:, :]),
             waits=[(ts, t_pnv)], inc=vs)
        v_pnv = p.val(vs)
        for c in range(NCH):
            p.op("tensor", lambda e, c=c, par=par: e.matmul(
                out=psWn[par][:, :], lhsT=PNv[par][:, :], rhs=LT[par][:, c * Lc:(c + 1) * Lc],
                start=True, stop=True),
                waits=[(vs, v_pnv)] if c == 0 else [(vs, p.val(vs))], inc=ts)
            t_y = p.val(ts)
            p.op("vector", lambda e, par=par: e.tensor_copy(out=ytmp[par][:, :],
                                                            in_=psWn[par][:, :]),
                 waits=[(ts, t_y), (prefix + "dy", p.val(prefix + "dy"))], inc=vs)
            v_y = p.val(vs)
            p.op("gpsimd", lambda e, pr=pr, c=c, par=par: e.dma_start(
                out=ystash[pr, :, c * Lc:(c + 1) * Lc], in_=ytmp[par][:, :]),
                waits=[(vs, v_y)], inc=prefix + "dy", inc_amt=16)
            yield
        chain_done[par] = p.val(vs)

    def drain(g, n=10 ** 9):
        for _ in range(n):
            try:
                next(g)
            except StopIteration:
                return True
        return False

    drain(emit_qkv(0))
    prev = None
    for pr in range(1, NPR):
        cur = emit_chain(pr - 1)
        while prev is not None:
            done_p = drain(prev, 3)
            drain(cur, 1)
            if done_p:
                prev = None
        qg = emit_qkv(pr)
        while not drain(qg, 1):
            drain(cur, 2)
        prev = cur
    cur = emit_chain(NPR - 1)
    while prev is not None:
        done_p = drain(prev, 3)
        drain(cur, 1)
        if done_p:
            prev = None
    drain(cur)
    prev_bar = p.barrier()

    dwp = None
    for k in range(ND):
        dwp = p.op("gpsimd", lambda e, k=k: e.dma_start(
            out=w2sb[:, k * 1024:(k + 1) * 1024],
            in_=d["wproj"][k * 128:(k + 1) * 128, :]), inc=prefix + "dwp", inc_amt=16)
    y_done = [(prefix + "dy", p.val(prefix + "dy")), (prefix + "dwp", p.val(prefix + "dwp"))]
    for c in range(NCH):
        sl = slice(c * Lc, (c + 1) * Lc)
        dx = prefix + ("dra" if c % 2 == 0 else "drb")
        dl = None
        for i in range(ND):
            dl = p.op("sync", lambda e, i=i, sl=sl: e.dma_start(
                out=sx[i][:, :], in_=xin_ap(i, sl)),
                waits=prev_bar if i == 0 else (), inc=dx, inc_amt=16)
        for pr in range(NPR):
            dl = p.op("sync", lambda e, pr=pr, sl=sl: e.dma_start(
                out=ypr[pr][:, :], in_=ystash[pr, :, sl]),
                waits=y_done if (c == 0 and pr == 0) else (), inc=dx, inc_amt=16)
        for o in range(ND):
            for pr in range(NPR):
                p.op("tensor", lambda e, o=o, pr=pr: e.matmul(
                    out=psW0[:, :], lhsT=w2sb[:, pr * 1024 + o * 128:pr * 1024 + (o + 1) * 128],
                    rhs=ypr[pr][:, :], start=(pr == 0), stop=(pr == NPR - 1)),
                    waits=[(dx, dl)] if (o == 0 and pr == 0) else (),
                    inc=ts if pr == NPR - 1 else None)
            t_pr = p.val(ts)
            p.op("vector", lambda e, o=o: e.tensor_add(
                out=xo[o][:, :], in0=psW0[:, :], in1=sx[o][:, :]),
                waits=[(ts, t_pr)], inc=vs)
            v_x = p.val(vs)
            p.op("sync", lambda e, o=o, sl=sl: e.dma_start(
                out=xout_ap(o, sl), in_=xo[o][:, :]),
                waits=[(vs, v_x)], inc=prefix + "do", inc_amt=16)
            p.op("tensor", lambda e: None, waits=[(vs, v_x)], real=False)
        prev_bar = p.barrier()
    p.wait("sync", [(prefix + "do", p.val(prefix + "do"))])


def build_att0_launch(in_shapes, T=4096):
    nc = bass.Bass("TRN2", target_bir_lowering=False)
    d = declare_inputs(nc, in_shapes)
    xout = nc.dram_tensor("xout", [ND, 128, T], F32, kind="ExternalOutput").ap()
    p = Plan(nc)
    m = Sbuf(p)
    m.dram = d
    emit_const_loads(p, m)
    emit_att0(p, m, lambda i, sl: d["xin"][i, :, sl],
              lambda o, sl: xout[o, :, sl], T)
    p.build()
    return nc, m



def build_fused():
    nc = bass.Bass("TRN2", target_bir_lowering=False)
    shapes = {
        "xin": np.zeros((8, 8, 128, 512), np.float32),
        "wattn": np.zeros((1024, 3072), np.float32),
        "wproj": np.zeros((1024, 1024), np.float32),
        "ipT": np.zeros((1024, 4096), np.float32),
        "xpT": np.zeros((2048, 128), np.float32),
        "dtT": np.zeros((128, 2048), np.float32),
        "opT": np.zeros((2048, 1024), np.float32),
        "A_logT": np.zeros((128, 256), np.float32),
        "convWT": np.zeros((128, 64), np.float32),
        "convBT": np.zeros((128, 16), np.float32),
        "DT": np.zeros((128, 16), np.float32),
        "dtbT": np.zeros((128, 16), np.float32),
        "gammaT": np.zeros((128, 8), np.float32),
        "betaT": np.zeros((128, 8), np.float32),
        "wgT": np.zeros((1024, 3072), np.int32),
        "wuT": np.zeros((1024, 3072), np.int32),
        "wdT": np.zeros((3072, 1024), np.int32),
        **host_consts(),
    }
    for nm in ("wfT", "wiT", "wcT", "ufT", "uiT", "ucT", "woT", "uoT"):
        shapes[nm] = np.zeros((1024, 1024), np.float32)
    for nm in ("bfT", "biT", "bcT", "boT", "incT"):
        shapes[nm] = np.zeros((128, 8), np.float32)
    d = declare_inputs(nc, shapes)
    xout = nc.dram_tensor("xout", [ND, 128, 4096], F32, kind="ExternalOutput").ap()

    hshape = [8, ND, 128, 512]
    h1 = nc.dram_tensor("fz_h1", hshape, F32, kind="Internal")
    h2 = nc.dram_tensor("fz_h2", hshape, F32, kind="Internal")
    h3 = nc.dram_tensor("fz_h3", hshape, F32, kind="Internal")
    g1 = nc.dram_tensor("fz_g1", [8 * 8 * ND * 128, 512], F32, kind="Internal",
                        addr_space="Shared")
    g2 = nc.dram_tensor("fz_g2", [8 * 8 * ND * 128, 512], F32, kind="Internal",
                        addr_space="Shared")
    g3 = nc.dram_tensor("fz_g3", [8 * 8 * ND * 128, 512], F32, kind="Internal",
                        addr_space="Shared")
    xc_b1 = nc.dram_tensor("fz_xb1", [ND, 128, 4096], F32, kind="Internal").ap()
    xc_s1 = nc.dram_tensor("fz_xs1", hshape, F32, kind="Internal").ap()
    xc_b2 = nc.dram_tensor("fz_xb2", [ND, 128, 4096], F32, kind="Internal").ap()
    xc_b3 = nc.dram_tensor("fz_xb3", [ND, 128, 4096], F32, kind="Internal").ap()
    xc_b4 = nc.dram_tensor("fz_xb4", [ND, 128, 4096], F32, kind="Internal").ap()

    p = Plan(nc)
    m = Sbuf(p)
    m.dram = d
    emit_const_loads(p, m)
    m.push()
    emit_ffn1_prep(p, m)

    def phase_end():
        conds = p.barrier()
        for e in ENGINES:
            p.wait(e, conds)
        return conds

    def reshard(tag, h, g, emit_copy, store_sem):
        cc = tag + "cc"
        sw = tag + "sw"
        p.sem(cc); p.sem(sw)
        p.op("gpsimd", lambda e: e.collective_compute(
            "AllGather", ALU.bypass, replica_groups=[list(range(8))],
            ins=[h.ap().rearrange("a b c d -> (a b c) d")],
            outs=[g.ap()]),
            waits=[store_sem], inc=cc, inc_amt=1)
        cc_target = p.val(cc)
        p.counts[sw] += 8 * 16

        def switch_fn(nc_, rank, sems):
            nc_.sync.wait_ge(sems[cc], cc_target)
            for blk in range(8):
                emit_copy(nc_.sync, rank, blk).then_inc(sems[sw], 16)
        p.switch_segment(switch_fn)
        return [(sw, p.val(sw))]

    m.push()
    emit_ffn0(p, m, lambda t, i: d["xin"][t, i, :, :],
              lambda t, k: h1.ap()[t, k, :, :], 512)
    m.pop()
    f0_store = ("f0do", p.val("f0do"))
    phase_end()
    g1v = g1.ap().rearrange("(i b d p) l -> i b d p l", i=8, b=8, d=ND)
    xb1v = xc_b1.rearrange("d p (i l) -> i d p l", i=8)
    rw = reshard("r1", h1, g1,
                 lambda eng, rank, blk: eng.dma_start(
                     out=xb1v[blk], in_=g1v[blk, rank]), f0_store)

    def h2_out(o, sl):
        assert sl.start % 512 == 0 and sl.stop - sl.start == 512
        return h2.ap()[sl.start // 512, o, :, :]
    m.push()
    emit_att0(p, m, lambda i, sl: xc_b1[i, :, sl], h2_out, 4096, entry_waits=rw)
    m.pop()
    at_store = ("atdo", p.val("atdo"))
    phase_end()
    g2v = g2.ap().rearrange("(b j d p) l -> b j d p l", b=8, j=8, d=ND)
    rw = reshard("r2", h2, g2,
                 lambda eng, rank, blk: eng.dma_start(
                     out=xc_s1[blk], in_=g2v[blk, rank]), at_store)

    m.push()
    emit_ffn0(p, m, lambda t, i: xc_s1[t, i, :, :],
              lambda t, k: h3.ap()[t, k, :, :], 512, entry_waits=rw)
    m.pop()
    f0_store2 = ("f0do", p.val("f0do"))
    phase_end()
    g3v = g3.ap().rearrange("(i b d p) l -> i b d p l", i=8, b=8, d=ND)
    xb2v = xc_b2.rearrange("d p (i l) -> i d p l", i=8)
    rw = reshard("r3", h3, g3,
                 lambda eng, rank, blk: eng.dma_start(
                     out=xb2v[blk], in_=g3v[blk, rank]), f0_store2)
    m.pop()

    m.push()
    emit_ffn1(p, m, lambda i, sl: xc_b2[i, :, sl],
              lambda o, sl: xc_b3[o, :, sl], 4096, 512, entry_waits=rw)
    m.pop()
    phase_end()

    m.push()
    emit_att1(p, m, lambda i, sl: xc_b3[i, :, sl],
              lambda o, sl: xc_b4[o, :, sl], 4096, 512)
    m.pop()
    phase_end()

    m.push()
    emit_ffn1(p, m, lambda i, sl: xc_b4[i, :, sl],
              lambda o, sl: xout[o, :, sl], 4096, 512)
    m.pop()
    p.wait("sync", [("f1d", p.val("f1d"))])
    p.build()
    return nc, m



from concourse.bass_utils import run_bass_kernel_spmd

_FUSED_CACHE = {}


def kernel(**inputs):
    z = inputs
    if "prog" not in _FUSED_CACHE:
        _FUSED_CACHE["prog"] = build_fused()
    nc, m = _FUSED_CACHE["prog"]

    ins = {}
    for nm in ("wf", "wi", "wc", "uf", "ui", "uc", "wo", "uo"):
        ins[nm + "T"] = np.ascontiguousarray(z["L0_ffn_" + nm].T)
    for nm, src in (("bfT", "bf"), ("biT", "bi"), ("bcT", "bc"), ("boT", "bo"),
                    ("incT", "in_c")):
        ins[nm] = np.ascontiguousarray(z["L0_ffn_" + src].reshape(8, 128).T)
    ins["wattn"] = z["L0_attn_W_attn"]
    ins["wproj"] = z["L0_attn_W_proj"]
    g = {k[len("L1_attn_"):]: z[k] for k in z if k.startswith("L1_attn_")}
    ins["ipT"] = np.ascontiguousarray(g["in_proj_w"].T)
    xpT = np.zeros((2048, 128), np.float32); xpT[:, :96] = g["x_proj_w"].T
    ins["xpT"] = xpT
    dtT = np.zeros((128, 2048), np.float32); dtT[:64, :] = g["dt_proj_w"].T
    ins["dtT"] = dtT
    ins["opT"] = np.ascontiguousarray(g["out_proj_w"].T)
    ins["A_logT"] = np.ascontiguousarray(
        g["A_log"].reshape(16, 128, 16).transpose(1, 0, 2).reshape(128, 256))
    ins["convWT"] = np.ascontiguousarray(
        g["conv_w"][:, 0, :].reshape(16, 128, 4).transpose(1, 0, 2).reshape(128, 64))
    ins["convBT"] = np.ascontiguousarray(g["conv_b"].reshape(16, 128).T)
    ins["DT"] = np.ascontiguousarray(g["D"].reshape(16, 128).T)
    ins["dtbT"] = np.ascontiguousarray(g["dt_proj_b"].reshape(16, 128).T)
    ins["gammaT"] = np.ascontiguousarray(g["ln_gamma"].reshape(8, 128).T)
    ins["betaT"] = np.ascontiguousarray(g["ln_beta"].reshape(8, 128).T)
    ins["wgT"] = np.ascontiguousarray(z["L1_ffn_w_gate"].T)
    ins["wuT"] = np.ascontiguousarray(z["L1_ffn_w_up"].T)
    ins["wdT"] = np.ascontiguousarray(z["L1_ffn_w_down"].T)
    ins.update(host_consts(m.const_vals))

    xfm = np.ascontiguousarray(z["L0_attn_x"].transpose(0, 2, 1))
    in_maps = []
    for i in range(8):
        xs = np.ascontiguousarray(xfm[:, :, i * 512:(i + 1) * 512])
        in_maps.append({**ins, "xin": xs.reshape(8, 8, 128, 512)})
    res = run_bass_kernel_spmd(nc, in_maps, core_ids=list(range(8)))
    out = np.empty((8, 1024, 4096), np.float32)
    for b in range(8):
        out[b] = np.asarray(res.results[b]["xout"]).reshape(1024, 4096)
    return np.ascontiguousarray(out.transpose(0, 2, 1))


# version 14
# speedup vs baseline: 1.0150x; 1.0150x over previous
import numpy as np
from contextlib import ExitStack

import concourse.bass as bass
import concourse.mybir as mybir

F32 = mybir.dt.float32
BF16 = mybir.dt.bfloat16
I32 = mybir.dt.int32
AF = mybir.ActivationFunctionType
ALU = mybir.AluOpType
AX = mybir.AxisListType.X

D = 1024
ND = D // 128
LN_EPS = 1e-5
ENGINES = ("sync", "vector", "scalar", "tensor", "gpsimd")


class Plan:
    def __init__(self, nc):
        self.nc = nc
        self.ops = {e: [] for e in ENGINES}
        self.sems = {}
        self.counts = {}
        self.es = ExitStack()
        self.bar = {e: 0 for e in ENGINES}
        self._since_bar = {e: False for e in ENGINES}
        self._last = {e: None for e in ENGINES}
        self._last_inc = {e: None for e in ENGINES}

    def sem(self, name):
        if name not in self.sems:
            self.sems[name] = self.es.enter_context(self.nc.semaphore(name))
            self.counts[name] = 0
        return self.sems[name]

    def op(self, engine, fn, waits=(), inc=None, inc_amt=1, dma=None, real=True):
        for w, _ in waits:
            self.sem(w)
        if dma is None:
            dma = (inc_amt == 16)
        waits = list(waits)
        if not dma and real and self._last_inc[engine] is not None:
            se = self._last_inc[engine]
            if se not in waits:
                waits.append(se)
        if inc is not None:
            self.sem(inc)
            self.counts[inc] += inc_amt
        rec = {"fn": fn, "waits": waits, "dma": dma,
               "incs": [] if inc is None else [(inc, inc_amt)]}
        if inc is not None and not dma and real:
            self._last_inc[engine] = (inc, self.counts[inc])
        self.ops[engine].append(rec)
        if real:
            self._since_bar[engine] = True
            self._last[engine] = rec
        return self.counts[inc] if inc is not None else None

    def val(self, name):
        return self.counts.get(name, 0)

    def barrier(self):
        conds = dict(getattr(self, "_bar_conds", {}))
        for e in ENGINES:
            if self._since_bar[e]:
                rec = self._last[e]
                if rec["incs"]:
                    s, _ = rec["incs"][0]
                    conds[e] = (s, self.counts[s])
                else:
                    amt = 16 if rec["dma"] else 1
                    self.sem("bar_" + e)
                    self.counts["bar_" + e] += amt
                    rec["incs"].append(("bar_" + e, amt))
                    conds[e] = ("bar_" + e, self.counts["bar_" + e])
                self._since_bar[e] = False
        self._bar_conds = conds
        return list(conds.values())

    def wait(self, engine, waits):
        if waits:
            self.op(engine, lambda eng: None, waits=waits, real=False)

    def switch_segment(self, fn):
        if not hasattr(self, "segments"):
            self.segments = []
        self.segments.append(("block", self.ops))
        self.ops = {e: [] for e in ENGINES}
        self.segments.append(("switch", fn))

    def build(self):
        nc = self.nc
        sems = self.sems
        segs = getattr(self, "segments", [])
        segs = segs + [("block", self.ops)]

        def runner(ops, engine):
            def run(eng):
                for rec in ops[engine]:
                    for w, v in rec["waits"]:
                        eng.wait_ge(sems[w], v)
                    ins = rec["fn"](eng)
                    if rec["incs"]:
                        assert ins is not None, f"{engine} op with inc returned None"
                        for s, a in rec["incs"]:
                            ins.then_inc(sems[s], a)
            return run

        for kind, payload in segs:
            if kind == "block":
                if not any(payload[e] for e in ENGINES):
                    continue
                with nc.Block() as blk:
                    blk.sync(runner(payload, "sync"))
                    blk.vector(runner(payload, "vector"))
                    blk.scalar(runner(payload, "scalar"))
                    blk.tensor(runner(payload, "tensor"))
                    blk.gpsimd(runner(payload, "gpsimd"))
            else:
                for arm in nc.switch_core_id(engines=[nc.sync], n=128):
                    payload(nc, arm.logical % 8, sems)
        self.es.close()



def host_consts(const_vals=()):
    c = np.zeros((128, NCONST), dtype=np.float32)
    for j, v in enumerate(const_vals):
        c[:, j] = v
    sel0 = np.zeros((128, 128), dtype=np.float32)
    sel0[0, :] = 1.0
    sel1 = np.zeros((128, 128), dtype=np.float32)
    sel1[1, :] = 1.0
    onesblk = np.zeros((128, 128), dtype=np.float32)
    onesblk[:64, :64] = 1.0
    onesblk[64:, 64:] = 1.0
    return {
        "c_eye": np.eye(128, dtype=np.float32),
        "c_onesblk": onesblk,
        "c_ones128": np.ones((128, 128), dtype=np.float32),
        "c_sel0": sel0,
        "c_sel1": sel1,
        "c_scal": c,
    }


def declare_inputs(nc, arrs):
    out = {}
    for k, v in arrs.items():
        dt = I32 if v.dtype == np.int32 else F32
        out[k] = nc.dram_tensor(k, list(v.shape), dt, kind="ExternalInput").ap()
    return out


NCONST = 32


class Sbuf:

    def __init__(self, p):
        self.p = p
        self.t = {}
        self.const_vals = []
        self._scopes = []

    def push(self):
        self._scopes.append((ExitStack(), []))

    def alias(self, name, ap):
        self.t[name] = ap
        if self._scopes:
            self._scopes[-1][1].append(name)

    def once(self, key):
        if key in self.t:
            return False
        self.alias(key, True)
        return True

    def pop(self):
        es, names = self._scopes.pop()
        for n in names:
            del self.t[n]
        es.close()

    def const(self, val, parts=128):
        val = float(val)
        tile = self.sb("k_scal", (128, NCONST))
        if val not in self.const_vals:
            assert len(self.const_vals) < NCONST
            self.const_vals.append(val)
        j = self.const_vals.index(val)
        return tile[0:parts, j:j + 1]

    def _alloc(self, name, cm):
        if self._scopes:
            es, names = self._scopes[-1]
            self.t[name] = es.enter_context(cm)
            names.append(name)
        else:
            self.t[name] = self.p.es.enter_context(cm)
        return self.t[name]

    def _uname(self, name):
        self._uctr = getattr(self, "_uctr", 0) + 1
        return name if not self._scopes else f"u{self._uctr}_{name}"

    def sb(self, name, shape, dtype=F32):
        if name not in self.t:
            self._alloc(name, self.p.nc.sbuf_tensor(self._uname(name), list(shape), dtype))
        return self.t[name]

    def ps(self, name, shape, dtype=F32):
        if name not in self.t:
            self._alloc(name, self.p.nc.psum_tensor(self._uname(name), list(shape), dtype))
        return self.t[name]



def emit_ln(p, m, sx, xln, Lc, tag, dwaits=(), gamma=None, beta=None, eps2=None,
            vsem="v", ssem="s", tsem="t"):
    psA = m.ps("psA", (128, Lc))
    psB = m.ps("psB", (128, Lc))
    psR = m.ps("psR", (128, Lc))
    xbf = [m.sb(f"ln_xbf{i}", (128, Lc), BF16) for i in range(ND)]
    xsq = [m.sb(f"ln_xsq{i}", (128, Lc), BF16) for i in range(ND)]
    rowr = m.sb("ln_rowr", (128, Lc), BF16)
    e2row = m.sb("ln_e2row", (1, Lc))
    m2row = m.sb("ln_m2row", (1, Lc))
    varrow = m.sb("ln_varrow", (1, Lc))
    tln = xbf
    ones128 = m.sb("k_ones128_bf", (128, 128), BF16)
    sel0 = m.sb("k_sel0_bf", (128, 128), BF16)

    if m.once("ln_rows_zeroed"):
        p.op("vector", lambda e: e.memset(rowr[:, :], 0.0), inc=vsem)

    for i in range(ND):
        p.op("vector", lambda e, i=i: e.tensor_copy(out=xbf[i][:, :], in_=sx[i]),
             waits=dwaits if i == 0 else (), inc=vsem)
    for i in range(ND):
        p.op("scalar", lambda e, i=i: e.activation(out=xsq[i][:, :], in_=sx[i],
                                                   func=AF.Square),
             waits=dwaits if i == 0 else (), inc=ssem)
    vcopy, scopy = p.val(vsem), p.val(ssem)
    for i in range(ND):
        p.op("tensor", lambda e, i=i: e.matmul(out=psA[:, :], lhsT=ones128[:, :],
                                               rhs=xbf[i][:, :], start=(i == 0),
                                               stop=(i == ND - 1)),
             waits=[(vsem, vcopy)] if i == 0 else ())
    for i in range(ND):
        p.op("tensor", lambda e, i=i: e.matmul(out=psB[:, :], lhsT=ones128[:, :],
                                               rhs=xsq[i][:, :], start=(i == 0),
                                               stop=(i == ND - 1)),
             waits=[(ssem, scopy)] if i == 0 else (),
             inc=tsem if i == ND - 1 else None)
    t_stats = p.val(tsem)
    p.op("scalar", lambda e: e.activation(out=e2row[:, :], in_=psB[0:1, :],
                                          func=AF.Copy, scale=1.0 / D),
         waits=[(tsem, t_stats)], inc=ssem)
    p.op("scalar", lambda e: e.activation(out=m2row[:, :], in_=psA[0:1, :],
                                          func=AF.Square, scale=1.0 / D), inc=ssem)
    s_rows = p.val(ssem)
    p.op("vector", lambda e: e.tensor_sub(out=varrow[:, :], in0=e2row[:, :],
                                          in1=m2row[:, :]),
         waits=[(ssem, s_rows)], inc=vsem)
    if eps2 is not None:
        p.op("vector", lambda e: e.scalar_tensor_tensor(
            out=varrow[:, :], in0=varrow[:, :], scalar=float(eps2),
            in1=varrow[:, :], op0=ALU.mult, op1=ALU.add),
            waits=[(vsem, p.val(vsem))], inc=vsem)
        eps_b = float(LN_EPS * eps2)
    else:
        eps_b = float(LN_EPS)
    v_var = p.val(vsem)
    eps_ap = m.const(eps_b, parts=1)
    p.op("scalar", lambda e: e.activation(out=varrow[:, :], in_=varrow[:, :],
                                          func=AF.Ln, bias=eps_ap),
         waits=[(vsem, v_var)], inc=ssem)
    p.op("scalar", lambda e: e.activation(out=varrow[:, :], in_=varrow[:, :],
                                          func=AF.Exp, scale=-0.5), inc=ssem)
    s_rstd = p.val(ssem)
    p.op("vector", lambda e: e.tensor_scalar(out=rowr[0:1, :], in0=varrow[:, :],
                                             scalar1=-1.0, scalar2=None,
                                             op0=ALU.mult),
         waits=[(ssem, s_rstd)], inc=vsem)
    v_neg = p.val(vsem)
    p.op("tensor", lambda e: e.matmul(out=psR[:, :], lhsT=sel0[:, :],
                                      rhs=rowr[:, :], start=True, stop=True),
         waits=[(vsem, v_neg)], inc=tsem)
    t_rep = p.val(tsem)
    for i in range(ND):
        p.op("vector", lambda e, i=i: e.scalar_tensor_tensor(
            out=tln[i][:, :], in0=psA[:, :], scalar=1.0 / D, in1=sx[i],
            op0=ALU.mult, op1=ALU.subtract),
             waits=[(tsem, t_rep)] if i == 0 else (), inc=vsem)
        if gamma is None:
            p.op("vector", lambda e, i=i: e.tensor_mul(out=xln[i][:, :],
                                                       in0=tln[i][:, :], in1=psR[:, :]),
                 waits=[(vsem, p.val(vsem))], inc=vsem)
        else:
            p.op("vector", lambda e, i=i: e.tensor_mul(out=tln[i][:, :],
                                                       in0=tln[i][:, :], in1=psR[:, :]),
                 waits=[(vsem, p.val(vsem))], inc=vsem)
    if gamma is not None:
        for i in range(ND):
            p.op("vector", lambda e, i=i: e.scalar_tensor_tensor(
                out=xln[i][:, :], in0=tln[i][:, :], scalar=gamma[i],
                in1=beta[i].broadcast_to([128, Lc]), op0=ALU.mult, op1=ALU.add),
                inc=vsem)
    return (("vector", vsem, p.val(vsem)), ("scalar", ssem, p.val(ssem)))



def emit_ffn1_prep(p, m):
    nc = p.nc
    dram = m.dram
    p._wd_bf = nc.dram_tensor("f1_wd_bf", [24, 128, 1024], BF16,
                              kind="Internal").ap()
    p._wu_bf = nc.dram_tensor("f1_wu_bf", [8, 128, 3072], BF16,
                              kind="Internal").ap()
    stg = [m.sb(f"f1p_stage{j}", (128, 1024)) for j in range(2)]
    sdone = [0, 0]
    jobs = ([("wdT", k, 0, p._wd_bf[k]) for k in range(24)] +
            [("wuT", k, j, p._wu_bf[k][:, j * 1024:(j + 1) * 1024])
             for k in range(ND) for j in range(3)])
    for i, (nm, k, j, dst) in enumerate(jobs):
        par = i % 2
        dgot = p.op("gpsimd", lambda e, nm=nm, k=k, j=j, par=par: e.dma_start(
            out=stg[par][:, :],
            in_=dram[nm][k * 128:(k + 1) * 128, j * 1024:(j + 1) * 1024]),
            waits=[("f1ps", sdone[par])] if sdone[par] else (),
            inc="f1pd", inc_amt=16)
        p.op("gpsimd", lambda e, dst=dst, par=par: e.dma_start(
            out=dst, in_=stg[par][:, :]),
            waits=[("f1pd", dgot)], inc="f1ps", inc_amt=16)
        sdone[par] = p.val("f1ps")
    p._wd_ready = ("f1ps", p.val("f1ps"))


def emit_ffn1(p, m, xin_ap, xout_ap, L, Lc, prefix="f1", dma_eng="sync", entry_waits=()):
    nc = p.nc
    HT = 3072 // 128
    dram = m.dram
    wg = m.sb(prefix + "wg", (128, 8 * 3072), BF16)
    wuW = [m.sb(f"{prefix}wuW{j}", (128, ND * 128), BF16) for j in range(2)]
    wdW = [m.sb(f"{prefix}wdW{j}", (128, HT * 128), BF16) for j in range(2)]
    hh = m.sb(prefix + "hh", (128, HT * Lc), BF16)
    stage = hh[:, :].bitcast(F32)[:, 0:3072]
    wdtmp = hh[:, 6144:7168]
    sx = [[m.sb(f"{prefix}sx{par}_{i}", (128, Lc)) for i in range(ND)]
          for par in range(2)]
    xln = [[m.sb(f"{prefix}xln{par}_{i}", (128, Lc), BF16) for i in range(ND)]
           for par in range(2)]
    sg = [m.sb(f"{prefix}sg{j}", (128, Lc), BF16) for j in range(2)]
    xo = [m.sb(f"{prefix}xo{i}", (128, Lc)) for i in range(ND)]
    psG = [m.ps(f"{prefix}psG{j}", (128, Lc)) for j in range(2)]
    psU = [m.ps(f"{prefix}psU{j}", (128, Lc)) for j in range(2)]

    ds, vs, ss, ts = prefix + "d", prefix + "v", prefix + "s", prefix + "t"
    dsg = prefix + "dg"

    for name, dst in (("wgT", wg),):
        for k in range(ND):
            dgot = p.op("gpsimd", lambda e, name=name, k=k: e.dma_start(
                out=stage[:, :], in_=dram[name][k * 128:(k + 1) * 128, :]),
                waits=[(vs, p.val(vs))], inc=dsg, inc_amt=16)
            p.op("vector", lambda e, dst=dst, k=k: e.tensor_copy(
                out=dst[:, k * 3072:(k + 1) * 3072], in_=stage[:, :]),
                waits=[(dsg, dgot)], inc=vs)
    if not hasattr(p, "_wd_bf"):
        emit_ffn1_prep(p, m)
    wd_bf = p._wd_bf
    wu_bf = p._wu_bf
    wd_ready = p._wd_ready
    p.barrier()

    nchunks = L // Lc
    ln_done = {}
    resid_done = [0, 0]

    def stage_ln(c):
        cp = c % 2
        sl = slice(c * Lc, (c + 1) * Lc)
        dx = prefix + ("dxa" if cp == 0 else "dxb")
        dgot = None
        for i in range(ND):
            w0 = (([(vs, resid_done[cp])] if resid_done[cp] else []) +
                  (list(entry_waits) if c == 0 else [])) if i == 0 else ()
            dgot = p.op(dma_eng, lambda e, i=i, sl=sl, cp=cp: e.dma_start(
                out=sx[cp][i][:, :], in_=xin_ap(i, sl)),
                waits=w0, inc=dx, inc_amt=16)
        emit_ln(p, m, [sx[cp][i][:, :] for i in range(ND)], xln[cp], Lc, tag=prefix,
                dwaits=[(dx, dgot)], vsem=vs, ssem=ss, tsem=ts)
        ln_done[c] = (p.val(vs), p.val(ss))

    stage_ln(0)
    for c in range(nchunks):
        cp = c % 2
        sl = slice(c * Lc, (c + 1) * Lc)
        v_ln, s_ln = ln_done[c]
        guard = [None, None]
        wul = [0, 0]
        for h in range(HT):
            par = h % 2
            du_ = prefix + ("dua" if par == 0 else "dub")
            dwu = p.op("gpsimd", lambda e, h=h, par=par: e.dma_start(
                out=wuW[par][:, :].rearrange("p (k j) -> p k j", j=128),
                in_=wu_bf[:, :, h * 128:(h + 1) * 128].rearrange("k p j -> p k j")),
                waits=([(ts, wul[par])] if wul[par] else [wd_ready]),
                inc=du_, inc_amt=16)
            w0 = [(vs, v_ln), (ss, s_ln)] if h == 0 else (
                [(vs, guard[par])] if guard[par] else ())
            for k in range(ND):
                p.op("tensor", lambda e, h=h, k=k, par=par, cp=cp: e.matmul(
                    out=psG[par][:, :],
                    lhsT=wg[:, k * 3072 + h * 128:k * 3072 + (h + 1) * 128],
                    rhs=xln[cp][k][:, :], start=(k == 0), stop=(k == ND - 1)),
                    waits=w0 if k == 0 else ())
            for k in range(ND):
                p.op("tensor", lambda e, h=h, k=k, par=par, cp=cp: e.matmul(
                    out=psU[par][:, :],
                    lhsT=wuW[par][:, k * 128:(k + 1) * 128],
                    rhs=xln[cp][k][:, :], start=(k == 0), stop=(k == ND - 1)),
                    waits=[(du_, dwu)] if k == 0 else (),
                    inc=ts if k == ND - 1 else None)
            t_gu = p.val(ts)
            wul[par] = t_gu
            p.op("scalar", lambda e, h=h, par=par: e.activation(
                out=sg[par][:, :], in_=psG[par][:, :], func=AF.Silu,
                scale=1.0 / 127.0), waits=[(ts, t_gu)], inc=ss)
            s_silu = p.val(ss)
            p.op("vector", lambda e, h=h, par=par: e.scalar_tensor_tensor(
                out=hh[:, h * Lc:(h + 1) * Lc], in0=psU[par][:, :],
                scalar=1.0 / 127.0, in1=sg[par][:, :], op0=ALU.mult, op1=ALU.mult),
                waits=[(ss, s_silu)], inc=vs)
            guard[par] = p.val(vs)
            if h == 1 and c + 1 < nchunks:
                stage_ln(c + 1)
        v_hh_all = p.val(vs)
        psD = [m.t["psA"], m.t["psB"]]
        dguard = [None, None]
        dwl = [0, 0]
        for o in range(ND):
            par = o % 2
            dd = prefix + ("dda" if par == 0 else "ddb")
            dw = p.op("gpsimd", lambda e, o=o, par=par: e.dma_start(
                out=wdW[par][:, :].rearrange("p (k j) -> p k j", j=128),
                in_=wd_bf[:, :, o * 128:(o + 1) * 128].rearrange("k p j -> p k j")),
                waits=([(ts, dwl[par])] if dwl[par] else
                       ([wd_ready] if o < 2 else [])), inc=dd, inc_amt=16)
            w0 = [(dd, dw), (vs, v_hh_all)] + (
                [(vs, dguard[par])] if dguard[par] else [])
            for k in range(HT):
                p.op("tensor", lambda e, o=o, k=k, par=par: e.matmul(
                    out=psD[par][:, :],
                    lhsT=wdW[par][:, k * 128:(k + 1) * 128],
                    rhs=hh[:, k * Lc:(k + 1) * Lc], start=(k == 0),
                    stop=(k == HT - 1)),
                    waits=w0 if k == 0 else (), inc=ts if k == HT - 1 else None)
            t_d = p.val(ts)
            dwl[par] = t_d
            p.op("vector", lambda e, o=o, par=par, cp=cp: e.scalar_tensor_tensor(
                out=xo[o][:, :], in0=psD[par][:, :], scalar=0.5 / 127.0,
                in1=sx[cp][o][:, :], op0=ALU.mult, op1=ALU.add),
                waits=[(ts, t_d)], inc=vs)
            dguard[par] = p.val(vs)
            v_o = p.val(vs)
            p.op(dma_eng, lambda e, o=o, sl=sl: e.dma_start(
                out=xout_ap(o, sl), in_=xo[o][:, :]),
                waits=[(vs, v_o)], inc=ds, inc_amt=16)
        resid_done[cp] = p.val(vs)
    p.wait(dma_eng, [(ds, p.val(ds))])


def emit_const_loads(p, m):
    d = m.dram
    oc = m.sb("k_ones128_bf", (128, 128), BF16)
    sel0 = m.sb("k_sel0_bf", (128, 128), BF16)
    sel1 = m.sb("k_sel1_bf", (128, 128), BF16)
    eye = m.sb("k_eye_bf", (128, 128), BF16)
    onesblk = m.sb("k_onesblk_bf", (128, 128), BF16)
    scal = m.sb("k_scal", (128, NCONST))
    p.op("gpsimd", lambda e: e.dma_start(out=oc[:, :], in_=d["c_ones128"]),
         inc="cd_g", inc_amt=16)
    p.op("gpsimd", lambda e: e.dma_start(out=sel0[:, :], in_=d["c_sel0"]),
         inc="cd_g", inc_amt=16)
    p.op("gpsimd", lambda e: e.dma_start(out=sel1[:, :], in_=d["c_sel1"]),
         inc="cd_g", inc_amt=16)
    p.op("gpsimd", lambda e: e.dma_start(out=eye[:, :], in_=d["c_eye"]),
         inc="cd_g", inc_amt=16)
    p.op("gpsimd", lambda e: e.dma_start(out=onesblk[:, :], in_=d["c_onesblk"]),
         inc="cd_g", inc_amt=16)
    p.op("sync", lambda e: e.dma_start(out=scal[:, :], in_=d["c_scal"]),
         inc="cd_s", inc_amt=16)
    w = [("cd_g", p.val("cd_g")), ("cd_s", p.val("cd_s"))]
    p.wait("tensor", w)
    p.wait("vector", w)
    p.wait("scalar", w)


def build_ffn1_launch(in_shapes, L=4096, Lc=512):
    nc = bass.Bass("TRN2", target_bir_lowering=False)
    d = declare_inputs(nc, in_shapes)
    xout = nc.dram_tensor("xout", [ND, 128, L], F32, kind="ExternalOutput").ap()
    p = Plan(nc)
    m = Sbuf(p)
    m.dram = d
    emit_const_loads(p, m)
    emit_ffn1(p, m, lambda i, sl: d["xin"][i, :, sl],
              lambda o, sl: xout[o, :, sl], L, Lc)
    p.build()
    return nc, m


def build_ln_launch(in_shapes, L=4096, Lc=256):
    nc = bass.Bass("TRN2", target_bir_lowering=False)
    d = declare_inputs(nc, in_shapes)
    xout = nc.dram_tensor("xout", [ND, 128, L], BF16, kind="ExternalOutput").ap()
    p = Plan(nc)
    m = Sbuf(p)
    m.dram = d
    emit_const_loads(p, m)
    sx = [m.sb(f"dsx{i}", (128, Lc)) for i in range(ND)]
    xln = [m.sb(f"dxln{i}", (128, Lc), BF16) for i in range(ND)]
    prev_bar = ()
    for c in range(L // Lc):
        sl = slice(c * Lc, (c + 1) * Lc)
        dgot = None
        for i in range(ND):
            dgot = p.op("sync", lambda e, i=i, sl=sl: e.dma_start(
                out=sx[i][:, :], in_=d["xin"][i, :, sl]),
                waits=prev_bar if i == 0 else (), inc="d", inc_amt=16)
        emit_ln(p, m, [sx[i][:, :] for i in range(ND)], xln, Lc, tag="ln",
                dwaits=[("d", dgot)], vsem="v", ssem="s", tsem="t")
        v_ln = p.val("v")
        for i in range(ND):
            p.op("sync", lambda e, i=i, sl=sl: e.dma_start(
                out=xout[i, :, sl], in_=xln[i][:, :]),
                waits=[("v", v_ln)] if i == 0 else (), inc="d", inc_amt=16)
        prev_bar = p.barrier()
    p.wait("sync", [("d", p.val("d"))])
    p.build()
    return nc, m



def emit_ffn0(p, m, xin_ap, xout_ap, Ll, prefix="f0", entry_waits=()):
    nc = p.nc
    d = m.dram
    T = 8
    WA = m.sb(prefix + "WA", (128, 6 * ND * 1024), BF16)
    woff = {"wfT": 0, "wiT": 1, "wcT": 2, "ufT": 3, "uiT": 4, "ucT": 5}

    def wtile(name, k, o):
        base = woff[name] * ND * 1024 + k * 1024
        return WA[:, base + o * 128: base + (o + 1) * 128]

    bias = {nm: m.sb(prefix + nm, (128, ND)) for nm in ("bfT", "biT", "bcT", "boT", "incT")}
    psF = [m.ps(f"{prefix}psF{j}", (128, Ll)) for j in range(2)]
    psI = [m.ps(f"{prefix}psI{j}", (128, Ll)) for j in range(2)]
    psC = [m.ps(f"{prefix}psC{j}", (128, Ll)) for j in range(2)]
    psO = m.ps(prefix + "psO", (128, Ll))
    m.alias("psA", psF[0])
    m.alias("psB", psI[0])
    m.alias("psR", psC[0])
    sx = [m.sb(f"{prefix}sx{i}", (128, Ll)) for i in range(ND)]
    xln = [m.sb(f"{prefix}xln{i}", (128, Ll), BF16) for i in range(ND)]
    cst = [m.sb(f"{prefix}c{i}", (128, Ll)) for i in range(ND)]
    cbf = [[m.sb(f"{prefix}cb{par}_{i}", (128, Ll), BF16) for i in range(ND)]
           for par in range(2)]
    gF = [m.sb(f"{prefix}gF{j}", (128, Ll), BF16) for j in range(2)]
    gI = [m.sb(f"{prefix}gI{j}", (128, Ll), BF16) for j in range(2)]
    gC = [m.sb(f"{prefix}gC{j}", (128, Ll), BF16) for j in range(2)]
    t1b = m.sb(prefix + "t1", (128, Ll), BF16)
    t2f = m.sb(prefix + "t2", (128, Ll))
    xstash = nc.dram_tensor(f"{prefix}_xstash{sum(len(v) for v in p.ops.values())}",
                            [T, ND, 128, Ll], BF16, kind="Internal").ap()

    ds, dsg, vs, ss, ts = (prefix + "d", prefix + "dg", prefix + "v",
                           prefix + "s", prefix + "t")

    for nm in woff:
        for k in range(ND):
            p.op("gpsimd", lambda e, nm=nm, k=k: e.dma_start(
                out=WA[:, woff[nm] * ND * 1024 + k * 1024:
                       woff[nm] * ND * 1024 + (k + 1) * 1024],
                in_=d[nm][k * 128:(k + 1) * 128, :]), inc=dsg, inc_amt=16)
    for nm in bias:
        p.op("sync", lambda e, nm=nm: e.dma_start(out=bias[nm][:, :], in_=d[nm]),
             inc=prefix + "db", inc_amt=16)
    w_weights = [(dsg, p.val(dsg)), (prefix + "db", p.val(prefix + "db"))]

    for k in range(ND):
        p.op("vector", lambda e, k=k: e.tensor_scalar(
            out=cst[k][:, :], in0=bias["incT"][:, k:k + 1].broadcast_to([128, Ll]),
            scalar1=1.0, scalar2=None, op0=ALU.mult),
            waits=w_weights if k == 0 else (), inc=vs)
        p.op("vector", lambda e, k=k: e.tensor_copy(out=cbf[0][k][:, :], in_=cst[k][:, :]),
             inc=vs)

    prev_bar = tuple(entry_waits)
    for t in range(T):
        dx = prefix + ("dxa" if t % 2 == 0 else "dxb")
        dgot = None
        for i in range(ND):
            dgot = p.op("sync", lambda e, t=t, i=i: e.dma_start(
                out=sx[i][:, :], in_=xin_ap(t, i)),
                waits=prev_bar if i == 0 else (), inc=dx, inc_amt=16)
        emit_ln(p, m, [sx[i][:, :] for i in range(ND)], xln, Ll, tag=prefix,
                dwaits=[(dx, dgot)], vsem=vs, ssem=ss, tsem=ts)
        v_ln = p.val(vs)
        for i in range(ND):
            p.op("sync", lambda e, t=t, i=i: e.dma_start(
                out=xstash[t, i, :, :], in_=xln[i][:, :]),
                waits=[(vs, v_ln)] if i == 0 else (), inc=prefix + "dst", inc_amt=16)
        prev_bar = p.barrier()

    xt = [m.sb(f"{prefix}xt{i}", (128, Ll), BF16) for i in range(ND)]
    stash_done = [(prefix + "dst", p.val(prefix + "dst"))]
    for t in range(T):
        dx = prefix + ("dsa" if t % 2 == 0 else "dsb")
        dgot = None
        for i in range(ND):
            dgot = p.op("sync", lambda e, t=t, i=i: e.dma_start(
                out=xt[i][:, :], in_=xstash[t, i, :, :]),
                waits=(list(prev_bar) + stash_done) if i == 0 else (),
                inc=dx, inc_amt=16)
        v_prev_c = p.val(vs)
        guard = [None, None]
        for k in range(ND):
            par = k % 2
            if guard[par] is not None:
                p.op("tensor", lambda e: None, waits=[(vs, guard[par])], real=False)
            for ps, wname, uname in ((psF[par], "wfT", "ufT"), (psI[par], "wiT", "uiT"),
                                     (psC[par], "wcT", "ucT")):
                for kk in range(ND):
                    p.op("tensor", lambda e, ps=ps, wname=wname, k=k, kk=kk: e.matmul(
                        out=ps[:, :], lhsT=wtile(wname, kk, k), rhs=xt[kk][:, :],
                        start=(kk == 0), stop=False),
                        waits=[(dx, dgot), (vs, v_prev_c)] if (k == 0 and kk == 0 and ps is psF[par]) else ())
                for kk in range(ND):
                    p.op("tensor", lambda e, ps=ps, uname=uname, k=k, kk=kk, t=t: e.matmul(
                        out=ps[:, :], lhsT=wtile(uname, kk, k), rhs=cbf[t % 2][kk][:, :],
                        start=False, stop=(kk == ND - 1)),
                        inc=ts if (kk == ND - 1 and ps is psC[par]) else None)
            t_g = p.val(ts)
            p.op("scalar", lambda e, k=k, par=par: e.activation(
                out=gF[par][:, :], in_=psF[par][:, :], func=AF.Sigmoid,
                bias=bias["bfT"][:, k:k + 1]), waits=[(ts, t_g)], inc=ss)
            p.op("scalar", lambda e, k=k, par=par: e.activation(
                out=gI[par][:, :], in_=psI[par][:, :], func=AF.Sigmoid,
                bias=bias["biT"][:, k:k + 1]), inc=ss)
            p.op("scalar", lambda e, k=k, par=par: e.activation(
                out=gC[par][:, :], in_=psC[par][:, :], func=AF.Tanh,
                bias=bias["bcT"][:, k:k + 1]), inc=ss)
            s_g = p.val(ss)
            p.op("vector", lambda e, k=k, par=par: e.tensor_mul(
                out=t1b[:, :], in0=gI[par][:, :], in1=gC[par][:, :]),
                waits=[(ss, s_g)], inc=vs)
            p.op("vector", lambda e, k=k, par=par: e.tensor_mul(
                out=t2f[:, :], in0=gF[par][:, :], in1=cst[k][:, :]), inc=vs)
            p.op("vector", lambda e, k=k: e.tensor_add(out=cst[k][:, :], in0=t2f[:, :],
                                                       in1=t1b[:, :]), inc=vs)
            p.op("vector", lambda e, k=k, t=t: e.tensor_copy(
                out=cbf[(t + 1) % 2][k][:, :], in_=cst[k][:, :]), inc=vs)
            guard[par] = p.val(vs)
        prev_bar = p.barrier()

    steps_done = p.barrier()
    for nm, off in (("woT", 0), ("uoT", 1)):
        for k in range(ND):
            p.op("gpsimd", lambda e, nm=nm, off=off, k=k: e.dma_start(
                out=WA[:, off * ND * 1024 + k * 1024:off * ND * 1024 + (k + 1) * 1024],
                in_=d[nm][k * 128:(k + 1) * 128, :]),
                waits=steps_done if (nm == "woT" and k == 0) else (),
                inc=dsg, inc_amt=16)
    w_wo = [(dsg, p.val(dsg))]

    def wotile(off, k, o):
        base = off * ND * 1024 + k * 1024
        return WA[:, base + o * 128: base + (o + 1) * 128]

    xo = cst
    for t in range(T):
        dx = prefix + ("doa" if t % 2 == 0 else "dob")
        dgx = None
        for i in range(ND):
            p.op("sync", lambda e, t=t, i=i: e.dma_start(
                out=xt[i][:, :], in_=xstash[t, i, :, :]),
                waits=prev_bar if i == 0 else (), inc=dx, inc_amt=16)
            dgx = p.op("sync", lambda e, t=t, i=i: e.dma_start(
                out=sx[i][:, :], in_=xin_ap(t, i)), inc=dx, inc_amt=16)
        for k in range(ND):
            for kk in range(ND):
                p.op("tensor", lambda e, k=k, kk=kk: e.matmul(
                    out=psO[:, :], lhsT=wotile(0, kk, k), rhs=xt[kk][:, :],
                    start=(kk == 0), stop=False),
                    waits=([(dx, dgx)] + (w_wo if t == 0 else [])) if kk == 0 else ())
            for kk in range(ND):
                p.op("tensor", lambda e, k=k, kk=kk: e.matmul(
                    out=psO[:, :], lhsT=wotile(1, kk, k), rhs=cbf[0][kk][:, :],
                    start=False, stop=(kk == ND - 1)),
                    inc=ts if kk == ND - 1 else None)
            t_o = p.val(ts)
            p.op("scalar", lambda e, k=k: e.activation(
                out=t2f[:, :], in_=psO[:, :], func=AF.Tanh, bias=bias["boT"][:, k:k + 1]),
                waits=[(ts, t_o)], inc=ss)
            s_t = p.val(ss)
            p.op("vector", lambda e, k=k: e.scalar_tensor_tensor(
                out=xo[k][:, :], in0=t2f[:, :], scalar=0.5, in1=sx[k][:, :],
                op0=ALU.mult, op1=ALU.add), waits=[(ss, s_t)], inc=vs)
            v_x = p.val(vs)
            p.op("sync", lambda e, t=t, k=k: e.dma_start(
                out=xout_ap(t, k), in_=xo[k][:, :]),
                waits=[(vs, v_x)], inc=prefix + "do", inc_amt=16)
            p.op("tensor", lambda e: None, waits=[(vs, v_x)], real=False)
        prev_bar = p.barrier()
    p.wait("sync", [(prefix + "do", p.val(prefix + "do"))])


def build_ffn0_launch(in_shapes, Ll=512):
    nc = bass.Bass("TRN2", target_bir_lowering=False)
    d = declare_inputs(nc, in_shapes)
    xout = nc.dram_tensor("xout", [8, ND, 128, Ll], F32, kind="ExternalOutput").ap()
    p = Plan(nc)
    m = Sbuf(p)
    m.dram = d
    emit_const_loads(p, m)
    emit_ffn0(p, m, lambda t, i: d["xin"][t, i, :, :],
              lambda t, k: xout[t, k, :, :], Ll)
    p.build()
    return nc, m



def emit_att1(p, m, xin_ap, xout_ap, L, Lc, prefix="mb", entry_waits=()):
    nc = p.nc
    d = m.dram
    NDT = 16
    NE = 32
    NN = 16
    ds, dsg, vs, ss, ts = (prefix + "d", prefix + "dg", prefix + "v",
                           prefix + "s", prefix + "t")

    ipW = [m.sb(f"{prefix}ipW{par}", (128, ND * 128), BF16) for par in range(2)]
    opW = [m.sb(f"{prefix}opW{par}", (128, NDT * 128), BF16) for par in range(2)]
    xpW = m.sb(prefix + "xpW", (128, NDT * 128), BF16)
    dtW = m.sb(prefix + "dtW", (128, NDT * 128), BF16)
    Alog = m.sb(prefix + "Alog", (128, NDT * NN))
    Asb = m.sb(prefix + "A", (128, NDT * NN))
    convW = m.sb(prefix + "convW", (128, NDT * 4))
    convB = m.sb(prefix + "convB", (128, NDT))
    Dd = m.sb(prefix + "Dd", (128, NDT))
    dtb = m.sb(prefix + "dtb", (128, NDT))
    gamT = m.sb(prefix + "gamT", (128, ND))
    betT = m.sb(prefix + "betT", (128, ND))

    sx = [m.sb(f"{prefix}sx{i}", (128, Lc)) for i in range(ND)]
    scrA = m.sb(prefix + "scrA", (128, Lc * ND))
    xwf = [scrA[:, i * Lc:(i + 1) * Lc] for i in range(ND)]
    states = scrA[:, :].bitcast(BF16)
    xz = [m.sb(f"{prefix}xz{i}", (128, Lc), BF16) for i in range(ND)]
    ubig = m.sb(prefix + "ubig", (128, NDT * (Lc + 4)), BF16)
    uti = [ubig[:, t * (Lc + 4): t * (Lc + 4) + Lc + 4] for t in range(NDT)]
    xob = ubig[:, :].bitcast(F32)
    xo = [xob[:, i * ((Lc + 4) // 2 * 2): i * ((Lc + 4) // 2 * 2) + Lc] for i in range(ND)]
    utail = m.sb(prefix + "utail", (128, NDT * 4), BF16)
    uc = m.sb(prefix + "uc", (128, NDT * Lc), BF16)
    delta = m.sb(prefix + "delta", (128, NDT * Lc), BF16)
    du = m.sb(prefix + "du", (128, Lc), BF16)
    bcrows = m.sb(prefix + "bcrows", (128, NN * Lc), BF16)
    Brep = m.sb(prefix + "Brep", (128, NN * Lc), BF16)
    Crep = m.sb(prefix + "Crep", (128, NN * Lc), BF16)
    yg = m.sb(prefix + "yg", (128, NDT * Lc), BF16)
    xdbl = m.sb(prefix + "xdbl", (128, Lc))
    xdbf = m.sb(prefix + "xdbf", (128, Lc), BF16)
    dAb = m.sb(prefix + "dAb", (128, Lc), BF16)
    dAb2 = m.sb(prefix + "dAb2", (128, Lc), BF16)
    dBub = m.sb(prefix + "dBub", (128, Lc), BF16)
    yf = m.sb(prefix + "yf", (128, Lc))
    zt = m.sb(prefix + "zt", (128, Lc), BF16)
    carry = m.sb(prefix + "carry", (128, NDT * NN))
    tmp16 = m.sb(prefix + "tmp16", (128, 16))
    sel0 = m.sb("k_sel0_bf", (128, 128), BF16)
    sel1 = m.sb("k_sel1_bf", (128, 128), BF16)

    for i in range(ND):
        m.alias(f"ln_xsq{i}", uc[:, i * Lc:(i + 1) * Lc])
    psW0 = m.ps(prefix + "psW0", (128, Lc))
    psW1 = m.ps(prefix + "psW1", (128, Lc))
    psO = m.ps(prefix + "psO", (128, Lc))

    _uq = sum(len(v) for v in p.ops.values())
    zstash = nc.dram_tensor(f"{prefix}_zst{_uq}", [NDT, 128, Lc], BF16, kind="Internal").ap()
    xwstash = nc.dram_tensor(f"{prefix}_xwst{_uq}", [ND, 128, Lc], BF16, kind="Internal").ap()
    bcstash = nc.dram_tensor(f"{prefix}_bcst{_uq}", [2, NN, Lc], F32, kind="Internal").ap()

    p.op("sync", lambda e: e.dma_start(out=Alog[:, :], in_=d["A_logT"]),
         inc=prefix + "db", inc_amt=16)
    for nm, dst in (("convWT", convW), ("convBT", convB), ("DT", Dd),
                    ("dtbT", dtb), ("gammaT", gamT), ("betaT", betT)):
        p.op("sync", lambda e, nm=nm, dst=dst: e.dma_start(out=dst[:, :], in_=d[nm]),
             inc=prefix + "db", inc_amt=16)
    for k in range(NDT):
        p.op("gpsimd", lambda e, k=k: e.dma_start(
            out=xpW[:, k * 128:(k + 1) * 128],
            in_=d["xpT"][k * 128:(k + 1) * 128, :]), inc=dsg, inc_amt=16)
        p.op("gpsimd", lambda e, k=k: e.dma_start(
            out=dtW[:, k * 128:(k + 1) * 128],
            in_=d["dtT"][:, k * 128:(k + 1) * 128]), inc=dsg, inc_amt=16)
    w_bias = [(prefix + "db", p.val(prefix + "db")), (dsg, p.val(dsg))]
    p.op("scalar", lambda e: e.activation(out=Asb[:, :], in_=Alog[:, :], func=AF.Exp),
         waits=w_bias, inc=ss)
    p.op("vector", lambda e: e.tensor_scalar(out=Asb[:, :], in0=Asb[:, :],
                                             scalar1=-1.0, scalar2=None, op0=ALU.mult),
         waits=[(ss, p.val(ss))], inc=vs)
    p.op("vector", lambda e: e.memset(carry[:, :], 0.0), inc=vs)
    p.op("vector", lambda e: e.memset(bcrows[:, :], 0.0), inc=vs)
    p.op("vector", lambda e: e.memset(ubig[:, :], 0.0), inc=vs)
    p.op("vector", lambda e: e.memset(utail[:, :], 0.0), inc=vs)

    prev_bar = tuple(entry_waits)
    nch = L // Lc
    for c in range(nch):
        sl = slice(c * Lc, (c + 1) * Lc)
        dx = prefix + ("dxa" if c % 2 == 0 else "dxb")
        dgot = None
        for i in range(ND):
            dgot = p.op("sync", lambda e, i=i, sl=sl: e.dma_start(
                out=sx[i][:, :], in_=xin_ap(i, sl)),
                waits=prev_bar if i == 0 else (), inc=dx, inc_amt=16)
        emit_ln(p, m, [sx[i][:, :] for i in range(ND)], xwf, Lc, tag=prefix,
                dwaits=[(dx, dgot)], vsem=vs, ssem=ss, tsem=ts)
        v_l1 = p.val(vs)
        for i in range(ND):
            p.op("gpsimd", lambda e, i=i: e.dma_start(out=xwstash[i, :, :],
                                                      in_=xwf[i]),
                 waits=[(vs, v_l1)] if i == 0 else (), inc=prefix + "dst", inc_amt=16)
        emit_ln(p, m, xwf, xz, Lc, tag=prefix,
                dwaits=[(vs, v_l1)],
                gamma=[gamT[:, i:i + 1] for i in range(ND)],
                beta=[betT[:, i:i + 1] for i in range(ND)],
                eps2=1e-6, vsem=vs, ssem=ss, tsem=ts)
        v_l2, s_l2 = p.val(vs), p.val(ss)
        ipguard = [None, None]
        ipmm = [0, 0]
        for e_t in range(NE):
            par = e_t % 2
            dip = prefix + ("dia" if par == 0 else "dib")
            dw = p.op("gpsimd", lambda e, e_t=e_t, par=par: e.dma_start(
                out=ipW[par][:, :].rearrange("p (k j) -> p k j", j=128),
                in_=d["ipT"][:, e_t * 128:(e_t + 1) * 128].rearrange(
                    "(k p) j -> p k j", p=128)),
                waits=([(ts, ipmm[par])] if ipmm[par] else [(ts, p.val(ts))]),
                inc=dip, inc_amt=16)
            ps = psW0 if par == 0 else psW1
            for k in range(ND):
                w0 = (([(dip, dw), (vs, v_l2), (ss, s_l2)] +
                       ([ipguard[par]] if ipguard[par] else []))
                      if k == 0 else ())
                p.op("tensor", lambda e, e_t=e_t, k=k, ps=ps, par=par: e.matmul(
                    out=ps[:, :], lhsT=ipW[par][:, k * 128:(k + 1) * 128],
                    rhs=xz[k][:, :], start=(k == 0), stop=(k == ND - 1)),
                    waits=w0, inc=ts if k == ND - 1 else None)
            t_ip = p.val(ts)
            ipmm[par] = t_ip
            if e_t < NDT:
                p.op("scalar", lambda e, e_t=e_t, ps=ps: e.activation(
                    out=uti[e_t][:, 4:], in_=ps[:, :], func=AF.Copy),
                    waits=[(ts, t_ip)], inc=ss)
                p.op("scalar", lambda e, e_t=e_t: e.activation(
                    out=uti[e_t][:, 0:4], in_=utail[:, e_t * 4:(e_t + 1) * 4],
                    func=AF.Copy), inc=ss)
            else:
                zi = e_t - NDT
                p.op("scalar", lambda e, ps=ps: e.activation(
                    out=zt[:, :], in_=ps[:, :], func=AF.Copy),
                    waits=[(ts, t_ip), (prefix + "dst", p.val(prefix + "dst"))], inc=ss)
                s_z = p.val(ss)
                p.op("gpsimd", lambda e, zi=zi: e.dma_start(
                    out=zstash[zi, :, :], in_=zt[:, :]),
                    waits=[(ss, s_z)], inc=prefix + "dst", inc_amt=16)
            ipguard[par] = (ss, p.val(ss))
        s_ip_all = p.val(ss)
        for t in range(NDT):
            p.op("vector", lambda e, t=t: e.tensor_copy(
                out=utail[:, t * 4:(t + 1) * 4], in_=uti[t][:, Lc:Lc + 4]),
                waits=[(ss, s_ip_all)] if t == 0 else (), inc=vs)
        v_tails = p.val(vs)
        for t in range(NDT):
            p.op("scalar", lambda e, t=t: e.activation(
                out=yf[:, :], in_=uti[t][:, 1:1 + Lc], func=AF.Copy,
                scale=convW[:, 4 * t:4 * t + 1]),
                waits=[(vs, v_tails)] if t == 0 else (), inc=ss)
            s_c0 = p.val(ss)
            for kk in range(1, 4):
                p.op("vector", lambda e, t=t, kk=kk: e.scalar_tensor_tensor(
                    out=yf[:, :], in0=uti[t][:, 1 + kk:1 + kk + Lc],
                    scalar=convW[:, 4 * t + kk:4 * t + kk + 1], in1=yf[:, :],
                    op0=ALU.mult, op1=ALU.add),
                    waits=[(ss, s_c0)] if kk == 1 else (), inc=vs)
            v_cv = p.val(vs)
            p.op("scalar", lambda e, t=t: e.activation(
                out=uc[:, t * Lc:(t + 1) * Lc], in_=yf[:, :], func=AF.Silu,
                bias=convB[:, t:t + 1]), waits=[(vs, v_cv)], inc=ss)
        s_uc = p.val(ss)
        for k in range(NDT):
            p.op("tensor", lambda e, k=k: e.matmul(
                out=psW0[:, :], lhsT=xpW[:, k * 128:(k + 1) * 128],
                rhs=uc[:, k * Lc:(k + 1) * Lc], start=(k == 0), stop=(k == NDT - 1)),
                waits=[(ss, s_uc)] if k == 0 else (), inc=ts if k == NDT - 1 else None)
        t_xd = p.val(ts)
        p.op("vector", lambda e: e.tensor_copy(out=xdbl[:, :], in_=psW0[:, :]),
             waits=[(ts, t_xd)], inc=vs)
        p.op("vector", lambda e: e.tensor_copy(out=xdbf[:, :], in_=xdbl[:, :]), inc=vs)
        v_xd = p.val(vs)
        p.op("gpsimd", lambda e: e.dma_start(out=bcstash[0, :, :], in_=xdbl[64:80, :]),
             waits=[(vs, v_xd)], inc=prefix + "dbc", inc_amt=16)
        p.op("gpsimd", lambda e: e.dma_start(out=bcstash[1, :, :], in_=xdbl[80:96, :]),
             inc=prefix + "dbc", inc_amt=16)
        d_bc1 = p.val(prefix + "dbc")
        p.op("gpsimd", lambda e: e.dma_start(
            out=bcrows[0:1, :], in_=bcstash[0:1].rearrange("o n l -> o (n l)")),
            waits=[(prefix + "dbc", d_bc1)], inc=prefix + "dbc", inc_amt=16)
        p.op("gpsimd", lambda e: e.dma_start(
            out=bcrows[1:2, :], in_=bcstash[1:2].rearrange("o n l -> o (n l)")),
            inc=prefix + "dbc", inc_amt=16)
        d_bc = p.val(prefix + "dbc")
        bev, cev = None, None
        for n in range(NN):
            p.op("tensor", lambda e, n=n: e.matmul(
                out=psW0[:, :], lhsT=sel0[:, :], rhs=bcrows[:, n * Lc:(n + 1) * Lc],
                start=True, stop=True),
                waits=([(prefix + "dbc", d_bc)] if n == 0 else
                       ([(vs, bev)] if bev else ())), inc=ts)
            t_b = p.val(ts)
            p.op("vector", lambda e, n=n: e.tensor_copy(
                out=Brep[:, n * Lc:(n + 1) * Lc], in_=psW0[:, :]),
                waits=[(ts, t_b)], inc=vs)
            bev = p.val(vs)
            p.op("tensor", lambda e, n=n: e.matmul(
                out=psW1[:, :], lhsT=sel1[:, :], rhs=bcrows[:, n * Lc:(n + 1) * Lc],
                start=True, stop=True),
                waits=([(vs, cev)] if cev else ()), inc=ts)
            t_c = p.val(ts)
            p.op("vector", lambda e, n=n: e.tensor_copy(
                out=Crep[:, n * Lc:(n + 1) * Lc], in_=psW1[:, :]),
                waits=[(ts, t_c)], inc=vs)
            cev = p.val(vs)
        v_bc = p.val(vs)
        for t in range(NDT):
            ps = psW0 if t % 2 == 0 else psW1
            p.op("tensor", lambda e, t=t, ps=ps: e.matmul(
                out=ps[:, :], lhsT=dtW[:, t * 128:(t + 1) * 128], rhs=xdbf[:, :],
                start=True, stop=True),
                waits=[(vs, v_bc)] if t == 0 else (), inc=ts)
            t_dt = p.val(ts)
            p.op("scalar", lambda e, t=t, ps=ps: e.activation(
                out=yf[:, :], in_=ps[:, :], func=AF.Abs, bias=dtb[:, t:t + 1]),
                waits=[(ts, t_dt)], inc=ss)
            p.op("scalar", lambda e: e.activation(
                out=yf[:, :], in_=yf[:, :], func=AF.Exp, scale=-1.0), inc=ss)
            p.op("scalar", lambda e: e.activation(
                out=yf[:, :], in_=yf[:, :], func=AF.Ln, bias=1.0), inc=ss)
            s_sp = p.val(ss)
            p.op("vector", lambda e, t=t, ps=ps: e.scalar_tensor_tensor(
                out=dAb[:, :], in0=ps[:, :], scalar=dtb[:, t:t + 1],
                op0=ALU.add, op1=ALU.max, in1=m.const(0.0).broadcast_to([128, Lc])),
                waits=[(ss, s_sp)], inc=vs)
            p.op("vector", lambda e, t=t: e.tensor_add(
                out=delta[:, t * Lc:(t + 1) * Lc], in0=dAb[:, :], in1=yf[:, :]),
                inc=vs)
            v_dl = p.val(vs)
            p.op("tensor", lambda e: None, waits=[(vs, v_dl)], real=False)
        NH = NN // 2
        dAh = [bcrows[:, 0:NH * Lc], bcrows[:, NH * Lc:NN * Lc]]
        dAh3 = [a.rearrange("p (n l) -> p n l", l=Lc) for a in dAh]
        dBu_all = ubig[:, 0:NN * Lc]
        dBu3 = dBu_all.rearrange("p (n l) -> p n l", l=Lc)
        Brep3 = Brep[:, :].rearrange("p (n l) -> p n l", l=Lc)
        zc = m.const(0.0)
        scan_done = [0, 0]
        for t in range(NDT):
            p.op("vector", lambda e, t=t: e.tensor_mul(
                out=du[:, :], in0=delta[:, t * Lc:(t + 1) * Lc],
                in1=uc[:, t * Lc:(t + 1) * Lc]), inc=vs)
            v_du = p.val(vs)
            p.op("vector", lambda e: e.tensor_tensor(
                out=dBu3, in0=Brep3,
                in1=du[:, :].unsqueeze(1).broadcast_to([128, NN, Lc]),
                op=ALU.mult), inc=vs)
            for h in range(2):
                base = t * NN + h * NH
                war = ([(vs, scan_done[h])] if scan_done[h]
                       else [(vs, v_du)])
                for n in range(NH):
                    p.op("scalar", lambda e, t=t, n=n, h=h, base=base: e.activation(
                        out=dAh[h][:, n * Lc:(n + 1) * Lc],
                        in_=delta[:, t * Lc:(t + 1) * Lc], func=AF.Exp,
                        scale=Asb[:, base + n:base + n + 1]),
                        waits=war if n == 0 else (), inc=ss)
                s_dA = p.val(ss)
                p.op("vector", lambda e, h=h, base=base: e.tensor_tensor(
                    out=tmp16[:, 0:NH].unsqueeze(2), in0=dAh3[h][:, :, 0:1],
                    in1=carry[:, base:base + NH].unsqueeze(2), op=ALU.mult),
                    waits=[(ss, s_dA)], inc=vs)
                p.op("vector", lambda e, h=h: e.tensor_add(
                    out=dBu3[:, h * NH:(h + 1) * NH, 0:1],
                    in0=dBu3[:, h * NH:(h + 1) * NH, 0:1],
                    in1=tmp16[:, 0:NH].unsqueeze(2)), inc=vs)
                p.op("vector", lambda e, h=h: e.tensor_scalar(
                    out=dAh3[h][:, :, 0:1], in0=dAh3[h][:, :, 0:1],
                    scalar1=0.0, scalar2=None, op0=ALU.mult), inc=vs)
                p.op("vector", lambda e, h=h: e.tensor_tensor_scan(
                    out=states[:, h * NH * Lc:(h + 1) * NH * Lc], data0=dAh[h],
                    data1=dBu_all[:, h * NH * Lc:(h + 1) * NH * Lc], initial=zc,
                    op0=ALU.mult, op1=ALU.add), inc=vs)
                scan_done[h] = p.val(vs)
            v_scan = p.val(vs)
            sv = states.rearrange("p (n l) -> p n l", l=Lc)
            p.op("vector", lambda e, t=t, sv=sv: e.tensor_copy(
                out=carry[:, t * NN:(t + 1) * NN].unsqueeze(2),
                in_=sv[:, 0:NN, Lc - 1:Lc]), inc=vs)
            p.op("vector", lambda e: e.tensor_mul(
                out=states[:, 0:NN * Lc], in0=states[:, 0:NN * Lc],
                in1=Crep[:, :]), inc=vs)
            half = NN * Lc // 2
            p.op("vector", lambda e, half=half: e.tensor_add(
                out=states[:, 0:half], in0=states[:, 0:half],
                in1=states[:, half:2 * half]), inc=vs)
            q = half // 2
            p.op("vector", lambda e, q=q: e.tensor_add(
                out=states[:, 0:q], in0=states[:, 0:q], in1=states[:, q:2 * q]),
                inc=vs)
            o8 = q // 2
            p.op("vector", lambda e, o8=o8: e.tensor_add(
                out=states[:, 0:o8], in0=states[:, 0:o8], in1=states[:, o8:2 * o8]),
                inc=vs)
            p.op("vector", lambda e: e.tensor_add(
                out=yf[:, :], in0=states[:, 0:Lc], in1=states[:, Lc:2 * Lc]),
                inc=vs)
            p.op("vector", lambda e, t=t: e.scalar_tensor_tensor(
                out=yg[:, t * Lc:(t + 1) * Lc], in0=uc[:, t * Lc:(t + 1) * Lc],
                scalar=Dd[:, t:t + 1], in1=yf[:, :], op0=ALU.mult, op1=ALU.add),
                inc=vs)
        v_y = p.val(vs)
        zdone = [(prefix + "dst", p.val(prefix + "dst"))]
        for t in range(NDT):
            zb_t = zt if t % 2 == 0 else xdbf
            sz_t = dAb if t % 2 == 0 else dAb2
            dz = prefix + ("dza" if t % 2 == 0 else "dzb")
            dzg = p.op("gpsimd", lambda e, t=t, zb_t=zb_t: e.dma_start(
                out=zb_t[:, :], in_=zstash[t, :, :]),
                waits=(zdone + [(ss, p.val(ss)), (vs, p.val(vs))]) if t < 2 else
                      [(ss, p.val(ss))], inc=dz, inc_amt=16)
            p.op("scalar", lambda e, zb_t=zb_t, sz_t=sz_t: e.activation(
                out=sz_t[:, :], in_=zb_t[:, :], func=AF.Silu),
                waits=[(dz, dzg), (vs, p.val(vs))], inc=ss)
            s_z = p.val(ss)
            p.op("vector", lambda e, t=t, sz_t=sz_t: e.tensor_mul(
                out=yg[:, t * Lc:(t + 1) * Lc], in0=yg[:, t * Lc:(t + 1) * Lc],
                in1=sz_t[:, :]), waits=[(ss, s_z)], inc=vs)
        v_g = p.val(vs)
        xwdone = [(prefix + "dst", p.val(prefix + "dst"))]
        opmm = [0, 0]
        for o in range(ND):
            par = o % 2
            dop = prefix + ("dpa" if par == 0 else "dpb")
            dw = p.op("gpsimd", lambda e, o=o, par=par: e.dma_start(
                out=opW[par][:, :].rearrange("p (k j) -> p k j", j=128),
                in_=d["opT"][:, o * 128:(o + 1) * 128].rearrange(
                    "(k p) j -> p k j", p=128)),
                waits=([(ts, opmm[par])] if opmm[par] else [(ts, p.val(ts))]),
                inc=dop, inc_amt=16)
            dxw = p.op("sync", lambda e, o=o: e.dma_start(
                out=zt[:, :], in_=xwstash[o, :, :]),
                waits=(xwdone + [(vs, p.val(vs))]) if o == 0 else
                      [(vs, p.val(vs))], inc=prefix + "dxw", inc_amt=16)
            for k in range(NDT):
                p.op("tensor", lambda e, o=o, k=k, par=par: e.matmul(
                    out=psO[:, :], lhsT=opW[par][:, k * 128:(k + 1) * 128],
                    rhs=yg[:, k * Lc:(k + 1) * Lc], start=(k == 0), stop=(k == NDT - 1)),
                    waits=[(dop, dw), (vs, v_g)] if k == 0 else (),
                    inc=ts if k == NDT - 1 else None)
            t_o = p.val(ts)
            opmm[par] = t_o
            p.op("vector", lambda e, o=o: e.tensor_add(
                out=yf[:, :], in0=psO[:, :], in1=zt[:, :]),
                waits=[(ts, t_o), (prefix + "dxw", dxw)], inc=vs)
            p.op("vector", lambda e, o=o: e.tensor_add(
                out=xo[o], in0=yf[:, :], in1=sx[o][:, :]), inc=vs)
            v_x = p.val(vs)
            p.op("sync", lambda e, o=o, sl=sl: e.dma_start(
                out=xout_ap(o, sl), in_=xo[o]),
                waits=[(vs, v_x)], inc=prefix + "do", inc_amt=16)
            p.op("tensor", lambda e: None, waits=[(vs, v_x)], real=False)
        prev_bar = p.barrier()
    p.wait("sync", [(prefix + "do", p.val(prefix + "do"))])


def build_att1_launch(in_shapes, L=4096, Lc=512):
    nc = bass.Bass("TRN2", target_bir_lowering=False)
    d = declare_inputs(nc, in_shapes)
    xout = nc.dram_tensor("xout", [ND, 128, L], F32, kind="ExternalOutput").ap()
    p = Plan(nc)
    m = Sbuf(p)
    m.dram = d
    emit_const_loads(p, m)
    emit_att1(p, m, lambda i, sl: d["xin"][i, :, sl],
              lambda o, sl: xout[o, :, sl], L, Lc)
    p.build()
    return nc, m



def emit_att0(p, m, xin_ap, xout_ap, T, prefix="at", entry_waits=()):
    nc = p.nc
    d = m.dram
    Lc = 512
    NCH = T // Lc
    NTC = T // 128
    G = T // 64
    NPR = 8
    ds, dsg, vs, ss, ts = (prefix + "d", prefix + "dg", prefix + "v",
                           prefix + "s", prefix + "t")

    sx = [m.sb(f"{prefix}sx{i}", (128, Lc)) for i in range(ND)]
    xln = [m.sb(f"{prefix}xln{i}", (128, Lc), BF16) for i in range(ND)]
    B2 = lambda nm, sh, dt=BF16: [m.sb(f"{prefix}{nm}{j}", sh, dt) for j in range(2)]
    qp, kp, vT, ET, LT = (B2("qp", (128, T)), B2("kp", (128, T)), B2("vT", (128, T)),
                          B2("ET", (128, T)), B2("LT", (128, T)))
    wqkv = [[m.sb(f"{prefix}wqkv{j}_{par}", (128, ND * 128), BF16) for j in range(3)]
            for par in range(2)]
    xlc = [m.sb(f"{prefix}xlc{par}_{i}", (128, Lc), BF16) for par in range(2)
           for i in range(ND)]
    xlt = lambda par, i: xlc[par * ND + i]
    klraw, qlraw = B2("klraw", (128, 64), F32), B2("qlraw", (128, 64), F32)
    blk_k, blk_q = B2("blk_k", (128, 128)), B2("blk_q", (128, 128))
    Kblk, Vblk, KVb, Tb = (B2("Kblk", (128, 128)), B2("Vblk", (128, 128)),
                           B2("KVb", (128, 128)), B2("Tb", (128, 128)))
    PTsb, NVsb, PNv = B2("PTsb", (128, 128)), B2("NVsb", (128, 128)), B2("PNv", (128, 128))
    sm2, sm2b = B2("sm2", (128, 2), F32), B2("sm2b", (128, 2), F32)
    Lnc, vbf, Ech = B2("Lnc", (128, 128)), B2("vbf", (128, Lc)), B2("Ech", (128, Lc))
    Nmx, Ers = B2("Nmx", (128, NCH + 1), F32), B2("Ers", (128, NCH + 1), F32)
    col1 = B2("col1", (128, 1), F32)
    ytmp = B2("ytmp", (128, Lc))
    w2sb = m.sb(prefix + "w2sb", (128, ND * 1024), BF16)
    ypr = [m.sb(f"{prefix}ypr{pr}", (128, Lc), BF16) for pr in range(NPR)]
    xo = [m.sb(f"{prefix}xo{i}", (128, Lc)) for i in range(ND)]
    eye = m.sb("k_eye_bf", (128, 128), BF16)
    onesblk = m.sb("k_onesblk_bf", (128, 128), BF16)

    psW0 = m.ps(prefix + "psW0", (128, Lc))
    psW1 = m.ps(prefix + "psW1", (128, Lc))
    psWn = [psW0, psW1]
    psQ = [m.ps(f"{prefix}psQ{j}", (128, Lc)) for j in range(2)]
    psNV = [m.ps(f"{prefix}psNV{j}", (128, 128)) for j in range(2)]
    pst = [m.ps(f"{prefix}pst{j}", (128, 128), BF16) for j in range(2)]
    psNVp = [[psNV[par], psNV[par]] for par in range(2)]
    pstp = [[pst[par], pst[par]] for par in range(2)]
    pstQ = [psQ[1].bitcast(BF16)[:, j * 128:(j + 1) * 128] for j in range(2)]
    m.alias("psA", psW0)
    m.alias("psB", psW1)
    m.alias("psR", psQ[0])

    _uq = sum(len(v) for v in p.ops.values())
    xlnstash = nc.dram_tensor(f"{prefix}_xlnst{_uq}", [ND, 128, T], BF16, kind="Internal").ap()
    ystash = nc.dram_tensor(f"{prefix}_yst{_uq}", [NPR, 128, T], BF16, kind="Internal").ap()

    for i in range(ND):
        m.alias(f"ln_xbf{i}", ET[0][:, i * Lc:(i + 1) * Lc])
        m.alias(f"ln_xsq{i}", ET[1][:, i * Lc:(i + 1) * Lc])

    prev_bar = tuple(entry_waits)
    for c in range(NCH):
        sl = slice(c * Lc, (c + 1) * Lc)
        dx = prefix + ("dxa" if c % 2 == 0 else "dxb")
        dgot = None
        for i in range(ND):
            dgot = p.op("sync", lambda e, i=i, sl=sl: e.dma_start(
                out=sx[i][:, :], in_=xin_ap(i, sl)),
                waits=prev_bar if i == 0 else (), inc=dx, inc_amt=16)
        emit_ln(p, m, [sx[i][:, :] for i in range(ND)], xln, Lc, tag=prefix,
                dwaits=[(dx, dgot)], vsem=vs, ssem=ss, tsem=ts)
        v_ln = p.val(vs)
        for i in range(ND):
            p.op("sync", lambda e, i=i, sl=sl: e.dma_start(
                out=xlnstash[i, :, sl], in_=xln[i][:, :]),
                waits=[(vs, v_ln)] if i == 0 else (), inc=prefix + "dst", inc_amt=16)
        prev_bar = p.barrier()
    stash_done = [(prefix + "dst", p.val(prefix + "dst"))]

    qkv_mm_done = [0, 0]
    chain_done = [0, 0]
    xlc_done = [0, 0]
    vt_evac_done = [0]
    lm_state = {}

    def emit_qkv(pr):
        yield
        par = pr % 2
        dwsem = prefix + ("dwa" if par == 0 else "dwb")
        dw = None
        for j, coff in enumerate((pr * 128, 1024 + pr * 128, 2048 + pr * 128)):
            dw = p.op("gpsimd", lambda e, j=j, coff=coff, par=par: e.dma_start(
                out=wqkv[par][j][:, :].rearrange("p (k j2) -> p k j2", j2=128),
                in_=d["wattn"][:, coff:coff + 128].rearrange("(k p) j2 -> p k j2", p=128)),
                waits=[(ts, qkv_mm_done[par])] if j == 0 else (),
                inc=dwsem, inc_amt=16)
        first = [(vs, chain_done[par])] if chain_done[par] else []
        for c in range(NCH):
            sl = slice(c * Lc, (c + 1) * Lc)
            cpar = c % 2
            dx = prefix + ("dla" if cpar == 0 else "dlb")
            dl = None
            for i in range(ND):
                w0 = (stash_done + [(ts, p.val(ts))]) if (pr == 0 and c == 0 and i == 0) \
                    else ((([(ts, xlc_done[cpar])] if xlc_done[cpar] else []) + first)
                          if i == 0 else ())
                dl = p.op("sync", lambda e, i=i, sl=sl, cpar=cpar: e.dma_start(
                    out=xlt(cpar, i)[:, :], in_=xlnstash[i, :, sl]),
                    waits=w0, inc=dx, inc_amt=16)
            for j, dst in ((0, qp[par]), (1, kp[par])):
                ps = psQ[j]
                for k in range(ND):
                    w0 = ([(dx, dl), (dwsem, dw)] +
                          ([(vs, vt_evac_done[0])] if vt_evac_done[0] else [])
                          ) if k == 0 else ()
                    p.op("tensor", lambda e, j=j, k=k, ps=ps, cpar=cpar, par=par: e.matmul(
                        out=ps[:, :], lhsT=wqkv[par][j][:, k * 128:(k + 1) * 128],
                        rhs=xlt(cpar, k)[:, :], start=(k == 0), stop=(k == ND - 1)),
                        waits=w0, inc=ts if k == ND - 1 else None)
                t_qk = p.val(ts)
                p.op("vector", lambda e, j=j, dst=dst, sl=sl, ps=ps: e.tensor_copy(
                    out=dst[:, sl], in_=ps[:, :]), waits=[(ts, t_qk)], inc=vs)
            v_qk = p.val(vs)
            p.op("tensor", lambda e: None, waits=[(vs, v_qk)], real=False)
            for k in range(ND):
                p.op("tensor", lambda e, k=k, cpar=cpar, par=par: e.matmul(
                    out=psQ[0][:, :], lhsT=wqkv[par][2][:, k * 128:(k + 1) * 128],
                    rhs=xlt(cpar, k)[:, :], start=(k == 0), stop=(k == ND - 1)),
                    inc=ts if k == ND - 1 else None)
            t_v = p.val(ts)
            xlc_done[cpar] = t_v
            p.op("vector", lambda e, par=par: e.tensor_copy(out=vbf[par][:, :],
                                                            in_=psQ[0][:, :]),
                 waits=[(ts, t_v)], inc=vs)
            v_vb = p.val(vs)
            for j2 in range(Lc // 128):
                tp_ = pstQ[j2 % 2]
                p.op("tensor", lambda e, j2=j2, par=par, tp_=tp_: e.transpose(
                    out=tp_[:, :], in_=vbf[par][:, j2 * 128:(j2 + 1) * 128],
                    identity=eye[:, :]),
                    waits=[(vs, v_vb)] if j2 == 0 else [(vs, p.val(vs))], inc=ts)
                t_tr = p.val(ts)
                p.op("vector", lambda e, c=c, j2=j2, par=par, tp_=tp_: e.tensor_copy(
                    out=vT[par][:, (c * 4 + j2) * 128:(c * 4 + j2 + 1) * 128],
                    in_=tp_[:, :]), waits=[(ts, t_tr)], inc=vs)
            vt_evac_done[0] = p.val(vs)
            yield
        qkv_mm_done[par] = p.val(ts)
        p.op("vector", lambda e, par=par: e.tensor_reduce(
            out=qlraw[par][:, :], in_=qp[par][:, :].rearrange("p (mm g) -> p mm g", mm=64),
            axis=AX, op=ALU.add), waits=[(vs, p.val(vs))], inc=vs)
        p.op("vector", lambda e, par=par: e.tensor_reduce(
            out=klraw[par][:, :], in_=kp[par][:, :].rearrange("p (mm g) -> p mm g", mm=64),
            axis=AX, op=ALU.add), inc=vs)
        p.op("vector", lambda e, par=par: e.memset(blk_k[par][:, :], 0.0), inc=vs)
        p.op("vector", lambda e, par=par: e.memset(blk_q[par][:, :], 0.0), inc=vs)
        p.op("vector", lambda e, par=par: e.memset(sm2b[par][:, :], 0.0), inc=vs)
        v_lm = p.val(vs)
        p.op("scalar", lambda e, par=par: e.activation(
            out=blk_k[par][0:64, 0:64], in_=klraw[par][0:64, :], func=AF.Copy,
            scale=1.0 / G), waits=[(vs, v_lm)], inc=ss)
        p.op("scalar", lambda e, par=par: e.activation(
            out=blk_k[par][64:128, 64:128], in_=klraw[par][64:128, :], func=AF.Copy,
            scale=1.0 / G), inc=ss)
        p.op("scalar", lambda e, par=par: e.activation(
            out=blk_q[par][0:64, 0:64], in_=qlraw[par][0:64, :], func=AF.Copy,
            scale=1.0 / G), inc=ss)
        p.op("scalar", lambda e, par=par: e.activation(
            out=blk_q[par][64:128, 64:128], in_=qlraw[par][64:128, :], func=AF.Copy,
            scale=1.0 / G), inc=ss)
        lm_state[pr] = (p.val(vs), p.val(ss))

    def emit_chain(pr):
        yield
        par = pr % 2
        v_lm, s_blk = lm_state[pr]
        for tc in range(NTC):
            pv = psNVp[par][tc % 2]
            tp_ = pstp[par][tc % 2]
            p.op("tensor", lambda e, tc=tc, par=par, pv=pv: e.matmul(
                out=pv[:, :], lhsT=qp[par][:, tc * 128:(tc + 1) * 128],
                rhs=blk_k[par][:, :], start=True, stop=True),
                waits=[(ss, s_blk)] if tc == 0 else [(vs, p.val(vs))], inc=ts)
            t_L = p.val(ts)
            p.op("vector", lambda e, par=par, pv=pv: e.tensor_reduce(
                out=sm2[par][:, :], in_=pv[:, :].rearrange("p (h mm) -> p h mm", h=2),
                axis=AX, op=ALU.max), waits=[(ts, t_L)], inc=vs)
            p.op("vector", lambda e, par=par: e.tensor_scalar(
                out=sm2[par][:, :], in0=sm2[par][:, :], scalar1=-1.0, scalar2=None,
                op0=ALU.mult), inc=vs)
            v_mx = p.val(vs)
            p.op("scalar", lambda e, par=par, pv=pv: e.activation(
                out=Lnc[par][:, 0:64], in_=pv[:, 0:64], func=AF.Exp,
                bias=sm2[par][:, 0:1], accum_out=sm2b[par][:, 0:1]),
                waits=[(vs, v_mx)], inc=ss)
            p.op("scalar", lambda e, par=par, pv=pv: e.activation(
                out=Lnc[par][:, 64:128], in_=pv[:, 64:128], func=AF.Exp,
                bias=sm2[par][:, 1:2], accum_out=sm2b[par][:, 1:2]), inc=ss)
            s_ex = p.val(ss)
            p.op("vector", lambda e, par=par: e.reciprocal(out=sm2[par][:, :],
                                                           in_=sm2b[par][:, :]),
                 waits=[(ss, s_ex)], inc=vs)
            p.op("vector", lambda e, par=par: e.tensor_scalar(
                out=Lnc[par][:, 0:64], in0=Lnc[par][:, 0:64], scalar1=sm2[par][:, 0:1],
                scalar2=None, op0=ALU.mult), waits=[(vs, p.val(vs))], inc=vs)
            p.op("vector", lambda e, par=par: e.tensor_scalar(
                out=Lnc[par][:, 64:128], in0=Lnc[par][:, 64:128],
                scalar1=sm2[par][:, 1:2], scalar2=None, op0=ALU.mult), inc=vs)
            v_Ln = p.val(vs)
            p.op("tensor", lambda e, par=par, tp_=tp_: e.transpose(
                out=tp_[:, :], in_=Lnc[par][:, :], identity=eye[:, :]),
                waits=[(vs, v_Ln)], inc=ts)
            t_lt = p.val(ts)
            p.op("vector", lambda e, tc=tc, par=par, tp_=tp_: e.tensor_copy(
                out=LT[par][:, tc * 128:(tc + 1) * 128], in_=tp_[:, :]),
                waits=[(ts, t_lt)], inc=vs)
            yield
        for c in range(NCH):
            p.op("tensor", lambda e, c=c, par=par: e.matmul(
                out=psWn[par][:, :], lhsT=blk_q[par][:, :], rhs=kp[par][:, c * Lc:(c + 1) * Lc],
                start=True, stop=True), waits=[(vs, p.val(vs))], inc=ts)
            t_N = p.val(ts)
            p.op("vector", lambda e, c=c, par=par: e.tensor_reduce(
                out=Nmx[par][:, c:c + 1], in_=psWn[par][:, :], axis=AX, op=ALU.max),
                waits=[(ts, t_N)], inc=vs)
            yield
        p.op("vector", lambda e, par=par: e.tensor_reduce(
            out=Nmx[par][:, NCH:NCH + 1], in_=Nmx[par][:, 0:NCH], axis=AX, op=ALU.max),
            inc=vs)
        p.op("vector", lambda e, par=par: e.tensor_scalar(
            out=col1[par][:, :], in0=Nmx[par][:, NCH:NCH + 1], scalar1=-1.0,
            scalar2=None, op0=ALU.mult), inc=vs)
        v_gm = p.val(vs)
        for c in range(NCH):
            p.op("tensor", lambda e, c=c, par=par: e.matmul(
                out=psWn[par][:, :], lhsT=blk_q[par][:, :], rhs=kp[par][:, c * Lc:(c + 1) * Lc],
                start=True, stop=True),
                waits=[(vs, v_gm)] if c == 0 else [(ss, p.val(ss))], inc=ts)
            t_N = p.val(ts)
            p.op("scalar", lambda e, c=c, par=par: e.activation(
                out=Ech[par][:, :], in_=psWn[par][:, :], func=AF.Exp, bias=col1[par][:, 0:1],
                accum_out=Ers[par][:, c:c + 1]), waits=[(ts, t_N)], inc=ss)
            s_E = p.val(ss)
            for j2 in range(Lc // 128):
                tp_ = pstp[par][j2 % 2]
                p.op("tensor", lambda e, j2=j2, par=par, tp_=tp_: e.transpose(
                    out=tp_[:, :], in_=Ech[par][:, j2 * 128:(j2 + 1) * 128],
                    identity=eye[:, :]),
                    waits=[(ss, s_E)] if j2 == 0 else [(vs, p.val(vs))], inc=ts)
                t_te = p.val(ts)
                p.op("vector", lambda e, c=c, j2=j2, par=par, tp_=tp_: e.tensor_copy(
                    out=ET[par][:, (c * 4 + j2) * 128:(c * 4 + j2 + 1) * 128],
                    in_=tp_[:, :]), waits=[(ts, t_te)], inc=vs)
            yield
        p.op("vector", lambda e, par=par: e.tensor_reduce(
            out=Ers[par][:, NCH:NCH + 1], in_=Ers[par][:, 0:NCH], axis=AX, op=ALU.add),
            inc=vs)
        p.op("vector", lambda e, par=par: e.reciprocal(
            out=col1[par][:, :], in_=Ers[par][:, NCH:NCH + 1]),
            waits=[(vs, p.val(vs))], inc=vs)
        v_rsn = p.val(vs)
        for tc in range(NTC):
            p.op("tensor", lambda e, tc=tc, par=par: e.matmul(
                out=psNV[par][:, :], lhsT=ET[par][:, tc * 128:(tc + 1) * 128],
                rhs=vT[par][:, tc * 128:(tc + 1) * 128], start=(tc == 0),
                stop=(tc == NTC - 1)),
                waits=[(vs, v_rsn)] if tc == 0 else (),
                inc=ts if tc == NTC - 1 else None)
        t_nv = p.val(ts)
        yield
        p.op("vector", lambda e, par=par: e.memset(NVsb[par][:, :], 0.0),
             waits=[(ts, t_nv)], inc=vs)
        p.op("vector", lambda e, par=par: e.tensor_scalar(
            out=NVsb[par][0:64, 0:64], in0=psNV[par][0:64, 0:64],
            scalar1=col1[par][0:64, 0:1], scalar2=None, op0=ALU.mult),
            waits=[(vs, p.val(vs))], inc=vs)
        p.op("vector", lambda e, par=par: e.tensor_scalar(
            out=NVsb[par][64:128, 64:128], in0=psNV[par][64:128, 64:128],
            scalar1=col1[par][64:128, 0:1], scalar2=None, op0=ALU.mult), inc=vs)
        v_nv = p.val(vs)
        p.op("tensor", lambda e, par=par: e.matmul(
            out=psNV[par][:, :], lhsT=blk_q[par][:, :], rhs=blk_k[par][:, :],
            start=True, stop=True), waits=[(vs, v_nv)], inc=ts)
        t_P = p.val(ts)
        p.op("vector", lambda e, par=par: e.tensor_reduce(
            out=sm2[par][:, :], in_=psNV[par][:, :].rearrange("p (h mm) -> p h mm", h=2),
            axis=AX, op=ALU.max), waits=[(ts, t_P)], inc=vs)
        p.op("vector", lambda e, par=par: e.tensor_scalar(
            out=sm2[par][:, :], in0=sm2[par][:, :], scalar1=-1.0, scalar2=None,
            op0=ALU.mult), inc=vs)
        p.op("vector", lambda e, par=par: e.memset(Kblk[par][:, :], 0.0), inc=vs)
        v_pm = p.val(vs)
        p.op("scalar", lambda e, par=par: e.activation(
            out=Kblk[par][0:64, 0:64], in_=psNV[par][0:64, 0:64], func=AF.Exp,
            bias=sm2[par][0:64, 0:1], accum_out=sm2b[par][0:64, 0:1]),
            waits=[(vs, v_pm)], inc=ss)
        p.op("scalar", lambda e, par=par: e.activation(
            out=Kblk[par][64:128, 64:128], in_=psNV[par][64:128, 64:128], func=AF.Exp,
            bias=sm2[par][64:128, 1:2], accum_out=sm2b[par][64:128, 1:2]), inc=ss)
        s_pk = p.val(ss)
        p.op("vector", lambda e, par=par: e.reciprocal(out=sm2[par][:, 0:1],
                                                       in_=sm2b[par][:, 0:1]),
             waits=[(ss, s_pk)], inc=vs)
        p.op("vector", lambda e, par=par: e.reciprocal(out=sm2[par][:, 1:2],
                                                       in_=sm2b[par][:, 1:2]), inc=vs)
        p.op("vector", lambda e, par=par: e.tensor_scalar(
            out=Kblk[par][0:64, 0:64], in0=Kblk[par][0:64, 0:64],
            scalar1=sm2[par][0:64, 0:1], scalar2=None, op0=ALU.mult),
            waits=[(vs, p.val(vs))], inc=vs)
        p.op("vector", lambda e, par=par: e.tensor_scalar(
            out=Kblk[par][64:128, 64:128], in0=Kblk[par][64:128, 64:128],
            scalar1=sm2[par][64:128, 1:2], scalar2=None, op0=ALU.mult), inc=vs)
        v_K = p.val(vs)
        p.op("tensor", lambda e, par=par: e.matmul(
            out=psNV[par][:, :], lhsT=onesblk[:, :], rhs=Kblk[par][:, :],
            start=True, stop=True), waits=[(vs, v_K)], inc=ts)
        t_cs = p.val(ts)
        p.op("vector", lambda e, par=par: e.tensor_reduce(
            out=sm2b[par][:, 0:1], in_=psNV[par][:, :], axis=AX, op=ALU.max),
            waits=[(ts, t_cs)], inc=vs)
        p.op("vector", lambda e, par=par: e.reciprocal(out=sm2b[par][:, 1:2],
                                                       in_=sm2b[par][:, 0:1]),
             waits=[(vs, p.val(vs))], inc=vs)
        v_dn = p.val(vs)
        p.op("tensor", lambda e, par=par: e.transpose(
            out=pst[par][:, :], in_=Kblk[par][:, :], identity=eye[:, :]),
            waits=[(vs, v_dn)], inc=ts)
        t_kt = p.val(ts)
        p.op("vector", lambda e, par=par: e.tensor_scalar(
            out=Vblk[par][:, :], in0=pst[par][:, :], scalar1=sm2b[par][:, 1:2],
            scalar2=None, op0=ALU.mult), waits=[(ts, t_kt)], inc=vs)
        p.op("vector", lambda e, par=par: e.tensor_copy(out=PTsb[par][:, :],
                                                        in_=pst[par][:, :]), inc=vs)
        v_v0 = p.val(vs)
        for it in range(6):
            p.op("tensor", lambda e, par=par: e.matmul(
                out=psNV[par][:, :], lhsT=PTsb[par][:, :], rhs=Vblk[par][:, :],
                start=True, stop=True),
                waits=[(vs, v_v0 if it == 0 else p.val(vs))], inc=ts)
            t_kv = p.val(ts)
            p.op("vector", lambda e, par=par: e.tensor_copy(out=KVb[par][:, :],
                                                            in_=psNV[par][:, :]),
                 waits=[(ts, t_kv)], inc=vs)
            p.op("vector", lambda e, par=par: e.scalar_tensor_tensor(
                out=Tb[par][:, :], in0=eye[:, :], scalar=7.0, in1=psNV[par][:, :],
                op0=ALU.mult, op1=ALU.subtract), inc=vs)
            v_t1 = p.val(vs)
            p.op("tensor", lambda e, par=par: e.transpose(
                out=pst[par][:, :], in_=KVb[par][:, :], identity=eye[:, :]),
                waits=[(vs, v_t1)], inc=ts)
            t_kvt = p.val(ts)
            p.op("vector", lambda e, par=par: e.tensor_copy(out=KVb[par][:, :],
                                                            in_=pst[par][:, :]),
                 waits=[(ts, t_kvt)], inc=vs)
            v_kvt = p.val(vs)
            p.op("tensor", lambda e, par=par: e.matmul(
                out=psNV[par][:, :], lhsT=KVb[par][:, :], rhs=Tb[par][:, :],
                start=True, stop=True), waits=[(vs, v_kvt)], inc=ts)
            t_t2 = p.val(ts)
            p.op("vector", lambda e, par=par: e.scalar_tensor_tensor(
                out=Tb[par][:, :], in0=eye[:, :], scalar=15.0, in1=psNV[par][:, :],
                op0=ALU.mult, op1=ALU.subtract), waits=[(ts, t_t2)], inc=vs)
            v_t3 = p.val(vs)
            p.op("tensor", lambda e, par=par: e.matmul(
                out=psNV[par][:, :], lhsT=KVb[par][:, :], rhs=Tb[par][:, :],
                start=True, stop=True), waits=[(vs, v_t3)], inc=ts)
            t_t4 = p.val(ts)
            p.op("vector", lambda e, par=par: e.scalar_tensor_tensor(
                out=Tb[par][:, :], in0=eye[:, :], scalar=13.0, in1=psNV[par][:, :],
                op0=ALU.mult, op1=ALU.subtract), waits=[(ts, t_t4)], inc=vs)
            v_t5 = p.val(vs)
            p.op("tensor", lambda e, par=par: e.transpose(
                out=pst[par][:, :], in_=Vblk[par][:, :], identity=eye[:, :]),
                waits=[(vs, v_t5)], inc=ts)
            t_vt = p.val(ts)
            p.op("vector", lambda e, par=par: e.tensor_copy(out=PTsb[par][:, :],
                                                            in_=pst[par][:, :]),
                 waits=[(ts, t_vt)], inc=vs)
            v_vt = p.val(vs)
            p.op("tensor", lambda e, par=par: e.matmul(
                out=psNV[par][:, :], lhsT=PTsb[par][:, :], rhs=Tb[par][:, :],
                start=True, stop=True), waits=[(vs, v_vt)], inc=ts)
            t_vu = p.val(ts)
            p.op("scalar", lambda e, par=par: e.activation(
                out=Vblk[par][:, :], in_=psNV[par][:, :], func=AF.Copy, scale=0.25),
                waits=[(ts, t_vu)], inc=ss)
            s_vu = p.val(ss)
            if it < 5:
                p.op("tensor", lambda e, par=par: e.transpose(
                    out=pst[par][:, :], in_=Kblk[par][:, :], identity=eye[:, :]),
                    waits=[(ss, s_vu)], inc=ts)
                t_kt2 = p.val(ts)
                p.op("vector", lambda e, par=par: e.tensor_copy(
                    out=PTsb[par][:, :], in_=pst[par][:, :]),
                    waits=[(ts, t_kt2)], inc=vs)
            yield
        p.op("tensor", lambda e, par=par: e.transpose(
            out=pst[par][:, :], in_=Vblk[par][:, :], identity=eye[:, :]),
            waits=[(vs, p.val(vs)), (ss, p.val(ss))], inc=ts)
        t_pt = p.val(ts)
        p.op("vector", lambda e, par=par: e.tensor_copy(out=PTsb[par][:, :],
                                                        in_=pst[par][:, :]),
             waits=[(ts, t_pt)], inc=vs)
        v_pt = p.val(vs)
        p.op("tensor", lambda e, par=par: e.matmul(
            out=psNV[par][:, :], lhsT=PTsb[par][:, :], rhs=NVsb[par][:, :],
            start=True, stop=True), waits=[(vs, v_pt)], inc=ts)
        t_pnv = p.val(ts)
        p.op("vector", lambda e, par=par: e.tensor_copy(out=PNv[par][:, :],
                                                        in_=psNV[par][:, :]),
             waits=[(ts, t_pnv)], inc=vs)
        v_pnv = p.val(vs)
        for c in range(NCH):
            p.op("tensor", lambda e, c=c, par=par: e.matmul(
                out=psWn[par][:, :], lhsT=PNv[par][:, :], rhs=LT[par][:, c * Lc:(c + 1) * Lc],
                start=True, stop=True),
                waits=[(vs, v_pnv)] if c == 0 else [(vs, p.val(vs))], inc=ts)
            t_y = p.val(ts)
            p.op("vector", lambda e, par=par: e.tensor_copy(out=ytmp[par][:, :],
                                                            in_=psWn[par][:, :]),
                 waits=[(ts, t_y), (prefix + "dy", p.val(prefix + "dy"))], inc=vs)
            v_y = p.val(vs)
            p.op("gpsimd", lambda e, pr=pr, c=c, par=par: e.dma_start(
                out=ystash[pr, :, c * Lc:(c + 1) * Lc], in_=ytmp[par][:, :]),
                waits=[(vs, v_y)], inc=prefix + "dy", inc_amt=16)
            yield
        chain_done[par] = p.val(vs)

    def drain(g, n=10 ** 9):
        for _ in range(n):
            try:
                next(g)
            except StopIteration:
                return True
        return False

    drain(emit_qkv(0))
    prev = None
    for pr in range(1, NPR):
        cur = emit_chain(pr - 1)
        while prev is not None:
            done_p = drain(prev, 2)
            drain(cur, 2)
            if done_p:
                prev = None
        qg = emit_qkv(pr)
        while not drain(qg, 1):
            drain(cur, 2)
        prev = cur
    cur = emit_chain(NPR - 1)
    while prev is not None:
        done_p = drain(prev, 3)
        drain(cur, 1)
        if done_p:
            prev = None
    drain(cur)
    prev_bar = p.barrier()

    dwp = None
    for k in range(ND):
        dwp = p.op("gpsimd", lambda e, k=k: e.dma_start(
            out=w2sb[:, k * 1024:(k + 1) * 1024],
            in_=d["wproj"][k * 128:(k + 1) * 128, :]), inc=prefix + "dwp", inc_amt=16)
    y_done = [(prefix + "dy", p.val(prefix + "dy")), (prefix + "dwp", p.val(prefix + "dwp"))]
    for c in range(NCH):
        sl = slice(c * Lc, (c + 1) * Lc)
        dx = prefix + ("dra" if c % 2 == 0 else "drb")
        dl = None
        for i in range(ND):
            dl = p.op("sync", lambda e, i=i, sl=sl: e.dma_start(
                out=sx[i][:, :], in_=xin_ap(i, sl)),
                waits=prev_bar if i == 0 else (), inc=dx, inc_amt=16)
        for pr in range(NPR):
            dl = p.op("sync", lambda e, pr=pr, sl=sl: e.dma_start(
                out=ypr[pr][:, :], in_=ystash[pr, :, sl]),
                waits=y_done if (c == 0 and pr == 0) else (), inc=dx, inc_amt=16)
        for o in range(ND):
            for pr in range(NPR):
                p.op("tensor", lambda e, o=o, pr=pr: e.matmul(
                    out=psW0[:, :], lhsT=w2sb[:, pr * 1024 + o * 128:pr * 1024 + (o + 1) * 128],
                    rhs=ypr[pr][:, :], start=(pr == 0), stop=(pr == NPR - 1)),
                    waits=[(dx, dl)] if (o == 0 and pr == 0) else (),
                    inc=ts if pr == NPR - 1 else None)
            t_pr = p.val(ts)
            p.op("vector", lambda e, o=o: e.tensor_add(
                out=xo[o][:, :], in0=psW0[:, :], in1=sx[o][:, :]),
                waits=[(ts, t_pr)], inc=vs)
            v_x = p.val(vs)
            p.op("sync", lambda e, o=o, sl=sl: e.dma_start(
                out=xout_ap(o, sl), in_=xo[o][:, :]),
                waits=[(vs, v_x)], inc=prefix + "do", inc_amt=16)
            p.op("tensor", lambda e: None, waits=[(vs, v_x)], real=False)
        prev_bar = p.barrier()
    p.wait("sync", [(prefix + "do", p.val(prefix + "do"))])


def build_att0_launch(in_shapes, T=4096):
    nc = bass.Bass("TRN2", target_bir_lowering=False)
    d = declare_inputs(nc, in_shapes)
    xout = nc.dram_tensor("xout", [ND, 128, T], F32, kind="ExternalOutput").ap()
    p = Plan(nc)
    m = Sbuf(p)
    m.dram = d
    emit_const_loads(p, m)
    emit_att0(p, m, lambda i, sl: d["xin"][i, :, sl],
              lambda o, sl: xout[o, :, sl], T)
    p.build()
    return nc, m



def build_fused():
    nc = bass.Bass("TRN2", target_bir_lowering=False)
    shapes = {
        "xin": np.zeros((8, 8, 128, 512), np.float32),
        "wattn": np.zeros((1024, 3072), np.float32),
        "wproj": np.zeros((1024, 1024), np.float32),
        "ipT": np.zeros((1024, 4096), np.float32),
        "xpT": np.zeros((2048, 128), np.float32),
        "dtT": np.zeros((128, 2048), np.float32),
        "opT": np.zeros((2048, 1024), np.float32),
        "A_logT": np.zeros((128, 256), np.float32),
        "convWT": np.zeros((128, 64), np.float32),
        "convBT": np.zeros((128, 16), np.float32),
        "DT": np.zeros((128, 16), np.float32),
        "dtbT": np.zeros((128, 16), np.float32),
        "gammaT": np.zeros((128, 8), np.float32),
        "betaT": np.zeros((128, 8), np.float32),
        "wgT": np.zeros((1024, 3072), np.int32),
        "wuT": np.zeros((1024, 3072), np.int32),
        "wdT": np.zeros((3072, 1024), np.int32),
        **host_consts(),
    }
    for nm in ("wfT", "wiT", "wcT", "ufT", "uiT", "ucT", "woT", "uoT"):
        shapes[nm] = np.zeros((1024, 1024), np.float32)
    for nm in ("bfT", "biT", "bcT", "boT", "incT"):
        shapes[nm] = np.zeros((128, 8), np.float32)
    d = declare_inputs(nc, shapes)
    xout = nc.dram_tensor("xout", [ND, 128, 4096], F32, kind="ExternalOutput").ap()

    hshape = [8, ND, 128, 512]
    h1 = nc.dram_tensor("fz_h1", hshape, F32, kind="Internal")
    h2 = nc.dram_tensor("fz_h2", hshape, F32, kind="Internal")
    h3 = nc.dram_tensor("fz_h3", hshape, F32, kind="Internal")
    g1 = nc.dram_tensor("fz_g1", [8 * 8 * ND * 128, 512], F32, kind="Internal",
                        addr_space="Shared")
    g2 = nc.dram_tensor("fz_g2", [8 * 8 * ND * 128, 512], F32, kind="Internal",
                        addr_space="Shared")
    g3 = nc.dram_tensor("fz_g3", [8 * 8 * ND * 128, 512], F32, kind="Internal",
                        addr_space="Shared")
    xc_b1 = nc.dram_tensor("fz_xb1", [ND, 128, 4096], F32, kind="Internal").ap()
    xc_s1 = nc.dram_tensor("fz_xs1", hshape, F32, kind="Internal").ap()
    xc_b2 = nc.dram_tensor("fz_xb2", [ND, 128, 4096], F32, kind="Internal").ap()
    xc_b3 = nc.dram_tensor("fz_xb3", [ND, 128, 4096], F32, kind="Internal").ap()
    xc_b4 = nc.dram_tensor("fz_xb4", [ND, 128, 4096], F32, kind="Internal").ap()

    p = Plan(nc)
    m = Sbuf(p)
    m.dram = d
    emit_const_loads(p, m)
    m.push()
    for _j in range(2):
        m.sb(f"f1p_stage{_j}", (128, 1024))

    def phase_end():
        conds = p.barrier()
        for e in ENGINES:
            p.wait(e, conds)
        return conds

    def reshard(tag, h, g, emit_copy, store_sem):
        cc = tag + "cc"
        sw = tag + "sw"
        p.sem(cc); p.sem(sw)
        p.op("gpsimd", lambda e: e.collective_compute(
            "AllGather", ALU.bypass, replica_groups=[list(range(8))],
            ins=[h.ap().rearrange("a b c d -> (a b c) d")],
            outs=[g.ap()]),
            waits=[store_sem], inc=cc, inc_amt=1)
        cc_target = p.val(cc)
        p.counts[sw] += 8 * 16

        def switch_fn(nc_, rank, sems):
            nc_.sync.wait_ge(sems[cc], cc_target)
            for blk in range(8):
                emit_copy(nc_.sync, rank, blk).then_inc(sems[sw], 16)
        p.switch_segment(switch_fn)
        return [(sw, p.val(sw))]

    m.push()
    emit_ffn0(p, m, lambda t, i: d["xin"][t, i, :, :],
              lambda t, k: h1.ap()[t, k, :, :], 512)
    emit_ffn1_prep(p, m)
    m.pop()
    f0_store = ("f0do", p.val("f0do"))
    phase_end()
    g1v = g1.ap().rearrange("(i b d p) l -> i b d p l", i=8, b=8, d=ND)
    xb1v = xc_b1.rearrange("d p (i l) -> i d p l", i=8)
    rw = reshard("r1", h1, g1,
                 lambda eng, rank, blk: eng.dma_start(
                     out=xb1v[blk], in_=g1v[blk, rank]), f0_store)

    def h2_out(o, sl):
        assert sl.start % 512 == 0 and sl.stop - sl.start == 512
        return h2.ap()[sl.start // 512, o, :, :]
    m.push()
    emit_att0(p, m, lambda i, sl: xc_b1[i, :, sl], h2_out, 4096, entry_waits=rw)
    m.pop()
    at_store = ("atdo", p.val("atdo"))
    phase_end()
    g2v = g2.ap().rearrange("(b j d p) l -> b j d p l", b=8, j=8, d=ND)
    rw = reshard("r2", h2, g2,
                 lambda eng, rank, blk: eng.dma_start(
                     out=xc_s1[blk], in_=g2v[blk, rank]), at_store)

    m.push()
    emit_ffn0(p, m, lambda t, i: xc_s1[t, i, :, :],
              lambda t, k: h3.ap()[t, k, :, :], 512, entry_waits=rw)
    m.pop()
    f0_store2 = ("f0do", p.val("f0do"))
    phase_end()
    g3v = g3.ap().rearrange("(i b d p) l -> i b d p l", i=8, b=8, d=ND)
    xb2v = xc_b2.rearrange("d p (i l) -> i d p l", i=8)
    rw = reshard("r3", h3, g3,
                 lambda eng, rank, blk: eng.dma_start(
                     out=xb2v[blk], in_=g3v[blk, rank]), f0_store2)
    m.pop()

    m.push()
    emit_ffn1(p, m, lambda i, sl: xc_b2[i, :, sl],
              lambda o, sl: xc_b3[o, :, sl], 4096, 512, entry_waits=rw)
    m.pop()
    phase_end()

    m.push()
    emit_att1(p, m, lambda i, sl: xc_b3[i, :, sl],
              lambda o, sl: xc_b4[o, :, sl], 4096, 512)
    m.pop()
    phase_end()

    m.push()
    emit_ffn1(p, m, lambda i, sl: xc_b4[i, :, sl],
              lambda o, sl: xout[o, :, sl], 4096, 512)
    m.pop()
    p.wait("sync", [("f1d", p.val("f1d"))])
    p.build()
    return nc, m



from concourse.bass_utils import run_bass_kernel_spmd

_FUSED_CACHE = {}


def kernel(**inputs):
    z = inputs
    if "prog" not in _FUSED_CACHE:
        _FUSED_CACHE["prog"] = build_fused()
    nc, m = _FUSED_CACHE["prog"]

    ins = {}
    for nm in ("wf", "wi", "wc", "uf", "ui", "uc", "wo", "uo"):
        ins[nm + "T"] = np.ascontiguousarray(z["L0_ffn_" + nm].T)
    for nm, src in (("bfT", "bf"), ("biT", "bi"), ("bcT", "bc"), ("boT", "bo"),
                    ("incT", "in_c")):
        ins[nm] = np.ascontiguousarray(z["L0_ffn_" + src].reshape(8, 128).T)
    ins["wattn"] = z["L0_attn_W_attn"]
    ins["wproj"] = z["L0_attn_W_proj"]
    g = {k[len("L1_attn_"):]: z[k] for k in z if k.startswith("L1_attn_")}
    ins["ipT"] = np.ascontiguousarray(g["in_proj_w"].T)
    xpT = np.zeros((2048, 128), np.float32); xpT[:, :96] = g["x_proj_w"].T
    ins["xpT"] = xpT
    dtT = np.zeros((128, 2048), np.float32); dtT[:64, :] = g["dt_proj_w"].T
    ins["dtT"] = dtT
    ins["opT"] = np.ascontiguousarray(g["out_proj_w"].T)
    ins["A_logT"] = np.ascontiguousarray(
        g["A_log"].reshape(16, 128, 16).transpose(1, 0, 2).reshape(128, 256))
    ins["convWT"] = np.ascontiguousarray(
        g["conv_w"][:, 0, :].reshape(16, 128, 4).transpose(1, 0, 2).reshape(128, 64))
    ins["convBT"] = np.ascontiguousarray(g["conv_b"].reshape(16, 128).T)
    ins["DT"] = np.ascontiguousarray(g["D"].reshape(16, 128).T)
    ins["dtbT"] = np.ascontiguousarray(g["dt_proj_b"].reshape(16, 128).T)
    ins["gammaT"] = np.ascontiguousarray(g["ln_gamma"].reshape(8, 128).T)
    ins["betaT"] = np.ascontiguousarray(g["ln_beta"].reshape(8, 128).T)
    ins["wgT"] = np.ascontiguousarray(z["L1_ffn_w_gate"].T)
    ins["wuT"] = np.ascontiguousarray(z["L1_ffn_w_up"].T)
    ins["wdT"] = np.ascontiguousarray(z["L1_ffn_w_down"].T)
    ins.update(host_consts(m.const_vals))

    xfm = np.ascontiguousarray(z["L0_attn_x"].transpose(0, 2, 1))
    in_maps = []
    for i in range(8):
        xs = np.ascontiguousarray(xfm[:, :, i * 512:(i + 1) * 512])
        in_maps.append({**ins, "xin": xs.reshape(8, 8, 128, 512)})
    res = run_bass_kernel_spmd(nc, in_maps, core_ids=list(range(8)))
    out = np.empty((8, 1024, 4096), np.float32)
    for b in range(8):
        out[b] = np.asarray(res.results[b]["xout"]).reshape(1024, 4096)
    return np.ascontiguousarray(out.transpose(0, 2, 1))


# version 16 (final)
# speedup vs baseline: 1.0243x; 1.0091x over previous
import numpy as np
from contextlib import ExitStack

import concourse.bass as bass
import concourse.mybir as mybir

F32 = mybir.dt.float32
BF16 = mybir.dt.bfloat16
I32 = mybir.dt.int32
AF = mybir.ActivationFunctionType
ALU = mybir.AluOpType
AX = mybir.AxisListType.X

D = 1024
ND = D // 128
LN_EPS = 1e-5
ENGINES = ("sync", "vector", "scalar", "tensor", "gpsimd")


class Plan:
    def __init__(self, nc):
        self.nc = nc
        self.ops = {e: [] for e in ENGINES}
        self.sems = {}
        self.counts = {}
        self.es = ExitStack()
        self.bar = {e: 0 for e in ENGINES}
        self._since_bar = {e: False for e in ENGINES}
        self._last = {e: None for e in ENGINES}
        self._last_inc = {e: None for e in ENGINES}

    def sem(self, name):
        if name not in self.sems:
            self.sems[name] = self.es.enter_context(self.nc.semaphore(name))
            self.counts[name] = 0
        return self.sems[name]

    def op(self, engine, fn, waits=(), inc=None, inc_amt=1, dma=None, real=True):
        for w, _ in waits:
            self.sem(w)
        if dma is None:
            dma = (inc_amt == 16)
        waits = list(waits)
        if not dma and real and self._last_inc[engine] is not None:
            se = self._last_inc[engine]
            if se not in waits:
                waits.append(se)
        if inc is not None:
            self.sem(inc)
            self.counts[inc] += inc_amt
        rec = {"fn": fn, "waits": waits, "dma": dma,
               "incs": [] if inc is None else [(inc, inc_amt)]}
        if inc is not None and not dma and real:
            self._last_inc[engine] = (inc, self.counts[inc])
        self.ops[engine].append(rec)
        if real:
            self._since_bar[engine] = True
            self._last[engine] = rec
        return self.counts[inc] if inc is not None else None

    def val(self, name):
        return self.counts.get(name, 0)

    def barrier(self):
        conds = dict(getattr(self, "_bar_conds", {}))
        for e in ENGINES:
            if self._since_bar[e]:
                rec = self._last[e]
                if rec["incs"]:
                    s, _ = rec["incs"][0]
                    conds[e] = (s, self.counts[s])
                else:
                    amt = 16 if rec["dma"] else 1
                    self.sem("bar_" + e)
                    self.counts["bar_" + e] += amt
                    rec["incs"].append(("bar_" + e, amt))
                    conds[e] = ("bar_" + e, self.counts["bar_" + e])
                self._since_bar[e] = False
        self._bar_conds = conds
        return list(conds.values())

    def wait(self, engine, waits):
        if waits:
            self.op(engine, lambda eng: None, waits=waits, real=False)

    def switch_segment(self, fn):
        if not hasattr(self, "segments"):
            self.segments = []
        self.segments.append(("block", self.ops))
        self.ops = {e: [] for e in ENGINES}
        self.segments.append(("switch", fn))

    def build(self):
        nc = self.nc
        sems = self.sems
        segs = getattr(self, "segments", [])
        segs = segs + [("block", self.ops)]

        def runner(ops, engine):
            def run(eng):
                for rec in ops[engine]:
                    for w, v in rec["waits"]:
                        eng.wait_ge(sems[w], v)
                    ins = rec["fn"](eng)
                    if rec["incs"]:
                        assert ins is not None, f"{engine} op with inc returned None"
                        for s, a in rec["incs"]:
                            ins.then_inc(sems[s], a)
            return run

        for kind, payload in segs:
            if kind == "block":
                if not any(payload[e] for e in ENGINES):
                    continue
                with nc.Block() as blk:
                    blk.sync(runner(payload, "sync"))
                    blk.vector(runner(payload, "vector"))
                    blk.scalar(runner(payload, "scalar"))
                    blk.tensor(runner(payload, "tensor"))
                    blk.gpsimd(runner(payload, "gpsimd"))
            else:
                for arm in nc.switch_core_id(engines=[nc.sync], n=128):
                    payload(nc, arm.logical % 8, sems)
        self.es.close()



def host_consts(const_vals=()):
    c = np.zeros((128, NCONST), dtype=np.float32)
    for j, v in enumerate(const_vals):
        c[:, j] = v
    sel0 = np.zeros((128, 128), dtype=np.float32)
    sel0[0, :] = 1.0
    sel1 = np.zeros((128, 128), dtype=np.float32)
    sel1[1, :] = 1.0
    onesblk = np.zeros((128, 128), dtype=np.float32)
    onesblk[:64, :64] = 1.0
    onesblk[64:, 64:] = 1.0
    return {
        "c_eye": np.eye(128, dtype=np.float32),
        "c_onesblk": onesblk,
        "c_ones128": np.ones((128, 128), dtype=np.float32),
        "c_sel0": sel0,
        "c_sel1": sel1,
        "c_scal": c,
    }


def declare_inputs(nc, arrs):
    out = {}
    for k, v in arrs.items():
        dt = I32 if v.dtype == np.int32 else F32
        out[k] = nc.dram_tensor(k, list(v.shape), dt, kind="ExternalInput").ap()
    return out


NCONST = 32


class Sbuf:

    def __init__(self, p):
        self.p = p
        self.t = {}
        self.const_vals = []
        self._scopes = []

    def push(self):
        self._scopes.append((ExitStack(), []))

    def alias(self, name, ap):
        self.t[name] = ap
        if self._scopes:
            self._scopes[-1][1].append(name)

    def once(self, key):
        if key in self.t:
            return False
        self.alias(key, True)
        return True

    def pop(self):
        es, names = self._scopes.pop()
        for n in names:
            del self.t[n]
        es.close()

    def const(self, val, parts=128):
        val = float(val)
        tile = self.sb("k_scal", (128, NCONST))
        if val not in self.const_vals:
            assert len(self.const_vals) < NCONST
            self.const_vals.append(val)
        j = self.const_vals.index(val)
        return tile[0:parts, j:j + 1]

    def _alloc(self, name, cm):
        if self._scopes:
            es, names = self._scopes[-1]
            self.t[name] = es.enter_context(cm)
            names.append(name)
        else:
            self.t[name] = self.p.es.enter_context(cm)
        return self.t[name]

    def _uname(self, name):
        self._uctr = getattr(self, "_uctr", 0) + 1
        return name if not self._scopes else f"u{self._uctr}_{name}"

    def sb(self, name, shape, dtype=F32):
        if name not in self.t:
            self._alloc(name, self.p.nc.sbuf_tensor(self._uname(name), list(shape), dtype))
        return self.t[name]

    def ps(self, name, shape, dtype=F32):
        if name not in self.t:
            self._alloc(name, self.p.nc.psum_tensor(self._uname(name), list(shape), dtype))
        return self.t[name]



def emit_ln(p, m, sx, xln, Lc, tag, dwaits=(), gamma=None, beta=None, eps2=None,
            vsem="v", ssem="s", tsem="t"):
    psA = m.ps("psA", (128, Lc))
    psB = m.ps("psB", (128, Lc))
    psR = m.ps("psR", (128, Lc))
    xbf = [m.sb(f"ln_xbf{i}", (128, Lc), BF16) for i in range(ND)]
    xsq = [m.sb(f"ln_xsq{i}", (128, Lc), BF16) for i in range(ND)]
    rowr = m.sb("ln_rowr", (128, Lc), BF16)
    e2row = m.sb("ln_e2row", (1, Lc))
    m2row = m.sb("ln_m2row", (1, Lc))
    varrow = m.sb("ln_varrow", (1, Lc))
    tln = xbf
    ones128 = m.sb("k_ones128_bf", (128, 128), BF16)
    sel0 = m.sb("k_sel0_bf", (128, 128), BF16)

    if m.once("ln_rows_zeroed"):
        p.op("vector", lambda e: e.memset(rowr[:, :], 0.0), inc=vsem)

    for i in range(ND):
        p.op("vector", lambda e, i=i: e.tensor_copy(out=xbf[i][:, :], in_=sx[i]),
             waits=dwaits if i == 0 else (), inc=vsem)
    for i in range(ND):
        p.op("scalar", lambda e, i=i: e.activation(out=xsq[i][:, :], in_=sx[i],
                                                   func=AF.Square),
             waits=dwaits if i == 0 else (), inc=ssem)
    vcopy, scopy = p.val(vsem), p.val(ssem)
    for i in range(ND):
        p.op("tensor", lambda e, i=i: e.matmul(out=psA[:, :], lhsT=ones128[:, :],
                                               rhs=xbf[i][:, :], start=(i == 0),
                                               stop=(i == ND - 1)),
             waits=[(vsem, vcopy)] if i == 0 else ())
    for i in range(ND):
        p.op("tensor", lambda e, i=i: e.matmul(out=psB[:, :], lhsT=ones128[:, :],
                                               rhs=xsq[i][:, :], start=(i == 0),
                                               stop=(i == ND - 1)),
             waits=[(ssem, scopy)] if i == 0 else (),
             inc=tsem if i == ND - 1 else None)
    t_stats = p.val(tsem)
    p.op("scalar", lambda e: e.activation(out=e2row[:, :], in_=psB[0:1, :],
                                          func=AF.Copy, scale=1.0 / D),
         waits=[(tsem, t_stats)], inc=ssem)
    p.op("scalar", lambda e: e.activation(out=m2row[:, :], in_=psA[0:1, :],
                                          func=AF.Square, scale=1.0 / D), inc=ssem)
    s_rows = p.val(ssem)
    p.op("vector", lambda e: e.tensor_sub(out=varrow[:, :], in0=e2row[:, :],
                                          in1=m2row[:, :]),
         waits=[(ssem, s_rows)], inc=vsem)
    if eps2 is not None:
        p.op("vector", lambda e: e.scalar_tensor_tensor(
            out=varrow[:, :], in0=varrow[:, :], scalar=float(eps2),
            in1=varrow[:, :], op0=ALU.mult, op1=ALU.add),
            waits=[(vsem, p.val(vsem))], inc=vsem)
        eps_b = float(LN_EPS * eps2)
    else:
        eps_b = float(LN_EPS)
    v_var = p.val(vsem)
    eps_ap = m.const(eps_b, parts=1)
    p.op("scalar", lambda e: e.activation(out=varrow[:, :], in_=varrow[:, :],
                                          func=AF.Ln, bias=eps_ap),
         waits=[(vsem, v_var)], inc=ssem)
    p.op("scalar", lambda e: e.activation(out=varrow[:, :], in_=varrow[:, :],
                                          func=AF.Exp, scale=-0.5), inc=ssem)
    s_rstd = p.val(ssem)
    p.op("vector", lambda e: e.tensor_scalar(out=rowr[0:1, :], in0=varrow[:, :],
                                             scalar1=-1.0, scalar2=None,
                                             op0=ALU.mult),
         waits=[(ssem, s_rstd)], inc=vsem)
    v_neg = p.val(vsem)
    p.op("tensor", lambda e: e.matmul(out=psR[:, :], lhsT=sel0[:, :],
                                      rhs=rowr[:, :], start=True, stop=True),
         waits=[(vsem, v_neg)], inc=tsem)
    t_rep = p.val(tsem)
    for i in range(ND):
        p.op("vector", lambda e, i=i: e.scalar_tensor_tensor(
            out=tln[i][:, :], in0=psA[:, :], scalar=1.0 / D, in1=sx[i],
            op0=ALU.mult, op1=ALU.subtract),
             waits=[(tsem, t_rep)] if i == 0 else (), inc=vsem)
        if gamma is None:
            p.op("vector", lambda e, i=i: e.tensor_mul(out=xln[i][:, :],
                                                       in0=tln[i][:, :], in1=psR[:, :]),
                 waits=[(vsem, p.val(vsem))], inc=vsem)
        else:
            p.op("vector", lambda e, i=i: e.tensor_mul(out=tln[i][:, :],
                                                       in0=tln[i][:, :], in1=psR[:, :]),
                 waits=[(vsem, p.val(vsem))], inc=vsem)
    if gamma is not None:
        for i in range(ND):
            p.op("vector", lambda e, i=i: e.scalar_tensor_tensor(
                out=xln[i][:, :], in0=tln[i][:, :], scalar=gamma[i],
                in1=beta[i].broadcast_to([128, Lc]), op0=ALU.mult, op1=ALU.add),
                inc=vsem)
    return (("vector", vsem, p.val(vsem)), ("scalar", ssem, p.val(ssem)))



def emit_ffn1_prep(p, m):
    nc = p.nc
    dram = m.dram
    p._wd_bf = nc.dram_tensor("f1_wd_bf", [24, 128, 1024], BF16,
                              kind="Internal").ap()
    p._wu_bf = nc.dram_tensor("f1_wu_bf", [8, 128, 3072], BF16,
                              kind="Internal").ap()
    stg = [m.sb(f"f1p_stage{j}", (128, 1024)) for j in range(2)]
    sdone = [0, 0]
    jobs = ([("wdT", k, 0, p._wd_bf[k]) for k in range(24)] +
            [("wuT", k, j, p._wu_bf[k][:, j * 1024:(j + 1) * 1024])
             for k in range(ND) for j in range(3)])
    for i, (nm, k, j, dst) in enumerate(jobs):
        par = i % 2
        dgot = p.op("gpsimd", lambda e, nm=nm, k=k, j=j, par=par: e.dma_start(
            out=stg[par][:, :],
            in_=dram[nm][k * 128:(k + 1) * 128, j * 1024:(j + 1) * 1024]),
            waits=[("f1ps", sdone[par])] if sdone[par] else (),
            inc="f1pd", inc_amt=16)
        p.op("gpsimd", lambda e, dst=dst, par=par: e.dma_start(
            out=dst, in_=stg[par][:, :]),
            waits=[("f1pd", dgot)], inc="f1ps", inc_amt=16)
        sdone[par] = p.val("f1ps")
    p._wd_ready = ("f1ps", p.val("f1ps"))


def emit_ffn1(p, m, xin_ap, xout_ap, L, Lc, prefix="f1", dma_eng="sync", entry_waits=()):
    nc = p.nc
    HT = 3072 // 128
    dram = m.dram
    wg = m.sb(prefix + "wg", (128, 8 * 3072), BF16)
    wuW = [m.sb(f"{prefix}wuW{j}", (128, ND * 128), BF16) for j in range(2)]
    wdW = [m.sb(f"{prefix}wdW{j}", (128, HT * 128), BF16) for j in range(2)]
    hh = m.sb(prefix + "hh", (128, HT * Lc), BF16)
    stage = hh[:, :].bitcast(F32)[:, 0:3072]
    wdtmp = hh[:, 6144:7168]
    sx = [[m.sb(f"{prefix}sx{par}_{i}", (128, Lc)) for i in range(ND)]
          for par in range(2)]
    xln = [[m.sb(f"{prefix}xln{par}_{i}", (128, Lc), BF16) for i in range(ND)]
           for par in range(2)]
    sg = [m.sb(f"{prefix}sg{j}", (128, Lc), BF16) for j in range(2)]
    xo = [m.sb(f"{prefix}xo{i}", (128, Lc)) for i in range(ND)]
    psG = [m.ps(f"{prefix}psG{j}", (128, Lc)) for j in range(2)]
    psU = [m.ps(f"{prefix}psU{j}", (128, Lc)) for j in range(2)]

    ds, vs, ss, ts = prefix + "d", prefix + "v", prefix + "s", prefix + "t"
    dsg = prefix + "dg"

    for name, dst in (("wgT", wg),):
        for k in range(ND):
            dgot = p.op("gpsimd", lambda e, name=name, k=k: e.dma_start(
                out=stage[:, :], in_=dram[name][k * 128:(k + 1) * 128, :]),
                waits=[(vs, p.val(vs))], inc=dsg, inc_amt=16)
            p.op("vector", lambda e, dst=dst, k=k: e.tensor_copy(
                out=dst[:, k * 3072:(k + 1) * 3072], in_=stage[:, :]),
                waits=[(dsg, dgot)], inc=vs)
    if not hasattr(p, "_wd_bf"):
        emit_ffn1_prep(p, m)
    wd_bf = p._wd_bf
    wu_bf = p._wu_bf
    wd_ready = p._wd_ready
    p.barrier()

    nchunks = L // Lc
    ln_done = {}
    resid_done = [0, 0]

    def stage_ln(c):
        cp = c % 2
        sl = slice(c * Lc, (c + 1) * Lc)
        dx = prefix + ("dxa" if cp == 0 else "dxb")
        dgot = None
        for i in range(ND):
            w0 = (([(vs, resid_done[cp])] if resid_done[cp] else []) +
                  (list(entry_waits) if c == 0 else [])) if i == 0 else ()
            dgot = p.op(dma_eng, lambda e, i=i, sl=sl, cp=cp: e.dma_start(
                out=sx[cp][i][:, :], in_=xin_ap(i, sl)),
                waits=w0, inc=dx, inc_amt=16)
        emit_ln(p, m, [sx[cp][i][:, :] for i in range(ND)], xln[cp], Lc, tag=prefix,
                dwaits=[(dx, dgot)], vsem=vs, ssem=ss, tsem=ts)
        ln_done[c] = (p.val(vs), p.val(ss))

    stage_ln(0)
    for c in range(nchunks):
        cp = c % 2
        sl = slice(c * Lc, (c + 1) * Lc)
        v_ln, s_ln = ln_done[c]
        guard = [None, None]
        wul = [0, 0]
        for h in range(HT):
            par = h % 2
            du_ = prefix + ("dua" if par == 0 else "dub")
            dwu = p.op("gpsimd", lambda e, h=h, par=par: e.dma_start(
                out=wuW[par][:, :].rearrange("p (k j) -> p k j", j=128),
                in_=wu_bf[:, :, h * 128:(h + 1) * 128].rearrange("k p j -> p k j")),
                waits=([(ts, wul[par])] if wul[par] else [wd_ready]),
                inc=du_, inc_amt=16)
            w0 = [(vs, v_ln), (ss, s_ln)] if h == 0 else (
                [(vs, guard[par])] if guard[par] else ())
            for k in range(ND):
                p.op("tensor", lambda e, h=h, k=k, par=par, cp=cp: e.matmul(
                    out=psG[par][:, :],
                    lhsT=wg[:, k * 3072 + h * 128:k * 3072 + (h + 1) * 128],
                    rhs=xln[cp][k][:, :], start=(k == 0), stop=(k == ND - 1)),
                    waits=w0 if k == 0 else ())
            for k in range(ND):
                p.op("tensor", lambda e, h=h, k=k, par=par, cp=cp: e.matmul(
                    out=psU[par][:, :],
                    lhsT=wuW[par][:, k * 128:(k + 1) * 128],
                    rhs=xln[cp][k][:, :], start=(k == 0), stop=(k == ND - 1)),
                    waits=[(du_, dwu)] if k == 0 else (),
                    inc=ts if k == ND - 1 else None)
            t_gu = p.val(ts)
            wul[par] = t_gu
            p.op("scalar", lambda e, h=h, par=par: e.activation(
                out=sg[par][:, :], in_=psG[par][:, :], func=AF.Silu,
                scale=1.0 / 127.0), waits=[(ts, t_gu)], inc=ss)
            s_silu = p.val(ss)
            p.op("vector", lambda e, h=h, par=par: e.scalar_tensor_tensor(
                out=hh[:, h * Lc:(h + 1) * Lc], in0=psU[par][:, :],
                scalar=1.0 / 127.0, in1=sg[par][:, :], op0=ALU.mult, op1=ALU.mult),
                waits=[(ss, s_silu)], inc=vs)
            guard[par] = p.val(vs)
            if h == 1 and c + 1 < nchunks:
                stage_ln(c + 1)
        v_hh_all = p.val(vs)
        psD = [m.t["psA"], m.t["psB"]]
        dguard = [None, None]
        dwl = [0, 0]
        for o in range(ND):
            par = o % 2
            dd = prefix + ("dda" if par == 0 else "ddb")
            dw = p.op("gpsimd", lambda e, o=o, par=par: e.dma_start(
                out=wdW[par][:, :].rearrange("p (k j) -> p k j", j=128),
                in_=wd_bf[:, :, o * 128:(o + 1) * 128].rearrange("k p j -> p k j")),
                waits=([(ts, dwl[par])] if dwl[par] else
                       ([wd_ready] if o < 2 else [])), inc=dd, inc_amt=16)
            w0 = [(dd, dw), (vs, v_hh_all)] + (
                [(vs, dguard[par])] if dguard[par] else [])
            for k in range(HT):
                p.op("tensor", lambda e, o=o, k=k, par=par: e.matmul(
                    out=psD[par][:, :],
                    lhsT=wdW[par][:, k * 128:(k + 1) * 128],
                    rhs=hh[:, k * Lc:(k + 1) * Lc], start=(k == 0),
                    stop=(k == HT - 1)),
                    waits=w0 if k == 0 else (), inc=ts if k == HT - 1 else None)
            t_d = p.val(ts)
            dwl[par] = t_d
            p.op("vector", lambda e, o=o, par=par, cp=cp: e.scalar_tensor_tensor(
                out=xo[o][:, :], in0=psD[par][:, :], scalar=0.5 / 127.0,
                in1=sx[cp][o][:, :], op0=ALU.mult, op1=ALU.add),
                waits=[(ts, t_d)], inc=vs)
            dguard[par] = p.val(vs)
            v_o = p.val(vs)
            p.op(dma_eng, lambda e, o=o, sl=sl: e.dma_start(
                out=xout_ap(o, sl), in_=xo[o][:, :]),
                waits=[(vs, v_o)], inc=ds, inc_amt=16)
        resid_done[cp] = p.val(vs)
    p.wait(dma_eng, [(ds, p.val(ds))])


def emit_const_loads(p, m):
    d = m.dram
    oc = m.sb("k_ones128_bf", (128, 128), BF16)
    sel0 = m.sb("k_sel0_bf", (128, 128), BF16)
    sel1 = m.sb("k_sel1_bf", (128, 128), BF16)
    eye = m.sb("k_eye_bf", (128, 128), BF16)
    onesblk = m.sb("k_onesblk_bf", (128, 128), BF16)
    scal = m.sb("k_scal", (128, NCONST))
    p.op("gpsimd", lambda e: e.dma_start(out=oc[:, :], in_=d["c_ones128"]),
         inc="cd_g", inc_amt=16)
    p.op("gpsimd", lambda e: e.dma_start(out=sel0[:, :], in_=d["c_sel0"]),
         inc="cd_g", inc_amt=16)
    p.op("gpsimd", lambda e: e.dma_start(out=sel1[:, :], in_=d["c_sel1"]),
         inc="cd_g", inc_amt=16)
    p.op("gpsimd", lambda e: e.dma_start(out=eye[:, :], in_=d["c_eye"]),
         inc="cd_g", inc_amt=16)
    p.op("gpsimd", lambda e: e.dma_start(out=onesblk[:, :], in_=d["c_onesblk"]),
         inc="cd_g", inc_amt=16)
    p.op("sync", lambda e: e.dma_start(out=scal[:, :], in_=d["c_scal"]),
         inc="cd_s", inc_amt=16)
    w = [("cd_g", p.val("cd_g")), ("cd_s", p.val("cd_s"))]
    p.wait("tensor", w)
    p.wait("vector", w)
    p.wait("scalar", w)


def build_ffn1_launch(in_shapes, L=4096, Lc=512):
    nc = bass.Bass("TRN2", target_bir_lowering=False)
    d = declare_inputs(nc, in_shapes)
    xout = nc.dram_tensor("xout", [ND, 128, L], F32, kind="ExternalOutput").ap()
    p = Plan(nc)
    m = Sbuf(p)
    m.dram = d
    emit_const_loads(p, m)
    emit_ffn1(p, m, lambda i, sl: d["xin"][i, :, sl],
              lambda o, sl: xout[o, :, sl], L, Lc)
    p.build()
    return nc, m


def build_ln_launch(in_shapes, L=4096, Lc=256):
    nc = bass.Bass("TRN2", target_bir_lowering=False)
    d = declare_inputs(nc, in_shapes)
    xout = nc.dram_tensor("xout", [ND, 128, L], BF16, kind="ExternalOutput").ap()
    p = Plan(nc)
    m = Sbuf(p)
    m.dram = d
    emit_const_loads(p, m)
    sx = [m.sb(f"dsx{i}", (128, Lc)) for i in range(ND)]
    xln = [m.sb(f"dxln{i}", (128, Lc), BF16) for i in range(ND)]
    prev_bar = ()
    for c in range(L // Lc):
        sl = slice(c * Lc, (c + 1) * Lc)
        dgot = None
        for i in range(ND):
            dgot = p.op("sync", lambda e, i=i, sl=sl: e.dma_start(
                out=sx[i][:, :], in_=d["xin"][i, :, sl]),
                waits=prev_bar if i == 0 else (), inc="d", inc_amt=16)
        emit_ln(p, m, [sx[i][:, :] for i in range(ND)], xln, Lc, tag="ln",
                dwaits=[("d", dgot)], vsem="v", ssem="s", tsem="t")
        v_ln = p.val("v")
        for i in range(ND):
            p.op("sync", lambda e, i=i, sl=sl: e.dma_start(
                out=xout[i, :, sl], in_=xln[i][:, :]),
                waits=[("v", v_ln)] if i == 0 else (), inc="d", inc_amt=16)
        prev_bar = p.barrier()
    p.wait("sync", [("d", p.val("d"))])
    p.build()
    return nc, m



def emit_ffn0(p, m, xin_ap, xout_ap, Ll, prefix="f0", entry_waits=()):
    nc = p.nc
    d = m.dram
    T = 8
    WA = m.sb(prefix + "WA", (128, 6 * ND * 1024), BF16)
    woff = {"wfT": 0, "wiT": 1, "wcT": 2, "ufT": 3, "uiT": 4, "ucT": 5}

    def wtile(name, k, o):
        base = woff[name] * ND * 1024 + k * 1024
        return WA[:, base + o * 128: base + (o + 1) * 128]

    bias = {nm: m.sb(prefix + nm, (128, ND)) for nm in ("bfT", "biT", "bcT", "boT", "incT")}
    psF = [m.ps(f"{prefix}psF{j}", (128, Ll)) for j in range(2)]
    psI = [m.ps(f"{prefix}psI{j}", (128, Ll)) for j in range(2)]
    psC = [m.ps(f"{prefix}psC{j}", (128, Ll)) for j in range(2)]
    psO = m.ps(prefix + "psO", (128, Ll))
    m.alias("psA", psF[0])
    m.alias("psB", psI[0])
    m.alias("psR", psC[0])
    sx = [m.sb(f"{prefix}sx{i}", (128, Ll)) for i in range(ND)]
    xln = [m.sb(f"{prefix}xln{i}", (128, Ll), BF16) for i in range(ND)]
    cst = [m.sb(f"{prefix}c{i}", (128, Ll)) for i in range(ND)]
    cbf = [[m.sb(f"{prefix}cb{par}_{i}", (128, Ll), BF16) for i in range(ND)]
           for par in range(2)]
    gF = [m.sb(f"{prefix}gF{j}", (128, Ll), BF16) for j in range(2)]
    gI = [m.sb(f"{prefix}gI{j}", (128, Ll), BF16) for j in range(2)]
    gC = [m.sb(f"{prefix}gC{j}", (128, Ll), BF16) for j in range(2)]
    t1b = m.sb(prefix + "t1", (128, Ll), BF16)
    t2f = m.sb(prefix + "t2", (128, Ll))
    xstash = nc.dram_tensor(f"{prefix}_xstash{sum(len(v) for v in p.ops.values())}",
                            [T, ND, 128, Ll], BF16, kind="Internal").ap()

    ds, dsg, vs, ss, ts = (prefix + "d", prefix + "dg", prefix + "v",
                           prefix + "s", prefix + "t")

    for nm in woff:
        for k in range(ND):
            p.op("gpsimd", lambda e, nm=nm, k=k: e.dma_start(
                out=WA[:, woff[nm] * ND * 1024 + k * 1024:
                       woff[nm] * ND * 1024 + (k + 1) * 1024],
                in_=d[nm][k * 128:(k + 1) * 128, :]), inc=dsg, inc_amt=16)
    for nm in bias:
        p.op("sync", lambda e, nm=nm: e.dma_start(out=bias[nm][:, :], in_=d[nm]),
             inc=prefix + "db", inc_amt=16)
    w_weights = [(dsg, p.val(dsg)), (prefix + "db", p.val(prefix + "db"))]

    for k in range(ND):
        p.op("vector", lambda e, k=k: e.tensor_scalar(
            out=cst[k][:, :], in0=bias["incT"][:, k:k + 1].broadcast_to([128, Ll]),
            scalar1=1.0, scalar2=None, op0=ALU.mult),
            waits=w_weights if k == 0 else (), inc=vs)
        p.op("vector", lambda e, k=k: e.tensor_copy(out=cbf[0][k][:, :], in_=cst[k][:, :]),
             inc=vs)

    prev_bar = tuple(entry_waits)
    for t in range(T):
        dx = prefix + ("dxa" if t % 2 == 0 else "dxb")
        dgot = None
        for i in range(ND):
            dgot = p.op("sync", lambda e, t=t, i=i: e.dma_start(
                out=sx[i][:, :], in_=xin_ap(t, i)),
                waits=prev_bar if i == 0 else (), inc=dx, inc_amt=16)
        emit_ln(p, m, [sx[i][:, :] for i in range(ND)], xln, Ll, tag=prefix,
                dwaits=[(dx, dgot)], vsem=vs, ssem=ss, tsem=ts)
        v_ln = p.val(vs)
        for i in range(ND):
            p.op("sync", lambda e, t=t, i=i: e.dma_start(
                out=xstash[t, i, :, :], in_=xln[i][:, :]),
                waits=[(vs, v_ln)] if i == 0 else (), inc=prefix + "dst", inc_amt=16)
        prev_bar = p.barrier()

    xt = [m.sb(f"{prefix}xt{i}", (128, Ll), BF16) for i in range(ND)]
    stash_done = [(prefix + "dst", p.val(prefix + "dst"))]
    for t in range(T):
        dx = prefix + ("dsa" if t % 2 == 0 else "dsb")
        dgot = None
        for i in range(ND):
            dgot = p.op("sync", lambda e, t=t, i=i: e.dma_start(
                out=xt[i][:, :], in_=xstash[t, i, :, :]),
                waits=(list(prev_bar) + stash_done) if i == 0 else (),
                inc=dx, inc_amt=16)
        v_prev_c = p.val(vs)
        guard = [None, None]
        for k in range(ND):
            par = k % 2
            if guard[par] is not None:
                p.op("tensor", lambda e: None, waits=[(vs, guard[par])], real=False)
            for ps, wname, uname in ((psF[par], "wfT", "ufT"), (psI[par], "wiT", "uiT"),
                                     (psC[par], "wcT", "ucT")):
                for kk in range(ND):
                    p.op("tensor", lambda e, ps=ps, wname=wname, k=k, kk=kk: e.matmul(
                        out=ps[:, :], lhsT=wtile(wname, kk, k), rhs=xt[kk][:, :],
                        start=(kk == 0), stop=False),
                        waits=[(dx, dgot), (vs, v_prev_c)] if (k == 0 and kk == 0 and ps is psF[par]) else ())
                for kk in range(ND):
                    p.op("tensor", lambda e, ps=ps, uname=uname, k=k, kk=kk, t=t: e.matmul(
                        out=ps[:, :], lhsT=wtile(uname, kk, k), rhs=cbf[t % 2][kk][:, :],
                        start=False, stop=(kk == ND - 1)),
                        inc=ts if (kk == ND - 1 and ps is psC[par]) else None)
            t_g = p.val(ts)
            p.op("scalar", lambda e, k=k, par=par: e.activation(
                out=gF[par][:, :], in_=psF[par][:, :], func=AF.Sigmoid,
                bias=bias["bfT"][:, k:k + 1]), waits=[(ts, t_g)], inc=ss)
            p.op("scalar", lambda e, k=k, par=par: e.activation(
                out=gI[par][:, :], in_=psI[par][:, :], func=AF.Sigmoid,
                bias=bias["biT"][:, k:k + 1]), inc=ss)
            p.op("scalar", lambda e, k=k, par=par: e.activation(
                out=gC[par][:, :], in_=psC[par][:, :], func=AF.Tanh,
                bias=bias["bcT"][:, k:k + 1]), inc=ss)
            s_g = p.val(ss)
            p.op("vector", lambda e, k=k, par=par: e.tensor_mul(
                out=t1b[:, :], in0=gI[par][:, :], in1=gC[par][:, :]),
                waits=[(ss, s_g)], inc=vs)
            p.op("vector", lambda e, k=k, par=par: e.tensor_mul(
                out=t2f[:, :], in0=gF[par][:, :], in1=cst[k][:, :]), inc=vs)
            p.op("vector", lambda e, k=k: e.tensor_add(out=cst[k][:, :], in0=t2f[:, :],
                                                       in1=t1b[:, :]), inc=vs)
            p.op("vector", lambda e, k=k, t=t: e.tensor_copy(
                out=cbf[(t + 1) % 2][k][:, :], in_=cst[k][:, :]), inc=vs)
            guard[par] = p.val(vs)
        prev_bar = p.barrier()

    steps_done = p.barrier()
    for nm, off in (("woT", 0), ("uoT", 1)):
        for k in range(ND):
            p.op("gpsimd", lambda e, nm=nm, off=off, k=k: e.dma_start(
                out=WA[:, off * ND * 1024 + k * 1024:off * ND * 1024 + (k + 1) * 1024],
                in_=d[nm][k * 128:(k + 1) * 128, :]),
                waits=steps_done if (nm == "woT" and k == 0) else (),
                inc=dsg, inc_amt=16)
    w_wo = [(dsg, p.val(dsg))]

    def wotile(off, k, o):
        base = off * ND * 1024 + k * 1024
        return WA[:, base + o * 128: base + (o + 1) * 128]

    xo = cst
    for t in range(T):
        dx = prefix + ("doa" if t % 2 == 0 else "dob")
        dgx = None
        for i in range(ND):
            p.op("sync", lambda e, t=t, i=i: e.dma_start(
                out=xt[i][:, :], in_=xstash[t, i, :, :]),
                waits=prev_bar if i == 0 else (), inc=dx, inc_amt=16)
            dgx = p.op("sync", lambda e, t=t, i=i: e.dma_start(
                out=sx[i][:, :], in_=xin_ap(t, i)), inc=dx, inc_amt=16)
        for k in range(ND):
            for kk in range(ND):
                p.op("tensor", lambda e, k=k, kk=kk: e.matmul(
                    out=psO[:, :], lhsT=wotile(0, kk, k), rhs=xt[kk][:, :],
                    start=(kk == 0), stop=False),
                    waits=([(dx, dgx)] + (w_wo if t == 0 else [])) if kk == 0 else ())
            for kk in range(ND):
                p.op("tensor", lambda e, k=k, kk=kk: e.matmul(
                    out=psO[:, :], lhsT=wotile(1, kk, k), rhs=cbf[0][kk][:, :],
                    start=False, stop=(kk == ND - 1)),
                    inc=ts if kk == ND - 1 else None)
            t_o = p.val(ts)
            p.op("scalar", lambda e, k=k: e.activation(
                out=t2f[:, :], in_=psO[:, :], func=AF.Tanh, bias=bias["boT"][:, k:k + 1]),
                waits=[(ts, t_o)], inc=ss)
            s_t = p.val(ss)
            p.op("vector", lambda e, k=k: e.scalar_tensor_tensor(
                out=xo[k][:, :], in0=t2f[:, :], scalar=0.5, in1=sx[k][:, :],
                op0=ALU.mult, op1=ALU.add), waits=[(ss, s_t)], inc=vs)
            v_x = p.val(vs)
            p.op("sync", lambda e, t=t, k=k: e.dma_start(
                out=xout_ap(t, k), in_=xo[k][:, :]),
                waits=[(vs, v_x)], inc=prefix + "do", inc_amt=16)
            p.op("tensor", lambda e: None, waits=[(vs, v_x)], real=False)
        prev_bar = p.barrier()
    p.wait("sync", [(prefix + "do", p.val(prefix + "do"))])


def build_ffn0_launch(in_shapes, Ll=512):
    nc = bass.Bass("TRN2", target_bir_lowering=False)
    d = declare_inputs(nc, in_shapes)
    xout = nc.dram_tensor("xout", [8, ND, 128, Ll], F32, kind="ExternalOutput").ap()
    p = Plan(nc)
    m = Sbuf(p)
    m.dram = d
    emit_const_loads(p, m)
    emit_ffn0(p, m, lambda t, i: d["xin"][t, i, :, :],
              lambda t, k: xout[t, k, :, :], Ll)
    p.build()
    return nc, m



def emit_att1(p, m, xin_ap, xout_ap, L, Lc, prefix="mb", entry_waits=()):
    nc = p.nc
    d = m.dram
    NDT = 16
    NE = 32
    NN = 16
    ds, dsg, vs, ss, ts = (prefix + "d", prefix + "dg", prefix + "v",
                           prefix + "s", prefix + "t")

    ipW = [m.sb(f"{prefix}ipW{par}", (128, ND * 128), BF16) for par in range(2)]
    opW = [m.sb(f"{prefix}opW{par}", (128, NDT * 128), BF16) for par in range(2)]
    xpW = m.sb(prefix + "xpW", (128, NDT * 128), BF16)
    dtW = m.sb(prefix + "dtW", (128, NDT * 128), BF16)
    Asb = m.sb(prefix + "A", (128, NDT * NN))
    convW = m.sb(prefix + "convW", (128, NDT * 4))
    convB = m.sb(prefix + "convB", (128, NDT))
    Dd = m.sb(prefix + "Dd", (128, NDT))
    dtb = m.sb(prefix + "dtb", (128, NDT))
    gamT = m.sb(prefix + "gamT", (128, ND))
    betT = m.sb(prefix + "betT", (128, ND))

    sx = [m.sb(f"{prefix}sx{i}", (128, Lc)) for i in range(ND)]
    scrA = m.sb(prefix + "scrA", (128, Lc * ND))
    xwf = [scrA[:, i * Lc:(i + 1) * Lc] for i in range(ND)]
    states = scrA[:, :].bitcast(BF16)
    xz = [m.sb(f"{prefix}xz{i}", (128, Lc), BF16) for i in range(ND)]
    ubig = m.sb(prefix + "ubig", (128, NDT * (Lc + 4)), BF16)
    uti = [ubig[:, t * (Lc + 4): t * (Lc + 4) + Lc + 4] for t in range(NDT)]
    utail = m.sb(prefix + "utail", (128, NDT * 4), BF16)
    uc = m.sb(prefix + "uc", (128, NDT * Lc), BF16)
    delta = m.sb(prefix + "delta", (128, NDT * Lc), BF16)
    du = m.sb(prefix + "du", (128, Lc), BF16)
    bcrows = m.sb(prefix + "bcrows", (128, NN * Lc), BF16)
    Brep = m.sb(prefix + "Brep", (128, NN * Lc), BF16)
    Crep = m.sb(prefix + "Crep", (128, NN * Lc), BF16)
    yg = m.sb(prefix + "yg", (128, NDT * Lc), BF16)
    xdbf = m.sb(prefix + "xdbf", (128, Lc), BF16)
    dAb = m.sb(prefix + "dAb", (128, Lc), BF16)
    dAb2 = m.sb(prefix + "dAb2", (128, Lc), BF16)
    dBu_all = m.sb(prefix + "dBu", (128, NN * Lc), BF16)
    yf = m.sb(prefix + "yf", (128, Lc))
    zt = m.sb(prefix + "zt", (128, Lc), BF16)
    carry = m.sb(prefix + "carry", (128, NDT * NN))
    tmp16 = m.sb(prefix + "tmp16", (128, 16))
    xwre = [m.sb(f"{prefix}xwre{j}", (128, Lc), BF16) for j in range(2)]
    xre = [m.sb(f"{prefix}xre{j}", (128, Lc)) for j in range(2)]
    sel0 = m.sb("k_sel0_bf", (128, 128), BF16)
    sel1 = m.sb("k_sel1_bf", (128, 128), BF16)

    for i in range(ND):
        m.alias(f"ln_xsq{i}", xz[i][:, :])
        m.alias(f"ln_xbf{i}", dBu_all[:, i * Lc:(i + 1) * Lc])
    m.alias("ln_rowr", zt[:, :])
    m.alias("ln_e2row", yf[0:1, :])
    m.alias("ln_m2row", yg[:, :].bitcast(F32)[0:1, 0:512])
    m.alias("ln_varrow", ubig[:, :].bitcast(F32)[0:1, 0:512])
    Alog = uc[:, 0:2 * NDT * NN].bitcast(F32)
    psW0 = m.ps(prefix + "psW0", (128, Lc))
    psW1 = m.ps(prefix + "psW1", (128, Lc))
    psO = m.ps(prefix + "psO", (128, Lc))

    _uq = sum(len(v) for v in p.ops.values())
    zstash = nc.dram_tensor(f"{prefix}_zst{_uq}", [2, NDT, 128, Lc], BF16, kind="Internal").ap()
    xwstash = nc.dram_tensor(f"{prefix}_xwst{_uq}", [2, ND, 128, Lc], BF16, kind="Internal").ap()
    bcstash = nc.dram_tensor(f"{prefix}_bcst{_uq}", [2, NN, Lc], BF16, kind="Internal").ap()

    p.op("sync", lambda e: e.dma_start(out=Alog[:, :], in_=d["A_logT"]),
         inc=prefix + "db", inc_amt=16)
    for nm, dst in (("convWT", convW), ("convBT", convB), ("DT", Dd),
                    ("dtbT", dtb), ("gammaT", gamT), ("betaT", betT)):
        p.op("sync", lambda e, nm=nm, dst=dst: e.dma_start(out=dst[:, :], in_=d[nm]),
             inc=prefix + "db", inc_amt=16)
    for k in range(NDT):
        p.op("gpsimd", lambda e, k=k: e.dma_start(
            out=xpW[:, k * 128:(k + 1) * 128],
            in_=d["xpT"][k * 128:(k + 1) * 128, :]), inc=dsg, inc_amt=16)
        p.op("gpsimd", lambda e, k=k: e.dma_start(
            out=dtW[:, k * 128:(k + 1) * 128],
            in_=d["dtT"][:, k * 128:(k + 1) * 128]), inc=dsg, inc_amt=16)
    w_bias = [(prefix + "db", p.val(prefix + "db")), (dsg, p.val(dsg))]
    p.op("scalar", lambda e: e.activation(out=Asb[:, :], in_=Alog[:, :], func=AF.Exp),
         waits=w_bias, inc=ss)
    p.op("vector", lambda e: e.tensor_scalar(out=Asb[:, :], in0=Asb[:, :],
                                             scalar1=-1.0, scalar2=None, op0=ALU.mult),
         waits=[(ss, p.val(ss))], inc=vs)
    p.op("vector", lambda e: e.memset(carry[:, :], 0.0), inc=vs)
    p.op("vector", lambda e: e.memset(bcrows[:, :], 0.0), inc=vs)
    p.op("vector", lambda e: e.memset(ubig[:, :], 0.0), inc=vs)
    p.op("vector", lambda e: e.memset(utail[:, :], 0.0), inc=vs)

    nch = L // Lc
    ln_done = {}
    ab = {}
    ipmm = [0, 0]
    ipguard = [None, None]

    def emit_AB(c):
        sl = slice(c * Lc, (c + 1) * Lc)
        dx = prefix + ("dxa" if c % 2 == 0 else "dxb")
        w0l = (list(entry_waits) if c == 0 else
               [(vs, ln_done[c - 1][0]), (ts, ab[c - 1]["ip"])])
        dgot = None
        for i in range(ND):
            dgot = p.op("sync", lambda e, i=i, sl=sl: e.dma_start(
                out=sx[i][:, :], in_=xin_ap(i, sl)),
                waits=w0l if i == 0 else (), inc=dx, inc_amt=16)
        emit_ln(p, m, [sx[i][:, :] for i in range(ND)], xwf, Lc, tag=prefix,
                dwaits=[(dx, dgot)], vsem=vs, ssem=ss, tsem=ts)
        v_l1 = p.val(vs)
        for i in range(ND):
            p.op("gpsimd", lambda e, i=i, c=c: e.dma_start(
                out=xwstash[c % 2, i, :, :], in_=xwf[i]),
                waits=[(vs, v_l1)] if i == 0 else (), inc=prefix + "dst",
                inc_amt=16)
        xw_st = p.val(prefix + "dst")
        emit_ln(p, m, xwf, xz, Lc, tag=prefix,
                dwaits=[(vs, v_l1)],
                gamma=[gamT[:, i:i + 1] for i in range(ND)],
                beta=[betT[:, i:i + 1] for i in range(ND)],
                eps2=1e-6, vsem=vs, ssem=ss, tsem=ts)
        ln_done[c] = (p.val(vs), p.val(ss))
        v_l2, s_l2 = ln_done[c]
        for e_t in range(NE):
            par = e_t % 2
            dip = prefix + ("dia" if par == 0 else "dib")
            dw = p.op("gpsimd", lambda e, e_t=e_t, par=par: e.dma_start(
                out=ipW[par][:, :].rearrange("p (k j) -> p k j", j=128),
                in_=d["ipT"][:, e_t * 128:(e_t + 1) * 128].rearrange(
                    "(k p) j -> p k j", p=128)),
                waits=([(ts, ipmm[par])] if ipmm[par] else [(ts, p.val(ts))]),
                inc=dip, inc_amt=16)
            ps = psW0 if par == 0 else psW1
            for k in range(ND):
                w0 = (([(dip, dw), (vs, v_l2), (ss, s_l2)] +
                       ([ipguard[par]] if ipguard[par] else []))
                      if k == 0 else ())
                p.op("tensor", lambda e, e_t=e_t, k=k, ps=ps, par=par: e.matmul(
                    out=ps[:, :], lhsT=ipW[par][:, k * 128:(k + 1) * 128],
                    rhs=xz[k][:, :], start=(k == 0), stop=(k == ND - 1)),
                    waits=w0, inc=ts if k == ND - 1 else None)
            t_ip = p.val(ts)
            ipmm[par] = t_ip
            if e_t < NDT:
                p.op("scalar", lambda e, e_t=e_t, ps=ps: e.activation(
                    out=uti[e_t][:, 4:], in_=ps[:, :], func=AF.Copy),
                    waits=[(ts, t_ip)], inc=ss)
                p.op("scalar", lambda e, e_t=e_t: e.activation(
                    out=uti[e_t][:, 0:4], in_=utail[:, e_t * 4:(e_t + 1) * 4],
                    func=AF.Copy), inc=ss)
            else:
                zi = e_t - NDT
                p.op("scalar", lambda e, ps=ps: e.activation(
                    out=zt[:, :], in_=ps[:, :], func=AF.Copy),
                    waits=[(ts, t_ip), (prefix + "dst", p.val(prefix + "dst"))], inc=ss)
                s_z = p.val(ss)
                p.op("gpsimd", lambda e, zi=zi, c=c: e.dma_start(
                    out=zstash[c % 2, zi, :, :], in_=zt[:, :]),
                    waits=[(ss, s_z)], inc=prefix + "dst", inc_amt=16)
            ipguard[par] = (ss, p.val(ss))
        ab[c] = {"ip": p.val(ts), "sip": p.val(ss),
                 "zst": p.val(prefix + "dst"), "xwst": xw_st}

    emit_AB(0)
    scan_last = 0
    opmm = [0, 0]
    xwre_war = [0, 0]
    do_val = [0, 0]
    for c in range(nch):
        sl = slice(c * Lc, (c + 1) * Lc)
        for t in range(NDT):
            p.op("vector", lambda e, t=t: e.tensor_copy(
                out=utail[:, t * 4:(t + 1) * 4], in_=uti[t][:, Lc:Lc + 4]),
                waits=[(ss, ab[c]["sip"])] if t == 0 else (), inc=vs)
        v_tails = p.val(vs)
        for t in range(NDT):
            p.op("scalar", lambda e, t=t: e.activation(
                out=yf[:, :], in_=uti[t][:, 1:1 + Lc], func=AF.Copy,
                scale=convW[:, 4 * t:4 * t + 1]),
                waits=[(vs, v_tails)] if t == 0 else (), inc=ss)
            s_c0 = p.val(ss)
            for kk in range(1, 4):
                p.op("vector", lambda e, t=t, kk=kk: e.scalar_tensor_tensor(
                    out=yf[:, :], in0=uti[t][:, 1 + kk:1 + kk + Lc],
                    scalar=convW[:, 4 * t + kk:4 * t + kk + 1], in1=yf[:, :],
                    op0=ALU.mult, op1=ALU.add),
                    waits=[(ss, s_c0)] if kk == 1 else (), inc=vs)
            v_cv = p.val(vs)
            p.op("scalar", lambda e, t=t: e.activation(
                out=uc[:, t * Lc:(t + 1) * Lc], in_=yf[:, :], func=AF.Silu,
                bias=convB[:, t:t + 1]), waits=[(vs, v_cv)], inc=ss)
        s_uc = p.val(ss)
        for k in range(NDT):
            p.op("tensor", lambda e, k=k: e.matmul(
                out=psW0[:, :], lhsT=xpW[:, k * 128:(k + 1) * 128],
                rhs=uc[:, k * Lc:(k + 1) * Lc], start=(k == 0), stop=(k == NDT - 1)),
                waits=[(ss, s_uc)] if k == 0 else (), inc=ts if k == NDT - 1 else None)
        t_xd = p.val(ts)
        p.op("scalar", lambda e: e.activation(out=xdbf[:, :], in_=psW0[:, :],
                                              func=AF.Copy),
             waits=[(ts, t_xd)], inc=ss)
        s_xd = p.val(ss)
        p.op("gpsimd", lambda e: e.dma_start(out=bcstash[0, :, :], in_=xdbf[64:80, :]),
             waits=[(ss, s_xd)], inc=prefix + "dbc", inc_amt=16)
        p.op("gpsimd", lambda e: e.dma_start(out=bcstash[1, :, :], in_=xdbf[80:96, :]),
             inc=prefix + "dbc", inc_amt=16)
        d_bc1 = p.val(prefix + "dbc")
        p.op("gpsimd", lambda e: e.dma_start(
            out=bcrows[0:1, :], in_=bcstash[0:1].rearrange("o n l -> o (n l)")),
            waits=[(prefix + "dbc", d_bc1)] +
                  ([(vs, scan_last)] if scan_last else []),
            inc=prefix + "dbc", inc_amt=16)
        p.op("gpsimd", lambda e: e.dma_start(
            out=bcrows[1:2, :], in_=bcstash[1:2].rearrange("o n l -> o (n l)")),
            inc=prefix + "dbc", inc_amt=16)
        d_bc = p.val(prefix + "dbc")
        bev, cev = None, None
        for n in range(NN):
            p.op("tensor", lambda e, n=n: e.matmul(
                out=psW0[:, :], lhsT=sel0[:, :], rhs=bcrows[:, n * Lc:(n + 1) * Lc],
                start=True, stop=True),
                waits=([(prefix + "dbc", d_bc)] if n == 0 else
                       ([(vs, bev)] if bev else ())), inc=ts)
            t_b = p.val(ts)
            p.op("vector", lambda e, n=n: e.tensor_copy(
                out=Brep[:, n * Lc:(n + 1) * Lc], in_=psW0[:, :]),
                waits=[(ts, t_b)], inc=vs)
            bev = p.val(vs)
            p.op("tensor", lambda e, n=n: e.matmul(
                out=psW1[:, :], lhsT=sel1[:, :], rhs=bcrows[:, n * Lc:(n + 1) * Lc],
                start=True, stop=True),
                waits=([(vs, cev)] if cev else ()), inc=ts)
            t_c = p.val(ts)
            p.op("vector", lambda e, n=n: e.tensor_copy(
                out=Crep[:, n * Lc:(n + 1) * Lc], in_=psW1[:, :]),
                waits=[(ts, t_c)], inc=vs)
            cev = p.val(vs)
        v_bc = p.val(vs)
        for t in range(NDT):
            ps = psW0 if t % 2 == 0 else psW1
            p.op("tensor", lambda e, t=t, ps=ps: e.matmul(
                out=ps[:, :], lhsT=dtW[:, t * 128:(t + 1) * 128], rhs=xdbf[:, :],
                start=True, stop=True),
                waits=[(vs, v_bc)] if t == 0 else (), inc=ts)
            t_dt = p.val(ts)
            p.op("scalar", lambda e, t=t, ps=ps: e.activation(
                out=yf[:, :], in_=ps[:, :], func=AF.Abs, bias=dtb[:, t:t + 1]),
                waits=[(ts, t_dt)], inc=ss)
            p.op("scalar", lambda e: e.activation(
                out=yf[:, :], in_=yf[:, :], func=AF.Exp, scale=-1.0), inc=ss)
            p.op("scalar", lambda e: e.activation(
                out=yf[:, :], in_=yf[:, :], func=AF.Ln, bias=1.0), inc=ss)
            s_sp = p.val(ss)
            p.op("vector", lambda e, t=t, ps=ps: e.scalar_tensor_tensor(
                out=dAb[:, :], in0=ps[:, :], scalar=dtb[:, t:t + 1],
                op0=ALU.add, op1=ALU.max, in1=m.const(0.0).broadcast_to([128, Lc])),
                waits=[(ss, s_sp)], inc=vs)
            p.op("vector", lambda e, t=t: e.tensor_add(
                out=delta[:, t * Lc:(t + 1) * Lc], in0=dAb[:, :], in1=yf[:, :]),
                inc=vs)
            v_dl = p.val(vs)
            p.op("tensor", lambda e: None, waits=[(vs, v_dl)], real=False)
        if c + 1 < nch:
            emit_AB(c + 1)
            xw_gate = [(prefix + "dst", ab[c + 1]["xwst"])]
            ln_gate = ln_done[c + 1][0]
        else:
            xw_gate = []
            ln_gate = 0
        NH = NN // 2
        dAh = [bcrows[:, 0:NH * Lc], bcrows[:, NH * Lc:NN * Lc]]
        dAh3 = [a.rearrange("p (n l) -> p n l", l=Lc) for a in dAh]
        dBu3 = dBu_all[:, :].rearrange("p (n l) -> p n l", l=Lc)
        Brep3 = Brep[:, :].rearrange("p (n l) -> p n l", l=Lc)
        zc = m.const(0.0)
        scan_done = [0, 0]
        for t in range(NDT):
            p.op("vector", lambda e, t=t: e.tensor_mul(
                out=du[:, :], in0=delta[:, t * Lc:(t + 1) * Lc],
                in1=uc[:, t * Lc:(t + 1) * Lc]), inc=vs)
            v_du = p.val(vs)
            p.op("vector", lambda e: e.tensor_tensor(
                out=dBu3, in0=Brep3,
                in1=du[:, :].unsqueeze(1).broadcast_to([128, NN, Lc]),
                op=ALU.mult), inc=vs)
            for h in range(2):
                base = t * NN + h * NH
                war = ([(vs, scan_done[h])] if scan_done[h]
                       else [(vs, v_du)])
                for n in range(NH):
                    p.op("scalar", lambda e, t=t, n=n, h=h, base=base: e.activation(
                        out=dAh[h][:, n * Lc:(n + 1) * Lc],
                        in_=delta[:, t * Lc:(t + 1) * Lc], func=AF.Exp,
                        scale=Asb[:, base + n:base + n + 1]),
                        waits=war if n == 0 else (), inc=ss)
                s_dA = p.val(ss)
                p.op("vector", lambda e, h=h, base=base: e.tensor_tensor(
                    out=tmp16[:, 0:NH].unsqueeze(2), in0=dAh3[h][:, :, 0:1],
                    in1=carry[:, base:base + NH].unsqueeze(2), op=ALU.mult),
                    waits=[(ss, s_dA)], inc=vs)
                p.op("vector", lambda e, h=h: e.tensor_add(
                    out=dBu3[:, h * NH:(h + 1) * NH, 0:1],
                    in0=dBu3[:, h * NH:(h + 1) * NH, 0:1],
                    in1=tmp16[:, 0:NH].unsqueeze(2)), inc=vs)
                p.op("vector", lambda e, h=h: e.tensor_scalar(
                    out=dAh3[h][:, :, 0:1], in0=dAh3[h][:, :, 0:1],
                    scalar1=0.0, scalar2=None, op0=ALU.mult), inc=vs)
                p.op("vector", lambda e, h=h: e.tensor_tensor_scan(
                    out=states[:, h * NH * Lc:(h + 1) * NH * Lc], data0=dAh[h],
                    data1=dBu_all[:, h * NH * Lc:(h + 1) * NH * Lc], initial=zc,
                    op0=ALU.mult, op1=ALU.add),
                    waits=xw_gate if (t == 0 and h == 0) else (), inc=vs)
                scan_done[h] = p.val(vs)
            v_scan = p.val(vs)
            scan_last = v_scan
            sv = states.rearrange("p (n l) -> p n l", l=Lc)
            p.op("vector", lambda e, t=t, sv=sv: e.tensor_copy(
                out=carry[:, t * NN:(t + 1) * NN].unsqueeze(2),
                in_=sv[:, 0:NN, Lc - 1:Lc]), inc=vs)
            p.op("vector", lambda e: e.tensor_mul(
                out=states[:, 0:NN * Lc], in0=states[:, 0:NN * Lc],
                in1=Crep[:, :]), inc=vs)
            half = NN * Lc // 2
            p.op("vector", lambda e, half=half: e.tensor_add(
                out=states[:, 0:half], in0=states[:, 0:half],
                in1=states[:, half:2 * half]), inc=vs)
            q = half // 2
            p.op("vector", lambda e, q=q: e.tensor_add(
                out=states[:, 0:q], in0=states[:, 0:q], in1=states[:, q:2 * q]),
                inc=vs)
            o8 = q // 2
            p.op("vector", lambda e, o8=o8: e.tensor_add(
                out=states[:, 0:o8], in0=states[:, 0:o8], in1=states[:, o8:2 * o8]),
                inc=vs)
            p.op("vector", lambda e: e.tensor_add(
                out=yf[:, :], in0=states[:, 0:Lc], in1=states[:, Lc:2 * Lc]),
                inc=vs)
            p.op("vector", lambda e, t=t: e.scalar_tensor_tensor(
                out=yg[:, t * Lc:(t + 1) * Lc], in0=uc[:, t * Lc:(t + 1) * Lc],
                scalar=Dd[:, t:t + 1], in1=yf[:, :], op0=ALU.mult, op1=ALU.add),
                inc=vs)
        v_y = p.val(vs)
        zdone = [(prefix + "dst", ab[c]["zst"])]
        for t in range(NDT):
            zb_t = zt if t % 2 == 0 else xdbf
            sz_t = dAb if t % 2 == 0 else dAb2
            dz = prefix + ("dza" if t % 2 == 0 else "dzb")
            dzg = p.op("gpsimd", lambda e, t=t, c=c, zb_t=zb_t: e.dma_start(
                out=zb_t[:, :], in_=zstash[c % 2, t, :, :]),
                waits=(zdone + [(ss, p.val(ss)), (vs, p.val(vs))]) if t < 2 else
                      [(ss, p.val(ss))], inc=dz, inc_amt=16)
            p.op("scalar", lambda e, zb_t=zb_t, sz_t=sz_t: e.activation(
                out=sz_t[:, :], in_=zb_t[:, :], func=AF.Silu),
                waits=[(dz, dzg), (vs, p.val(vs))], inc=ss)
            s_z = p.val(ss)
            p.op("vector", lambda e, t=t, sz_t=sz_t: e.tensor_mul(
                out=yg[:, t * Lc:(t + 1) * Lc], in0=yg[:, t * Lc:(t + 1) * Lc],
                in1=sz_t[:, :]), waits=[(ss, s_z)], inc=vs)
        v_g = p.val(vs)
        xwdone = [(prefix + "dst", ab[c]["xwst"])]
        for o in range(ND):
            par = o % 2
            dop = prefix + ("dpa" if par == 0 else "dpb")
            dw = p.op("gpsimd", lambda e, o=o, par=par: e.dma_start(
                out=opW[par][:, :].rearrange("p (k j) -> p k j", j=128),
                in_=d["opT"][:, o * 128:(o + 1) * 128].rearrange(
                    "(k p) j -> p k j", p=128)),
                waits=([(ts, opmm[par])] if opmm[par] else [(ts, p.val(ts))]),
                inc=dop, inc_amt=16)
            p.op("sync", lambda e, o=o, c=c, par=par: e.dma_start(
                out=xwre[par][:, :], in_=xwstash[c % 2, o, :, :]),
                waits=(xwdone if o == 0 else []) +
                      ([(vs, xwre_war[par])] if xwre_war[par] else []),
                inc=prefix + "dxw", inc_amt=16)
            dxr = p.op("sync", lambda e, o=o, sl=sl, par=par: e.dma_start(
                out=xre[par][:, :], in_=xin_ap(o, sl)),
                waits=[(prefix + "do", do_val[par])] if do_val[par] else (),
                inc=prefix + "dxw", inc_amt=16)
            for k in range(NDT):
                p.op("tensor", lambda e, o=o, k=k, par=par: e.matmul(
                    out=psO[:, :], lhsT=opW[par][:, k * 128:(k + 1) * 128],
                    rhs=yg[:, k * Lc:(k + 1) * Lc], start=(k == 0), stop=(k == NDT - 1)),
                    waits=[(dop, dw), (vs, v_g)] if k == 0 else (),
                    inc=ts if k == NDT - 1 else None)
            t_o = p.val(ts)
            opmm[par] = t_o
            p.op("vector", lambda e, par=par: e.tensor_add(
                out=yf[:, :], in0=psO[:, :], in1=xwre[par][:, :]),
                waits=[(ts, t_o), (prefix + "dxw", dxr)], inc=vs)
            p.op("vector", lambda e, par=par: e.tensor_add(
                out=xre[par][:, :], in0=yf[:, :], in1=xre[par][:, :]),
                inc=vs)
            v_x = p.val(vs)
            xwre_war[par] = v_x
            p.op("sync", lambda e, o=o, sl=sl, par=par: e.dma_start(
                out=xout_ap(o, sl), in_=xre[par][:, :]),
                waits=[(vs, v_x)], inc=prefix + "do", inc_amt=16)
            do_val[par] = p.val(prefix + "do")
            p.op("tensor", lambda e: None, waits=[(vs, v_x)], real=False)
    p.wait("sync", [(prefix + "do", p.val(prefix + "do"))])


def build_att1_launch(in_shapes, L=4096, Lc=512):
    nc = bass.Bass("TRN2", target_bir_lowering=False)
    d = declare_inputs(nc, in_shapes)
    xout = nc.dram_tensor("xout", [ND, 128, L], F32, kind="ExternalOutput").ap()
    p = Plan(nc)
    m = Sbuf(p)
    m.dram = d
    emit_const_loads(p, m)
    emit_att1(p, m, lambda i, sl: d["xin"][i, :, sl],
              lambda o, sl: xout[o, :, sl], L, Lc)
    p.build()
    return nc, m



def emit_att0(p, m, xin_ap, xout_ap, T, prefix="at", entry_waits=()):
    nc = p.nc
    d = m.dram
    Lc = 512
    NCH = T // Lc
    NTC = T // 128
    G = T // 64
    NPR = 8
    ds, dsg, vs, ss, ts = (prefix + "d", prefix + "dg", prefix + "v",
                           prefix + "s", prefix + "t")

    sx = [m.sb(f"{prefix}sx{i}", (128, Lc)) for i in range(ND)]
    xln = [m.sb(f"{prefix}xln{i}", (128, Lc), BF16) for i in range(ND)]
    B2 = lambda nm, sh, dt=BF16: [m.sb(f"{prefix}{nm}{j}", sh, dt) for j in range(2)]
    qp, kp, vT, ET, LT = (B2("qp", (128, T)), B2("kp", (128, T)), B2("vT", (128, T)),
                          B2("ET", (128, T)), B2("LT", (128, T)))
    wqkv = [[m.sb(f"{prefix}wqkv{j}_{par}", (128, ND * 128), BF16) for j in range(3)]
            for par in range(2)]
    xlc = [m.sb(f"{prefix}xlc{par}_{i}", (128, Lc), BF16) for par in range(2)
           for i in range(ND)]
    xlt = lambda par, i: xlc[par * ND + i]
    klraw, qlraw = B2("klraw", (128, 64), F32), B2("qlraw", (128, 64), F32)
    blk_k, blk_q = B2("blk_k", (128, 128)), B2("blk_q", (128, 128))
    Kblk, Vblk, KVb, Tb = (B2("Kblk", (128, 128)), B2("Vblk", (128, 128)),
                           B2("KVb", (128, 128)), B2("Tb", (128, 128)))
    PTsb, NVsb, PNv = B2("PTsb", (128, 128)), B2("NVsb", (128, 128)), B2("PNv", (128, 128))
    sm2, sm2b = B2("sm2", (128, 2), F32), B2("sm2b", (128, 2), F32)
    Lnc, vbf, Ech = B2("Lnc", (128, 128)), B2("vbf", (128, Lc)), B2("Ech", (128, Lc))
    Nmx, Ers = B2("Nmx", (128, NCH + 1), F32), B2("Ers", (128, NCH + 1), F32)
    col1 = B2("col1", (128, 1), F32)
    ytmp = B2("ytmp", (128, Lc))
    w2sb = m.sb(prefix + "w2sb", (128, ND * 1024), BF16)
    ypr = [m.sb(f"{prefix}ypr{pr}", (128, Lc), BF16) for pr in range(NPR)]
    xo = [m.sb(f"{prefix}xo{i}", (128, Lc)) for i in range(ND)]
    eye = m.sb("k_eye_bf", (128, 128), BF16)
    onesblk = m.sb("k_onesblk_bf", (128, 128), BF16)

    psW0 = m.ps(prefix + "psW0", (128, Lc))
    psW1 = m.ps(prefix + "psW1", (128, Lc))
    psWn = [psW0, psW1]
    psQ = [m.ps(f"{prefix}psQ{j}", (128, Lc)) for j in range(2)]
    psNV = [m.ps(f"{prefix}psNV{j}", (128, 128)) for j in range(2)]
    pst = [m.ps(f"{prefix}pst{j}", (128, 128), BF16) for j in range(2)]
    psNVp = [[psNV[par], psNV[par]] for par in range(2)]
    pstp = [[pst[par], pst[par]] for par in range(2)]
    pstQ = [psQ[1].bitcast(BF16)[:, j * 128:(j + 1) * 128] for j in range(2)]
    m.alias("psA", psW0)
    m.alias("psB", psW1)
    m.alias("psR", psQ[0])

    _uq = sum(len(v) for v in p.ops.values())
    xlnstash = nc.dram_tensor(f"{prefix}_xlnst{_uq}", [ND, 128, T], BF16, kind="Internal").ap()
    ystash = nc.dram_tensor(f"{prefix}_yst{_uq}", [NPR, 128, T], BF16, kind="Internal").ap()

    for i in range(ND):
        m.alias(f"ln_xbf{i}", ET[0][:, i * Lc:(i + 1) * Lc])
        m.alias(f"ln_xsq{i}", ET[1][:, i * Lc:(i + 1) * Lc])

    prev_bar = tuple(entry_waits)
    for c in range(NCH):
        sl = slice(c * Lc, (c + 1) * Lc)
        dx = prefix + ("dxa" if c % 2 == 0 else "dxb")
        dgot = None
        for i in range(ND):
            dgot = p.op("sync", lambda e, i=i, sl=sl: e.dma_start(
                out=sx[i][:, :], in_=xin_ap(i, sl)),
                waits=prev_bar if i == 0 else (), inc=dx, inc_amt=16)
        emit_ln(p, m, [sx[i][:, :] for i in range(ND)], xln, Lc, tag=prefix,
                dwaits=[(dx, dgot)], vsem=vs, ssem=ss, tsem=ts)
        v_ln = p.val(vs)
        for i in range(ND):
            p.op("sync", lambda e, i=i, sl=sl: e.dma_start(
                out=xlnstash[i, :, sl], in_=xln[i][:, :]),
                waits=[(vs, v_ln)] if i == 0 else (), inc=prefix + "dst", inc_amt=16)
        prev_bar = p.barrier()
    stash_done = [(prefix + "dst", p.val(prefix + "dst"))]

    qkv_mm_done = [0, 0]
    chain_done = [0, 0]
    xlc_done = [0, 0]
    vt_evac_done = [0]
    lm_state = {}

    def emit_qkv(pr):
        yield
        par = pr % 2
        dwsem = prefix + ("dwa" if par == 0 else "dwb")
        dw = None
        for j, coff in enumerate((pr * 128, 1024 + pr * 128, 2048 + pr * 128)):
            dw = p.op("gpsimd", lambda e, j=j, coff=coff, par=par: e.dma_start(
                out=wqkv[par][j][:, :].rearrange("p (k j2) -> p k j2", j2=128),
                in_=d["wattn"][:, coff:coff + 128].rearrange("(k p) j2 -> p k j2", p=128)),
                waits=[(ts, qkv_mm_done[par])] if j == 0 else (),
                inc=dwsem, inc_amt=16)
        first = [(vs, chain_done[par])] if chain_done[par] else []
        for c in range(NCH):
            sl = slice(c * Lc, (c + 1) * Lc)
            cpar = c % 2
            dx = prefix + ("dla" if cpar == 0 else "dlb")
            dl = None
            for i in range(ND):
                w0 = (stash_done + [(ts, p.val(ts))]) if (pr == 0 and c == 0 and i == 0) \
                    else ((([(ts, xlc_done[cpar])] if xlc_done[cpar] else []) + first)
                          if i == 0 else ())
                dl = p.op("sync", lambda e, i=i, sl=sl, cpar=cpar: e.dma_start(
                    out=xlt(cpar, i)[:, :], in_=xlnstash[i, :, sl]),
                    waits=w0, inc=dx, inc_amt=16)
            for j, dst in ((0, qp[par]), (1, kp[par])):
                ps = psQ[j]
                for k in range(ND):
                    w0 = ([(dx, dl), (dwsem, dw)] +
                          ([(vs, vt_evac_done[0])] if vt_evac_done[0] else [])
                          ) if k == 0 else ()
                    p.op("tensor", lambda e, j=j, k=k, ps=ps, cpar=cpar, par=par: e.matmul(
                        out=ps[:, :], lhsT=wqkv[par][j][:, k * 128:(k + 1) * 128],
                        rhs=xlt(cpar, k)[:, :], start=(k == 0), stop=(k == ND - 1)),
                        waits=w0, inc=ts if k == ND - 1 else None)
                t_qk = p.val(ts)
                p.op("vector", lambda e, j=j, dst=dst, sl=sl, ps=ps: e.tensor_copy(
                    out=dst[:, sl], in_=ps[:, :]), waits=[(ts, t_qk)], inc=vs)
            v_qk = p.val(vs)
            p.op("tensor", lambda e: None, waits=[(vs, v_qk)], real=False)
            for k in range(ND):
                p.op("tensor", lambda e, k=k, cpar=cpar, par=par: e.matmul(
                    out=psQ[0][:, :], lhsT=wqkv[par][2][:, k * 128:(k + 1) * 128],
                    rhs=xlt(cpar, k)[:, :], start=(k == 0), stop=(k == ND - 1)),
                    inc=ts if k == ND - 1 else None)
            t_v = p.val(ts)
            xlc_done[cpar] = t_v
            p.op("vector", lambda e, par=par: e.tensor_copy(out=vbf[par][:, :],
                                                            in_=psQ[0][:, :]),
                 waits=[(ts, t_v)], inc=vs)
            v_vb = p.val(vs)
            for j2 in range(Lc // 128):
                tp_ = pstQ[j2 % 2]
                p.op("tensor", lambda e, j2=j2, par=par, tp_=tp_: e.transpose(
                    out=tp_[:, :], in_=vbf[par][:, j2 * 128:(j2 + 1) * 128],
                    identity=eye[:, :]),
                    waits=[(vs, v_vb)] if j2 == 0 else [(vs, p.val(vs))], inc=ts)
                t_tr = p.val(ts)
                p.op("vector", lambda e, c=c, j2=j2, par=par, tp_=tp_: e.tensor_copy(
                    out=vT[par][:, (c * 4 + j2) * 128:(c * 4 + j2 + 1) * 128],
                    in_=tp_[:, :]), waits=[(ts, t_tr)], inc=vs)
            vt_evac_done[0] = p.val(vs)
            yield
        qkv_mm_done[par] = p.val(ts)
        p.op("vector", lambda e, par=par: e.tensor_reduce(
            out=qlraw[par][:, :], in_=qp[par][:, :].rearrange("p (mm g) -> p mm g", mm=64),
            axis=AX, op=ALU.add), waits=[(vs, p.val(vs))], inc=vs)
        p.op("vector", lambda e, par=par: e.tensor_reduce(
            out=klraw[par][:, :], in_=kp[par][:, :].rearrange("p (mm g) -> p mm g", mm=64),
            axis=AX, op=ALU.add), inc=vs)
        p.op("vector", lambda e, par=par: e.memset(blk_k[par][:, :], 0.0), inc=vs)
        p.op("vector", lambda e, par=par: e.memset(blk_q[par][:, :], 0.0), inc=vs)
        p.op("vector", lambda e, par=par: e.memset(sm2b[par][:, :], 0.0), inc=vs)
        v_lm = p.val(vs)
        p.op("scalar", lambda e, par=par: e.activation(
            out=blk_k[par][0:64, 0:64], in_=klraw[par][0:64, :], func=AF.Copy,
            scale=1.0 / G), waits=[(vs, v_lm)], inc=ss)
        p.op("scalar", lambda e, par=par: e.activation(
            out=blk_k[par][64:128, 64:128], in_=klraw[par][64:128, :], func=AF.Copy,
            scale=1.0 / G), inc=ss)
        p.op("scalar", lambda e, par=par: e.activation(
            out=blk_q[par][0:64, 0:64], in_=qlraw[par][0:64, :], func=AF.Copy,
            scale=1.0 / G), inc=ss)
        p.op("scalar", lambda e, par=par: e.activation(
            out=blk_q[par][64:128, 64:128], in_=qlraw[par][64:128, :], func=AF.Copy,
            scale=1.0 / G), inc=ss)
        lm_state[pr] = (p.val(vs), p.val(ss))

    def emit_chain(pr):
        yield
        par = pr % 2
        v_lm, s_blk = lm_state[pr]
        for tc in range(NTC):
            pv = psNVp[par][tc % 2]
            tp_ = pstp[par][tc % 2]
            p.op("tensor", lambda e, tc=tc, par=par, pv=pv: e.matmul(
                out=pv[:, :], lhsT=qp[par][:, tc * 128:(tc + 1) * 128],
                rhs=blk_k[par][:, :], start=True, stop=True),
                waits=[(ss, s_blk)] if tc == 0 else [(vs, p.val(vs))], inc=ts)
            t_L = p.val(ts)
            p.op("vector", lambda e, par=par, pv=pv: e.tensor_reduce(
                out=sm2[par][:, :], in_=pv[:, :].rearrange("p (h mm) -> p h mm", h=2),
                axis=AX, op=ALU.max), waits=[(ts, t_L)], inc=vs)
            p.op("vector", lambda e, par=par: e.tensor_scalar(
                out=sm2[par][:, :], in0=sm2[par][:, :], scalar1=-1.0, scalar2=None,
                op0=ALU.mult), inc=vs)
            v_mx = p.val(vs)
            p.op("scalar", lambda e, par=par, pv=pv: e.activation(
                out=Lnc[par][:, 0:64], in_=pv[:, 0:64], func=AF.Exp,
                bias=sm2[par][:, 0:1], accum_out=sm2b[par][:, 0:1]),
                waits=[(vs, v_mx)], inc=ss)
            p.op("scalar", lambda e, par=par, pv=pv: e.activation(
                out=Lnc[par][:, 64:128], in_=pv[:, 64:128], func=AF.Exp,
                bias=sm2[par][:, 1:2], accum_out=sm2b[par][:, 1:2]), inc=ss)
            s_ex = p.val(ss)
            p.op("vector", lambda e, par=par: e.reciprocal(out=sm2[par][:, :],
                                                           in_=sm2b[par][:, :]),
                 waits=[(ss, s_ex)], inc=vs)
            p.op("vector", lambda e, par=par: e.tensor_scalar(
                out=Lnc[par][:, 0:64], in0=Lnc[par][:, 0:64], scalar1=sm2[par][:, 0:1],
                scalar2=None, op0=ALU.mult), waits=[(vs, p.val(vs))], inc=vs)
            p.op("vector", lambda e, par=par: e.tensor_scalar(
                out=Lnc[par][:, 64:128], in0=Lnc[par][:, 64:128],
                scalar1=sm2[par][:, 1:2], scalar2=None, op0=ALU.mult), inc=vs)
            v_Ln = p.val(vs)
            p.op("tensor", lambda e, par=par, tp_=tp_: e.transpose(
                out=tp_[:, :], in_=Lnc[par][:, :], identity=eye[:, :]),
                waits=[(vs, v_Ln)], inc=ts)
            t_lt = p.val(ts)
            p.op("vector", lambda e, tc=tc, par=par, tp_=tp_: e.tensor_copy(
                out=LT[par][:, tc * 128:(tc + 1) * 128], in_=tp_[:, :]),
                waits=[(ts, t_lt)], inc=vs)
            yield
        for c in range(NCH):
            p.op("tensor", lambda e, c=c, par=par: e.matmul(
                out=psWn[par][:, :], lhsT=blk_q[par][:, :], rhs=kp[par][:, c * Lc:(c + 1) * Lc],
                start=True, stop=True), waits=[(vs, p.val(vs))], inc=ts)
            t_N = p.val(ts)
            p.op("vector", lambda e, c=c, par=par: e.tensor_reduce(
                out=Nmx[par][:, c:c + 1], in_=psWn[par][:, :], axis=AX, op=ALU.max),
                waits=[(ts, t_N)], inc=vs)
            yield
        p.op("vector", lambda e, par=par: e.tensor_reduce(
            out=Nmx[par][:, NCH:NCH + 1], in_=Nmx[par][:, 0:NCH], axis=AX, op=ALU.max),
            inc=vs)
        p.op("vector", lambda e, par=par: e.tensor_scalar(
            out=col1[par][:, :], in0=Nmx[par][:, NCH:NCH + 1], scalar1=-1.0,
            scalar2=None, op0=ALU.mult), inc=vs)
        v_gm = p.val(vs)
        for c in range(NCH):
            p.op("tensor", lambda e, c=c, par=par: e.matmul(
                out=psWn[par][:, :], lhsT=blk_q[par][:, :], rhs=kp[par][:, c * Lc:(c + 1) * Lc],
                start=True, stop=True),
                waits=[(vs, v_gm)] if c == 0 else [(ss, p.val(ss))], inc=ts)
            t_N = p.val(ts)
            p.op("scalar", lambda e, c=c, par=par: e.activation(
                out=Ech[par][:, :], in_=psWn[par][:, :], func=AF.Exp, bias=col1[par][:, 0:1],
                accum_out=Ers[par][:, c:c + 1]), waits=[(ts, t_N)], inc=ss)
            s_E = p.val(ss)
            for j2 in range(Lc // 128):
                tp_ = pstp[par][j2 % 2]
                p.op("tensor", lambda e, j2=j2, par=par, tp_=tp_: e.transpose(
                    out=tp_[:, :], in_=Ech[par][:, j2 * 128:(j2 + 1) * 128],
                    identity=eye[:, :]),
                    waits=[(ss, s_E)] if j2 == 0 else [(vs, p.val(vs))], inc=ts)
                t_te = p.val(ts)
                p.op("vector", lambda e, c=c, j2=j2, par=par, tp_=tp_: e.tensor_copy(
                    out=ET[par][:, (c * 4 + j2) * 128:(c * 4 + j2 + 1) * 128],
                    in_=tp_[:, :]), waits=[(ts, t_te)], inc=vs)
            yield
        p.op("vector", lambda e, par=par: e.tensor_reduce(
            out=Ers[par][:, NCH:NCH + 1], in_=Ers[par][:, 0:NCH], axis=AX, op=ALU.add),
            inc=vs)
        p.op("vector", lambda e, par=par: e.reciprocal(
            out=col1[par][:, :], in_=Ers[par][:, NCH:NCH + 1]),
            waits=[(vs, p.val(vs))], inc=vs)
        v_rsn = p.val(vs)
        for tc in range(NTC):
            p.op("tensor", lambda e, tc=tc, par=par: e.matmul(
                out=psNV[par][:, :], lhsT=ET[par][:, tc * 128:(tc + 1) * 128],
                rhs=vT[par][:, tc * 128:(tc + 1) * 128], start=(tc == 0),
                stop=(tc == NTC - 1)),
                waits=[(vs, v_rsn)] if tc == 0 else (),
                inc=ts if tc == NTC - 1 else None)
        t_nv = p.val(ts)
        yield
        p.op("vector", lambda e, par=par: e.memset(NVsb[par][:, :], 0.0),
             waits=[(ts, t_nv)], inc=vs)
        p.op("vector", lambda e, par=par: e.tensor_scalar(
            out=NVsb[par][0:64, 0:64], in0=psNV[par][0:64, 0:64],
            scalar1=col1[par][0:64, 0:1], scalar2=None, op0=ALU.mult),
            waits=[(vs, p.val(vs))], inc=vs)
        p.op("vector", lambda e, par=par: e.tensor_scalar(
            out=NVsb[par][64:128, 64:128], in0=psNV[par][64:128, 64:128],
            scalar1=col1[par][64:128, 0:1], scalar2=None, op0=ALU.mult), inc=vs)
        v_nv = p.val(vs)
        p.op("tensor", lambda e, par=par: e.matmul(
            out=psNV[par][:, :], lhsT=blk_q[par][:, :], rhs=blk_k[par][:, :],
            start=True, stop=True), waits=[(vs, v_nv)], inc=ts)
        t_P = p.val(ts)
        p.op("vector", lambda e, par=par: e.tensor_reduce(
            out=sm2[par][:, :], in_=psNV[par][:, :].rearrange("p (h mm) -> p h mm", h=2),
            axis=AX, op=ALU.max), waits=[(ts, t_P)], inc=vs)
        p.op("vector", lambda e, par=par: e.tensor_scalar(
            out=sm2[par][:, :], in0=sm2[par][:, :], scalar1=-1.0, scalar2=None,
            op0=ALU.mult), inc=vs)
        p.op("vector", lambda e, par=par: e.memset(Kblk[par][:, :], 0.0), inc=vs)
        v_pm = p.val(vs)
        p.op("scalar", lambda e, par=par: e.activation(
            out=Kblk[par][0:64, 0:64], in_=psNV[par][0:64, 0:64], func=AF.Exp,
            bias=sm2[par][0:64, 0:1], accum_out=sm2b[par][0:64, 0:1]),
            waits=[(vs, v_pm)], inc=ss)
        p.op("scalar", lambda e, par=par: e.activation(
            out=Kblk[par][64:128, 64:128], in_=psNV[par][64:128, 64:128], func=AF.Exp,
            bias=sm2[par][64:128, 1:2], accum_out=sm2b[par][64:128, 1:2]), inc=ss)
        s_pk = p.val(ss)
        p.op("vector", lambda e, par=par: e.reciprocal(out=sm2[par][:, 0:1],
                                                       in_=sm2b[par][:, 0:1]),
             waits=[(ss, s_pk)], inc=vs)
        p.op("vector", lambda e, par=par: e.reciprocal(out=sm2[par][:, 1:2],
                                                       in_=sm2b[par][:, 1:2]), inc=vs)
        p.op("vector", lambda e, par=par: e.tensor_scalar(
            out=Kblk[par][0:64, 0:64], in0=Kblk[par][0:64, 0:64],
            scalar1=sm2[par][0:64, 0:1], scalar2=None, op0=ALU.mult),
            waits=[(vs, p.val(vs))], inc=vs)
        p.op("vector", lambda e, par=par: e.tensor_scalar(
            out=Kblk[par][64:128, 64:128], in0=Kblk[par][64:128, 64:128],
            scalar1=sm2[par][64:128, 1:2], scalar2=None, op0=ALU.mult), inc=vs)
        v_K = p.val(vs)
        p.op("tensor", lambda e, par=par: e.matmul(
            out=psNV[par][:, :], lhsT=onesblk[:, :], rhs=Kblk[par][:, :],
            start=True, stop=True), waits=[(vs, v_K)], inc=ts)
        t_cs = p.val(ts)
        p.op("vector", lambda e, par=par: e.tensor_reduce(
            out=sm2b[par][:, 0:1], in_=psNV[par][:, :], axis=AX, op=ALU.max),
            waits=[(ts, t_cs)], inc=vs)
        p.op("vector", lambda e, par=par: e.reciprocal(out=sm2b[par][:, 1:2],
                                                       in_=sm2b[par][:, 0:1]),
             waits=[(vs, p.val(vs))], inc=vs)
        v_dn = p.val(vs)
        p.op("tensor", lambda e, par=par: e.transpose(
            out=pst[par][:, :], in_=Kblk[par][:, :], identity=eye[:, :]),
            waits=[(vs, v_dn)], inc=ts)
        t_kt = p.val(ts)
        p.op("vector", lambda e, par=par: e.tensor_scalar(
            out=Vblk[par][:, :], in0=pst[par][:, :], scalar1=sm2b[par][:, 1:2],
            scalar2=None, op0=ALU.mult), waits=[(ts, t_kt)], inc=vs)
        p.op("vector", lambda e, par=par: e.tensor_copy(out=PTsb[par][:, :],
                                                        in_=pst[par][:, :]), inc=vs)
        v_v0 = p.val(vs)
        for it in range(6):
            p.op("tensor", lambda e, par=par: e.matmul(
                out=psNV[par][:, :], lhsT=PTsb[par][:, :], rhs=Vblk[par][:, :],
                start=True, stop=True),
                waits=[(vs, v_v0 if it == 0 else p.val(vs))], inc=ts)
            t_kv = p.val(ts)
            p.op("vector", lambda e, par=par: e.tensor_copy(out=KVb[par][:, :],
                                                            in_=psNV[par][:, :]),
                 waits=[(ts, t_kv)], inc=vs)
            p.op("vector", lambda e, par=par: e.scalar_tensor_tensor(
                out=Tb[par][:, :], in0=eye[:, :], scalar=7.0, in1=psNV[par][:, :],
                op0=ALU.mult, op1=ALU.subtract), inc=vs)
            v_t1 = p.val(vs)
            p.op("tensor", lambda e, par=par: e.transpose(
                out=pst[par][:, :], in_=KVb[par][:, :], identity=eye[:, :]),
                waits=[(vs, v_t1)], inc=ts)
            t_kvt = p.val(ts)
            p.op("vector", lambda e, par=par: e.tensor_copy(out=KVb[par][:, :],
                                                            in_=pst[par][:, :]),
                 waits=[(ts, t_kvt)], inc=vs)
            v_kvt = p.val(vs)
            p.op("tensor", lambda e, par=par: e.matmul(
                out=psNV[par][:, :], lhsT=KVb[par][:, :], rhs=Tb[par][:, :],
                start=True, stop=True), waits=[(vs, v_kvt)], inc=ts)
            t_t2 = p.val(ts)
            p.op("vector", lambda e, par=par: e.scalar_tensor_tensor(
                out=Tb[par][:, :], in0=eye[:, :], scalar=15.0, in1=psNV[par][:, :],
                op0=ALU.mult, op1=ALU.subtract), waits=[(ts, t_t2)], inc=vs)
            v_t3 = p.val(vs)
            p.op("tensor", lambda e, par=par: e.matmul(
                out=psNV[par][:, :], lhsT=KVb[par][:, :], rhs=Tb[par][:, :],
                start=True, stop=True), waits=[(vs, v_t3)], inc=ts)
            t_t4 = p.val(ts)
            p.op("vector", lambda e, par=par: e.scalar_tensor_tensor(
                out=Tb[par][:, :], in0=eye[:, :], scalar=13.0, in1=psNV[par][:, :],
                op0=ALU.mult, op1=ALU.subtract), waits=[(ts, t_t4)], inc=vs)
            v_t5 = p.val(vs)
            p.op("tensor", lambda e, par=par: e.transpose(
                out=pst[par][:, :], in_=Vblk[par][:, :], identity=eye[:, :]),
                waits=[(vs, v_t5)], inc=ts)
            t_vt = p.val(ts)
            p.op("vector", lambda e, par=par: e.tensor_copy(out=PTsb[par][:, :],
                                                            in_=pst[par][:, :]),
                 waits=[(ts, t_vt)], inc=vs)
            v_vt = p.val(vs)
            p.op("tensor", lambda e, par=par: e.matmul(
                out=psNV[par][:, :], lhsT=PTsb[par][:, :], rhs=Tb[par][:, :],
                start=True, stop=True), waits=[(vs, v_vt)], inc=ts)
            t_vu = p.val(ts)
            p.op("scalar", lambda e, par=par: e.activation(
                out=Vblk[par][:, :], in_=psNV[par][:, :], func=AF.Copy, scale=0.25),
                waits=[(ts, t_vu)], inc=ss)
            s_vu = p.val(ss)
            if it < 5:
                p.op("tensor", lambda e, par=par: e.transpose(
                    out=pst[par][:, :], in_=Kblk[par][:, :], identity=eye[:, :]),
                    waits=[(ss, s_vu)], inc=ts)
                t_kt2 = p.val(ts)
                p.op("vector", lambda e, par=par: e.tensor_copy(
                    out=PTsb[par][:, :], in_=pst[par][:, :]),
                    waits=[(ts, t_kt2)], inc=vs)
            yield
        p.op("tensor", lambda e, par=par: e.transpose(
            out=pst[par][:, :], in_=Vblk[par][:, :], identity=eye[:, :]),
            waits=[(vs, p.val(vs)), (ss, p.val(ss))], inc=ts)
        t_pt = p.val(ts)
        p.op("vector", lambda e, par=par: e.tensor_copy(out=PTsb[par][:, :],
                                                        in_=pst[par][:, :]),
             waits=[(ts, t_pt)], inc=vs)
        v_pt = p.val(vs)
        p.op("tensor", lambda e, par=par: e.matmul(
            out=psNV[par][:, :], lhsT=PTsb[par][:, :], rhs=NVsb[par][:, :],
            start=True, stop=True), waits=[(vs, v_pt)], inc=ts)
        t_pnv = p.val(ts)
        p.op("vector", lambda e, par=par: e.tensor_copy(out=PNv[par][:, :],
                                                        in_=psNV[par][:, :]),
             waits=[(ts, t_pnv)], inc=vs)
        v_pnv = p.val(vs)
        for c in range(NCH):
            p.op("tensor", lambda e, c=c, par=par: e.matmul(
                out=psWn[par][:, :], lhsT=PNv[par][:, :], rhs=LT[par][:, c * Lc:(c + 1) * Lc],
                start=True, stop=True),
                waits=[(vs, v_pnv)] if c == 0 else [(vs, p.val(vs))], inc=ts)
            t_y = p.val(ts)
            p.op("vector", lambda e, par=par: e.tensor_copy(out=ytmp[par][:, :],
                                                            in_=psWn[par][:, :]),
                 waits=[(ts, t_y), (prefix + "dy", p.val(prefix + "dy"))], inc=vs)
            v_y = p.val(vs)
            p.op("gpsimd", lambda e, pr=pr, c=c, par=par: e.dma_start(
                out=ystash[pr, :, c * Lc:(c + 1) * Lc], in_=ytmp[par][:, :]),
                waits=[(vs, v_y)], inc=prefix + "dy", inc_amt=16)
            yield
        chain_done[par] = p.val(vs)

    def drain(g, n=10 ** 9):
        for _ in range(n):
            try:
                next(g)
            except StopIteration:
                return True
        return False

    drain(emit_qkv(0))
    prev = None
    for pr in range(1, NPR):
        cur = emit_chain(pr - 1)
        while prev is not None:
            done_p = drain(prev, 2)
            drain(cur, 2)
            if done_p:
                prev = None
        qg = emit_qkv(pr)
        while not drain(qg, 1):
            drain(cur, 2)
        prev = cur
    cur = emit_chain(NPR - 1)
    while prev is not None:
        done_p = drain(prev, 3)
        drain(cur, 1)
        if done_p:
            prev = None
    drain(cur)
    prev_bar = p.barrier()

    dwp = None
    for k in range(ND):
        dwp = p.op("gpsimd", lambda e, k=k: e.dma_start(
            out=w2sb[:, k * 1024:(k + 1) * 1024],
            in_=d["wproj"][k * 128:(k + 1) * 128, :]), inc=prefix + "dwp", inc_amt=16)
    y_done = [(prefix + "dy", p.val(prefix + "dy")), (prefix + "dwp", p.val(prefix + "dwp"))]
    for c in range(NCH):
        sl = slice(c * Lc, (c + 1) * Lc)
        dx = prefix + ("dra" if c % 2 == 0 else "drb")
        dl = None
        for i in range(ND):
            dl = p.op("sync", lambda e, i=i, sl=sl: e.dma_start(
                out=sx[i][:, :], in_=xin_ap(i, sl)),
                waits=prev_bar if i == 0 else (), inc=dx, inc_amt=16)
        for pr in range(NPR):
            dl = p.op("sync", lambda e, pr=pr, sl=sl: e.dma_start(
                out=ypr[pr][:, :], in_=ystash[pr, :, sl]),
                waits=y_done if (c == 0 and pr == 0) else (), inc=dx, inc_amt=16)
        for o in range(ND):
            for pr in range(NPR):
                p.op("tensor", lambda e, o=o, pr=pr: e.matmul(
                    out=psW0[:, :], lhsT=w2sb[:, pr * 1024 + o * 128:pr * 1024 + (o + 1) * 128],
                    rhs=ypr[pr][:, :], start=(pr == 0), stop=(pr == NPR - 1)),
                    waits=[(dx, dl)] if (o == 0 and pr == 0) else (),
                    inc=ts if pr == NPR - 1 else None)
            t_pr = p.val(ts)
            p.op("vector", lambda e, o=o: e.tensor_add(
                out=xo[o][:, :], in0=psW0[:, :], in1=sx[o][:, :]),
                waits=[(ts, t_pr)], inc=vs)
            v_x = p.val(vs)
            p.op("sync", lambda e, o=o, sl=sl: e.dma_start(
                out=xout_ap(o, sl), in_=xo[o][:, :]),
                waits=[(vs, v_x)], inc=prefix + "do", inc_amt=16)
            p.op("tensor", lambda e: None, waits=[(vs, v_x)], real=False)
        prev_bar = p.barrier()
    p.wait("sync", [(prefix + "do", p.val(prefix + "do"))])


def build_att0_launch(in_shapes, T=4096):
    nc = bass.Bass("TRN2", target_bir_lowering=False)
    d = declare_inputs(nc, in_shapes)
    xout = nc.dram_tensor("xout", [ND, 128, T], F32, kind="ExternalOutput").ap()
    p = Plan(nc)
    m = Sbuf(p)
    m.dram = d
    emit_const_loads(p, m)
    emit_att0(p, m, lambda i, sl: d["xin"][i, :, sl],
              lambda o, sl: xout[o, :, sl], T)
    p.build()
    return nc, m



def build_fused():
    nc = bass.Bass("TRN2", target_bir_lowering=False)
    shapes = {
        "xin": np.zeros((8, 8, 128, 512), np.float32),
        "wattn": np.zeros((1024, 3072), np.float32),
        "wproj": np.zeros((1024, 1024), np.float32),
        "ipT": np.zeros((1024, 4096), np.float32),
        "xpT": np.zeros((2048, 128), np.float32),
        "dtT": np.zeros((128, 2048), np.float32),
        "opT": np.zeros((2048, 1024), np.float32),
        "A_logT": np.zeros((128, 256), np.float32),
        "convWT": np.zeros((128, 64), np.float32),
        "convBT": np.zeros((128, 16), np.float32),
        "DT": np.zeros((128, 16), np.float32),
        "dtbT": np.zeros((128, 16), np.float32),
        "gammaT": np.zeros((128, 8), np.float32),
        "betaT": np.zeros((128, 8), np.float32),
        "wgT": np.zeros((1024, 3072), np.int32),
        "wuT": np.zeros((1024, 3072), np.int32),
        "wdT": np.zeros((3072, 1024), np.int32),
        **host_consts(),
    }
    for nm in ("wfT", "wiT", "wcT", "ufT", "uiT", "ucT", "woT", "uoT"):
        shapes[nm] = np.zeros((1024, 1024), np.float32)
    for nm in ("bfT", "biT", "bcT", "boT", "incT"):
        shapes[nm] = np.zeros((128, 8), np.float32)
    d = declare_inputs(nc, shapes)
    xout = nc.dram_tensor("xout", [ND, 128, 4096], F32, kind="ExternalOutput").ap()

    hshape = [8, ND, 128, 512]
    h1 = nc.dram_tensor("fz_h1", hshape, F32, kind="Internal")
    h2 = nc.dram_tensor("fz_h2", hshape, F32, kind="Internal")
    h3 = nc.dram_tensor("fz_h3", hshape, F32, kind="Internal")
    g1 = nc.dram_tensor("fz_g1", [8 * 8 * ND * 128, 512], F32, kind="Internal",
                        addr_space="Shared")
    g2 = nc.dram_tensor("fz_g2", [8 * 8 * ND * 128, 512], F32, kind="Internal",
                        addr_space="Shared")
    g3 = nc.dram_tensor("fz_g3", [8 * 8 * ND * 128, 512], F32, kind="Internal",
                        addr_space="Shared")
    xc_b1 = nc.dram_tensor("fz_xb1", [ND, 128, 4096], F32, kind="Internal").ap()
    xc_s1 = nc.dram_tensor("fz_xs1", hshape, F32, kind="Internal").ap()
    xc_b2 = nc.dram_tensor("fz_xb2", [ND, 128, 4096], F32, kind="Internal").ap()
    xc_b3 = nc.dram_tensor("fz_xb3", [ND, 128, 4096], F32, kind="Internal").ap()
    xc_b4 = nc.dram_tensor("fz_xb4", [ND, 128, 4096], F32, kind="Internal").ap()

    p = Plan(nc)
    m = Sbuf(p)
    m.dram = d
    emit_const_loads(p, m)
    m.push()
    for _j in range(2):
        m.sb(f"f1p_stage{_j}", (128, 1024))

    def phase_end():
        conds = p.barrier()
        for e in ENGINES:
            p.wait(e, conds)
        return conds

    def reshard(tag, h, g, emit_copy, store_sem):
        cc = tag + "cc"
        sw = tag + "sw"
        p.sem(cc); p.sem(sw)
        p.op("gpsimd", lambda e: e.collective_compute(
            "AllGather", ALU.bypass, replica_groups=[list(range(8))],
            ins=[h.ap().rearrange("a b c d -> (a b c) d")],
            outs=[g.ap()]),
            waits=[store_sem], inc=cc, inc_amt=1)
        cc_target = p.val(cc)
        p.counts[sw] += 8 * 16

        def switch_fn(nc_, rank, sems):
            nc_.sync.wait_ge(sems[cc], cc_target)
            for blk in range(8):
                emit_copy(nc_.sync, rank, blk).then_inc(sems[sw], 16)
        p.switch_segment(switch_fn)
        return [(sw, p.val(sw))]

    m.push()
    emit_ffn0(p, m, lambda t, i: d["xin"][t, i, :, :],
              lambda t, k: h1.ap()[t, k, :, :], 512)
    emit_ffn1_prep(p, m)
    m.pop()
    f0_store = ("f0do", p.val("f0do"))
    phase_end()
    g1v = g1.ap().rearrange("(i b d p) l -> i b d p l", i=8, b=8, d=ND)
    xb1v = xc_b1.rearrange("d p (i l) -> i d p l", i=8)
    rw = reshard("r1", h1, g1,
                 lambda eng, rank, blk: eng.dma_start(
                     out=xb1v[blk], in_=g1v[blk, rank]), f0_store)

    def h2_out(o, sl):
        assert sl.start % 512 == 0 and sl.stop - sl.start == 512
        return h2.ap()[sl.start // 512, o, :, :]
    m.push()
    emit_att0(p, m, lambda i, sl: xc_b1[i, :, sl], h2_out, 4096, entry_waits=rw)
    m.pop()
    at_store = ("atdo", p.val("atdo"))
    phase_end()
    g2v = g2.ap().rearrange("(b j d p) l -> b j d p l", b=8, j=8, d=ND)
    rw = reshard("r2", h2, g2,
                 lambda eng, rank, blk: eng.dma_start(
                     out=xc_s1[blk], in_=g2v[blk, rank]), at_store)

    m.push()
    emit_ffn0(p, m, lambda t, i: xc_s1[t, i, :, :],
              lambda t, k: h3.ap()[t, k, :, :], 512, entry_waits=rw)
    m.pop()
    f0_store2 = ("f0do", p.val("f0do"))
    phase_end()
    g3v = g3.ap().rearrange("(i b d p) l -> i b d p l", i=8, b=8, d=ND)
    xb2v = xc_b2.rearrange("d p (i l) -> i d p l", i=8)
    rw = reshard("r3", h3, g3,
                 lambda eng, rank, blk: eng.dma_start(
                     out=xb2v[blk], in_=g3v[blk, rank]), f0_store2)
    m.pop()

    m.push()
    emit_ffn1(p, m, lambda i, sl: xc_b2[i, :, sl],
              lambda o, sl: xc_b3[o, :, sl], 4096, 512, entry_waits=rw)
    m.pop()
    phase_end()

    m.push()
    emit_att1(p, m, lambda i, sl: xc_b3[i, :, sl],
              lambda o, sl: xc_b4[o, :, sl], 4096, 512)
    m.pop()
    phase_end()

    m.push()
    emit_ffn1(p, m, lambda i, sl: xc_b4[i, :, sl],
              lambda o, sl: xout[o, :, sl], 4096, 512)
    m.pop()
    p.wait("sync", [("f1d", p.val("f1d"))])
    p.build()
    return nc, m



from concourse.bass_utils import run_bass_kernel_spmd

_FUSED_CACHE = {}


def kernel(**inputs):
    z = inputs
    if "prog" not in _FUSED_CACHE:
        _FUSED_CACHE["prog"] = build_fused()
    nc, m = _FUSED_CACHE["prog"]

    ins = {}
    for nm in ("wf", "wi", "wc", "uf", "ui", "uc", "wo", "uo"):
        ins[nm + "T"] = np.ascontiguousarray(z["L0_ffn_" + nm].T)
    for nm, src in (("bfT", "bf"), ("biT", "bi"), ("bcT", "bc"), ("boT", "bo"),
                    ("incT", "in_c")):
        ins[nm] = np.ascontiguousarray(z["L0_ffn_" + src].reshape(8, 128).T)
    ins["wattn"] = z["L0_attn_W_attn"]
    ins["wproj"] = z["L0_attn_W_proj"]
    g = {k[len("L1_attn_"):]: z[k] for k in z if k.startswith("L1_attn_")}
    ins["ipT"] = np.ascontiguousarray(g["in_proj_w"].T)
    xpT = np.zeros((2048, 128), np.float32); xpT[:, :96] = g["x_proj_w"].T
    ins["xpT"] = xpT
    dtT = np.zeros((128, 2048), np.float32); dtT[:64, :] = g["dt_proj_w"].T
    ins["dtT"] = dtT
    ins["opT"] = np.ascontiguousarray(g["out_proj_w"].T)
    ins["A_logT"] = np.ascontiguousarray(
        g["A_log"].reshape(16, 128, 16).transpose(1, 0, 2).reshape(128, 256))
    ins["convWT"] = np.ascontiguousarray(
        g["conv_w"][:, 0, :].reshape(16, 128, 4).transpose(1, 0, 2).reshape(128, 64))
    ins["convBT"] = np.ascontiguousarray(g["conv_b"].reshape(16, 128).T)
    ins["DT"] = np.ascontiguousarray(g["D"].reshape(16, 128).T)
    ins["dtbT"] = np.ascontiguousarray(g["dt_proj_b"].reshape(16, 128).T)
    ins["gammaT"] = np.ascontiguousarray(g["ln_gamma"].reshape(8, 128).T)
    ins["betaT"] = np.ascontiguousarray(g["ln_beta"].reshape(8, 128).T)
    ins["wgT"] = np.ascontiguousarray(z["L1_ffn_w_gate"].T)
    ins["wuT"] = np.ascontiguousarray(z["L1_ffn_w_up"].T)
    ins["wdT"] = np.ascontiguousarray(z["L1_ffn_w_down"].T)
    ins.update(host_consts(m.const_vals))

    xfm = np.ascontiguousarray(z["L0_attn_x"].transpose(0, 2, 1))
    in_maps = []
    for i in range(8):
        xs = np.ascontiguousarray(xfm[:, :, i * 512:(i + 1) * 512])
        in_maps.append({**ins, "xin": xs.reshape(8, 8, 128, 512)})
    res = run_bass_kernel_spmd(nc, in_maps, core_ids=list(range(8)))
    out = np.empty((8, 1024, 4096), np.float32)
    for b in range(8):
        out[b] = np.asarray(res.results[b]["xout"]).reshape(1024, 4096)
    return np.ascontiguousarray(out.transpose(0, 2, 1))
